# version 16
# baseline (speedup 1.0000x reference)
import os
import numpy as np
from contextlib import ExitStack

import concourse.bass as bass
import concourse.mybir as mybir
from concourse.bass_utils import run_bass_kernel_spmd

F32 = mybir.dt.float32
BF16 = mybir.dt.bfloat16
AF = mybir.ActivationFunctionType
ALU = mybir.AluOpType
AX = mybir.AxisListType.X

NTOK, D, QKD, HID2, FF = 4096, 1024, 2048, 4096, 4096
NT = NTOK // 128
EPS = 1e-5
RND = 12582912.0

DBG = bool(int(os.environ.get("MK_DBG", "0")))


class Sched:
    ENGS = ("sync", "vector", "scalar", "tensor", "gpsimd")

    def __init__(self, nc):
        self.nc = nc
        self.q = {e: [] for e in self.ENGS}
        self.cnt = {}
        self.sem_h = {}

    def new_sem(self, stack, name):
        self.sem_h[name] = stack.enter_context(self.nc.semaphore(name))
        self.cnt[name] = 0

    def t(self, sem):
        return self.cnt[sem]

    def emit(self, eng, fn, waits=(), inc=None, amt=1):
        waits = [(s, v) for (s, v) in waits if v > 0]
        for s, v in waits:
            assert v <= self.cnt[s], f"future ticket {s}:{v}>{self.cnt[s]}"
        if eng == "vector":
            assert inc in (None, "dve")
            inc, amt = "dve", 1
        elif eng == "scalar":
            assert inc in (None, "se")
            inc, amt = "se", 1
        if inc is not None:
            self.cnt[inc] += amt
        self.q[eng].append((fn, tuple(waits), inc, amt))

    def dma(self, eng, out, in_, waits=(), inc=None):
        self.emit(eng, lambda e, o=out, i=in_: e.dma_start(out=o, in_=i),
                  waits=waits, inc=inc, amt=16)

    def replay(self, eng_name, eng):
        serial = {"vector": "dve", "scalar": "se"}.get(eng_name)
        k = 0
        kp = 0
        for fn, waits, inc, amt in self.q[eng_name]:
            if serial and k > 0:
                eng.wait_ge(self.sem_h[serial], k)
            if eng_name == "gpsimd" and inc == "pool" and kp > 0:
                eng.wait_ge(self.sem_h["pool"], kp)
            for s, v in waits:
                eng.wait_ge(self.sem_h[s], v)
            instr = fn(eng)
            if inc is not None:
                instr.then_inc(self.sem_h[inc], amt)
            if serial:
                k += 1
            if eng_name == "gpsimd" and inc == "pool":
                kp += 1


def build_nc(dbg=False):
    nc = bass.Bass("TRN2", target_bir_lowering=False, debug=False)

    x_d = nc.dram_tensor("x", [NTOK, D], F32, kind="ExternalInput").ap()
    wqk_d = nc.dram_tensor("wqk", [D, QKD], F32, kind="ExternalInput").ap()
    whid_d = nc.dram_tensor("whid", [D, HID2], F32, kind="ExternalInput").ap()
    wout_d = nc.dram_tensor("wout", [QKD, D], F32, kind="ExternalInput").ap()
    w1_d = nc.dram_tensor("w1", [FF, D], F32, kind="ExternalInput").ap()
    w2_d = nc.dram_tensor("w2", [D, FF], F32, kind="ExternalInput").ap()
    lng_d = nc.dram_tensor("lng", [1, D], F32, kind="ExternalInput").ap()
    lnb_d = nc.dram_tensor("lnb", [1, D], F32, kind="ExternalInput").ap()
    bqk_d = nc.dram_tensor("bqk", [1, QKD], F32, kind="ExternalInput").ap()
    bhid_d = nc.dram_tensor("bhid", [1, HID2], F32, kind="ExternalInput").ap()
    bout_d = nc.dram_tensor("bout", [1, D], F32, kind="ExternalInput").ap()
    g0_d = nc.dram_tensor("g0", [1, QKD], F32, kind="ExternalInput").ap()
    b0_d = nc.dram_tensor("b0", [1, QKD], F32, kind="ExternalInput").ap()
    g1_d = nc.dram_tensor("g1", [1, QKD], F32, kind="ExternalInput").ap()
    b1_d = nc.dram_tensor("b1", [1, QKD], F32, kind="ExternalInput").ap()
    fb1_d = nc.dram_tensor("fb1", [1, FF], F32, kind="ExternalInput").ap()
    fb2_d = nc.dram_tensor("fb2", [1, D], F32, kind="ExternalInput").ap()
    iden_d = nc.dram_tensor("iden", [128, 128], F32, kind="ExternalInput").ap()

    out_d = nc.dram_tensor("out", [NTOK, D], F32, kind="ExternalOutput").ap()

    skind = "ExternalOutput" if dbg else "Internal"
    q_h = nc.dram_tensor("q_h", [NTOK, QKD], BF16, kind="Internal").ap()
    nfm_h = nc.dram_tensor("nfm_h", [D, NTOK], BF16, kind="Internal").ap()
    r1_h = nc.dram_tensor("r1_h", [NTOK, D], F32, kind=skind).ap()
    x1_h = nc.dram_tensor("x1_h", [NTOK, D], F32, kind=skind).ap()
    bcast_h = nc.dram_tensor("bcast_h", [1, 256], F32, kind="Internal").ap()
    if dbg:
        sdbg_d = nc.dram_tensor("sdbg", [128, 16], F32,
                                kind="ExternalOutput").ap()
        qf_d = nc.dram_tensor("qf", [NTOK, QKD], F32,
                              kind="ExternalOutput").ap()
        ktf_d = nc.dram_tensor("ktf", [128, 65536], F32,
                               kind="ExternalOutput").ap()
        h1f_d = nc.dram_tensor("h1f", [128, 16384], F32,
                               kind="ExternalOutput").ap()
        ppf_d = nc.dram_tensor("ppfd", [128, 132], F32,
                               kind="ExternalOutput").ap()
        nfmf_d = nc.dram_tensor("nfmf", [D, NTOK], F32,
                                kind="ExternalOutput").ap()
        kf_d = nc.dram_tensor("kf", [NTOK, QKD], F32,
                              kind="ExternalOutput").ap()

    with ExitStack() as stk:
        sbuf = lambda *a, **k: stk.enter_context(nc.sbuf_tensor(*a, **k))

        KT = sbuf("KT", [128, 65536], BF16)
        wA = sbuf("wA", [128, 16384], BF16)
        scrA = sbuf("scrA", [128, 8192], BF16)
        scrB = sbuf("scrB", [128, 3072], F32)
        scrC = sbuf("scrC", [128, 1024], F32)
        qsb = sbuf("qsb", [128, 2048], BF16)
        nfm = sbuf("nfm", [128, 4096], BF16)
        qtm = sbuf("qtm", [128, 1024], BF16)
        stats = sbuf("stats", [128, 64], F32)
        ppf = sbuf("ppf", [128, 132], F32)
        ppb = sbuf("ppb", [128, 8], BF16)
        identb = sbuf("identb", [128, 128], BF16)
        onesb = sbuf("onesb", [1, 128], BF16)
        onesf = sbuf("onesf", [1, 128], F32)
        onef1 = sbuf("onef1", [1, 1], F32)

        psum = stk.enter_context(nc.psum_tensor("psum", [128, 4096], F32))

        S = Sched(nc)
        for s in ["ldx0", "ldx1", "ldw0", "ldw1", "ldq0", "ldq1", "ldnf",
                  "ldr1", "ldy0", "ldy1", "ldr", "ldr2", "stq0", "stq1",
                  "stn0", "stn1", "str1", "stx1", "stout", "pe", "dve", "se",
                  "pool", "stdbg"]:
            S.new_sem(stk, s)

        def bank(b, off=0, w=512):
            return psum[:, b * 512 + off:b * 512 + off + w]

        psum_b = psum.bitcast(BF16)
        pT = psum_b[:, 4096:5120]
        pT6 = psum_b[:, 6144:7168]
        pT7 = psum_b[:, 7168:8192]
        browp = psum[0:1, 3584:4096]

        KTf = KT.bitcast(F32)
        rowf = KTf[0:1, 31744:32768]
        rowf2 = KTf[0:1, 30720:31744]
        wstg = [KTf[:, 28672 + i * 512:28672 + (i + 1) * 512] for i in range(2)]
        idstg = KTf[:, 28672:28800]
        ubf = KT[:, 56320:56832]

        wqk_sb = [wA[:, c * 2048:(c + 1) * 2048] for c in range(8)]
        wout_sb = [wA[:, j * 1024:(j + 1) * 1024] for j in range(16)]
        h1_sb = [wA[:, j * 512:(j + 1) * 512] for j in range(32)]

        k_tile = [KT[:, t * 2048:(t + 1) * 2048] for t in range(NT)]
        w1t_ck = [KT[:, c * 4096:(c + 1) * 4096] for c in range(8)]
        whid_ck = [KT[:, 32768 + c * 4096:32768 + (c + 1) * 4096]
                   for c in range(8)]
        w2t_ck = [KT[:, 32768 + j * 1024:32768 + (j + 1) * 1024]
                  for j in range(32)]
        KT_w1t_3d = KT.rearrange("p (c m) -> p c m", m=4096)
        KT_w2t_3d = KT[:, 32768:65536].rearrange("p (j m) -> p j m", m=1024)

        reps = {v: scrA[:, i * 2048:(i + 1) * 2048]
                for i, v in enumerate(["g0", "b0", "g1", "b1"])}
        qslice = [scrA[:, i * 4096:(i + 1) * 4096] for i in range(2)]
        A_sl = [scrA[:, j * 512:(j + 1) * 512] for j in range(16)]
        sqdump3b = scrA[:, 0:1024]

        xb = [scrB[:, 0:1024], scrB[:, 1024:2048]]
        lnx = scrB[:, 2048:3072]
        attn_un = scrB[:, 0:2048]
        stgB = [scrB[:, 2048:2560], scrB[:, 2560:3072]]
        r1rd = scrB[:, 0:1024]
        vg = scrB[:, 1024:2048]
        x1rd = [scrB[:, 0:1024], scrB[:, 1024:2048]]

        evacC = scrC[:, 0:1024]

        normed_bf = qsb[:, 0:1024]
        qksilu = [qsb[:, 1024:1536], qsb[:, 1536:2048]]
        ubf3a = qsb[:, 0:512]
        w1q_stg = [qsb[:, 0:512], qsb[:, 512:1024]]
        fb2b = qsb[0:1, 0:1024]
        w2q_stg = [qsb[:, 1024:1536], qsb[:, 1536:2048]]
        y_bf = qsb[:, 1024:2048]

        nfm_wr = [nfm[:, 0:1024], nfm[:, 1024:2048]]
        brow_qk = KT[0:1, 63488:65536]
        lnxB = nfm.bitcast(F32)[:, 1024:2048]
        attn_dump = nfm[:, 0:2048]
        nfm_grp = nfm[:]
        nfm_grp3d = nfm.rearrange("p (c m) -> p c m", m=512)
        rb_row = nfm[0:1, 0:2048]

        qch = [qtm[:, 0:512], qtm[:, 512:1024]]
        boutb = qtm[0:1, 0:1024]

        lng_pp = ppf[:, 0:8]
        lnb_pp = ppb[:, 0:8]
        bhid_pp = ppf[:, 8:40]
        fb1_pp = ppf[:, 40:72]
        beff_h = ppf[:, 72:104]
        s_sb = ppf[:, 104:120]
        inv1_pp = ppf[:, 120:121]
        inv2_pp = ppf[:, 121:122]
        absacc = ppf[:, 122:124]
        tmpred = ppf[:, 126:127]
        epsc = ppf[:, 127:128]
        onescol = ppf[:, 128:129]
        mrow = ppf[0:1, 124:126]

        def sl(t, i):
            return stats[:, (t % 4) * 16 + i:(t % 4) * 16 + i + 1]

        nfm_h3 = nfm_h.rearrange("(c p) m -> p c m", p=128)
        q_h3 = q_h.rearrange("(t p) f -> p t f", p=128)

        xld_t = {}
        S.dma("sync", xb[0], x_d[0:128, :], inc="ldx0")
        xld_t[0] = ("ldx0", S.t("ldx0"))
        S.dma("sync", xb[1], x_d[128:256, :], inc="ldx1")
        xld_t[1] = ("ldx1", S.t("ldx1"))

        S.emit("vector", lambda e: e.memset(onesb[:], 1.0))
        S.emit("vector", lambda e: e.memset(onesf[:], 1.0))
        S.emit("vector", lambda e: e.memset(onef1[:], 1.0))
        S.emit("vector", lambda e: e.memset(epsc, EPS))
        S.emit("vector", lambda e: e.memset(onescol, 1.0), inc="dve")
        t_ones = S.t("dve")

        S.dma("gpsimd", idstg, iden_d, inc="ldr")
        S.dma("gpsimd", rowf, lng_d, inc="ldr")
        S.dma("gpsimd", rowf2, lnb_d, inc="ldr2")
        S.emit("vector", lambda e: e.tensor_copy(identb[:], idstg),
               waits=[("ldr", S.t("ldr"))], inc="dve")
        t_ident = S.t("dve")
        first5 = [True]

        def scat(col, row_ap, wts, last=False):
            f = first5[0]
            first5[0] = False
            S.emit("tensor", lambda e, c=col, r=row_ap, ff=f, ll=last: e.matmul(
                bank(5, c, 1), r, onef1[:], start=ff, stop=ll),
                waits=wts, inc="pe")

        for c in range(8):
            scat(c, rowf[0:1, c * 128:(c + 1) * 128],
                 [("ldr", S.t("ldr")), ("ldr2", S.t("ldr2")),
                  ("dve", t_ones)])
        for c in range(8):
            scat(8 + c, rowf2[0:1, c * 128:(c + 1) * 128], [])
        for h in range(4):
            S.dma("gpsimd", rowf, bhid_d[0:1, h * 1024:(h + 1) * 1024],
                  waits=[("pe", S.t("pe"))], inc="ldr")
            tl = S.t("ldr")
            for c in range(8):
                scat(16 + h * 8 + c, rowf[0:1, c * 128:(c + 1) * 128],
                     [("ldr", tl)])
        for h in range(4):
            S.dma("gpsimd", rowf, fb1_d[0:1, h * 1024:(h + 1) * 1024],
                  waits=[("pe", S.t("pe"))], inc="ldr")
            tl = S.t("ldr")
            for c in range(8):
                scat(48 + h * 8 + c, rowf[0:1, c * 128:(c + 1) * 128],
                     [("ldr", tl)], last=(h == 3 and c == 7))
        t_scat = S.t("pe")
        S.emit("vector", lambda e: e.tensor_copy(lng_pp[:], bank(5, 0, 8)),
               waits=[("pe", t_scat)], inc="dve")
        S.emit("vector", lambda e: e.tensor_copy(lnb_pp[:], bank(5, 8, 8)),
               inc="dve")
        S.emit("vector", lambda e: e.tensor_copy(bhid_pp[:], bank(5, 16, 32)),
               inc="dve")
        S.emit("vector", lambda e: e.tensor_copy(fb1_pp[:], bank(5, 48, 32)),
               inc="dve")
        t_ppcopy = S.t("dve")

        t_rep_prev = 0
        for i, vd in enumerate([g0_d, b0_d, g1_d, b1_d]):
            nm = ["g0", "b0", "g1", "b1"][i]
            for h in range(2):
                S.dma("gpsimd", rowf, vd[0:1, h * 1024:(h + 1) * 1024],
                      waits=[("pe", S.t("pe")), ("dve", S.t("dve"))],
                      inc="ldr")
                S.emit("vector", lambda e, hh=h: e.tensor_copy(
                    rb_row[0:1, hh * 1024:(hh + 1) * 1024], rowf),
                    waits=[("ldr", S.t("ldr"))], inc="dve")
            t_rb = S.t("dve")
            for n in range(4):
                S.emit("tensor", lambda e, nn=n: e.matmul(
                    bank(6), onesb[:], rb_row[0:1, nn * 512:(nn + 1) * 512],
                    start=True, stop=True),
                    waits=[("dve", t_rb), ("dve", t_ident),
                           ("dve", t_rep_prev)], inc="pe")
                S.emit("vector", lambda e, nn=n, nmx=nm: e.tensor_copy(
                    reps[nmx][:, nn * 512:(nn + 1) * 512], bank(6)),
                    waits=[("pe", S.t("pe"))], inc="dve")
                t_rep_prev = S.t("dve")

        t_wstg_free = [S.t("dve"), S.t("dve")]
        t_ubf_free = S.t("pe")
        for n in range(4):
            for c in range(8):
                sub = n * 8 + c
                slot = sub % 2
                S.dma("gpsimd", wstg[slot],
                      wqk_d[c * 128:(c + 1) * 128, n * 512:(n + 1) * 512],
                      waits=[("dve", t_wstg_free[slot])],
                      inc="ldw%d" % slot)
                tl = ("ldw%d" % slot, S.t("ldw%d" % slot))
                S.emit("vector", lambda e, s2=slot: e.tensor_copy(
                    ubf[:], wstg[s2]),
                    waits=[tl, ("pe", t_ubf_free)], inc="dve")
                t_cast = S.t("dve")
                S.emit("tensor", lambda e, cc=c: e.matmul(
                    browp, lnb_pp[:, cc:cc + 1], ubf[:],
                    start=(cc == 0), stop=(cc == 7)),
                    waits=[("dve", t_cast)], inc="pe")
                t_ubf_free = S.t("pe")
                S.emit("vector", lambda e, cc=c, nn=n, s2=slot: e.tensor_scalar(
                    out=wqk_sb[cc][:, nn * 512:(nn + 1) * 512], in0=wstg[s2],
                    scalar1=lng_pp[:, cc:cc + 1], scalar2=0.0,
                    op0=ALU.mult, op1=ALU.add),
                    waits=[("dve", t_ppcopy)], inc="dve")
                t_wstg_free[slot] = S.t("dve")
            t_brow_mm = S.t("pe")
            S.emit("scalar", lambda e: e.copy(rowf2[0:1, 0:512], browp),
                   waits=[("pe", t_brow_mm), ("dve", S.t("dve"))], inc="se")
            S.dma("gpsimd", rowf2[0:1, 512:1024],
                  bqk_d[0:1, n * 512:(n + 1) * 512],
                  waits=[("dve", S.t("dve"))], inc="ldr2")
            S.emit("vector", lambda e: e.tensor_add(
                rowf2[0:1, 0:512], rowf2[0:1, 0:512], rowf2[0:1, 512:1024]),
                waits=[("se", S.t("se")), ("ldr2", S.t("ldr2"))], inc="dve")
            S.emit("vector", lambda e, nn=n: e.tensor_copy(
                brow_qk[0:1, nn * 512:(nn + 1) * 512], rowf2[0:1, 0:512]),
                inc="dve")
        t_brow = S.t("dve")
        t_setup_pe = S.t("pe")
        t_setup_se = S.t("se")

        sq1_t = {}
        r1_t = {}
        transp_t = {}
        nfmcopy_t = {}
        silu_t = {}
        kaff_t = {}
        qaff_t = {}
        qdma_t = {}
        mmn_t = {}
        lnse_t = {}
        core_t = {}

        lnx_of = lambda t: lnx if t % 2 == 0 else lnxB
        sq2_t = {}

        last_dump = [0]

        def emit_chain_pair(ta, tb, guard_pe=None):
            tiles = [t for t in (ta, tb) if t is not None and t < NT]

            def both(emit_one):
                for t in tiles:
                    emit_one(t)

            def red1(t):
                wl = [xld_t[t]]
                if t >= 4:
                    wl.append(("se", lnse_t[t - 4]))
                    wl.append(("dve", core_t[t - 4]))
                S.emit("vector", lambda e, x2=xb[t % 2], tt=t: e.reduce_sum(
                    sl(tt, 0), x2, axis=AX), waits=wl, inc="dve")
            both(red1)
            t_red1s = S.t("dve")

            def negmu(t):
                S.emit("vector", lambda e, tt=t: e.tensor_scalar(
                    out=sl(tt, 1), in0=sl(tt, 0), scalar1=-1.0 / D,
                    scalar2=None, op0=ALU.mult), waits=[("dve", t_red1s)],
                    inc="dve")
            both(negmu)
            t_negmus = S.t("dve")

            def sq1(t):
                wl = [("dve", t_negmus)]
                if guard_pe is not None:
                    wl.append(("pe", guard_pe))
                S.emit("scalar", lambda e, x2=xb[t % 2], tt=t: e.activation(
                    out=normed_bf[:], in_=x2, func=AF.Square, bias=sl(tt, 1),
                    accum_out=sl(tt, 2)), waits=wl, inc="se")
                sq1_t[t] = S.t("se")
            both(sq1)

            def sqrt1(t):
                S.emit("scalar", lambda e, tt=t: e.activation(
                    out=sl(tt, 3), in_=sl(tt, 2), func=AF.Sqrt, scale=1.0 / D,
                    bias=epsc), inc="se")
            both(sqrt1)
            t_sqrt1s = S.t("se")

            def recip1(t):
                S.emit("vector", lambda e, tt=t: e.reciprocal(
                    sl(tt, 4), sl(tt, 3)), waits=[("se", t_sqrt1s)], inc="dve")
            both(recip1)
            t_recips = S.t("dve")

            def lnxm(t):
                S.emit("vector", lambda e, x2=xb[t % 2], tt=t,
                       lx=lnx_of(t): e.tensor_scalar(
                    out=lx, in0=x2, scalar1=sl(tt, 1), scalar2=sl(tt, 4),
                    op0=ALU.add, op1=ALU.mult), waits=[("dve", t_recips)],
                    inc="dve")
            both(lnxm)

            def mu2(t):
                S.emit("vector", lambda e, tt=t: e.scalar_tensor_tensor(
                    out=sl(tt, 5), in0=sl(tt, 1), scalar=float(D),
                    in1=sl(tt, 0), op0=ALU.mult, op1=ALU.add), inc="dve")
                S.emit("vector", lambda e, tt=t: e.tensor_scalar(
                    out=sl(tt, 6), in0=sl(tt, 5), scalar1=sl(tt, 4),
                    scalar2=-1.0 / D, op0=ALU.mult, op1=ALU.mult), inc="dve")
            both(mu2)
            t_mu2s = S.t("dve")

            def sq2(t):
                S.emit("scalar", lambda e, tt=t, lx=lnx_of(t): e.activation(
                    out=normed_bf[:], in_=lx, func=AF.Square, bias=sl(tt, 6),
                    accum_out=sl(tt, 7)), waits=[("dve", t_mu2s)], inc="se")
                sq2_t[t] = S.t("se")
            both(sq2)

            def sqrt2(t):
                S.emit("scalar", lambda e, tt=t: e.activation(
                    out=sl(tt, 8), in_=sl(tt, 7), func=AF.Sqrt, scale=1.0 / D,
                    bias=epsc), inc="se")
                lnse_t[t] = S.t("se")
            both(sqrt2)

            def recip2(t):
                S.emit("vector", lambda e, tt=t: e.reciprocal(
                    sl(tt, 9), sl(tt, 8)), waits=[("se", lnse_t[t])],
                    inc="dve")
            both(recip2)

            def r1(t):
                wl = [("str1", 16 * t)] if t >= 1 else []
                S.emit("vector", lambda e, x2=xb[t % 2], lx=lnx_of(t):
                       e.tensor_add(evacC, lx, x2), waits=wl, inc="dve")
                r1_t[t] = S.t("dve")
                S.dma("sync", r1_h[t * 128:(t + 1) * 128, :], evacC,
                      waits=[("dve", r1_t[t])], inc="str1")
            both(r1)
            last_dump[0] = S.t("se")

        def emit_core(t):
            wl = [("se", last_dump[0])]
            if t >= 1:
                wl.append(("pe", transp_t[t - 1]))
            S.emit("vector", lambda e, tt=t, lx=lnx_of(t): e.tensor_scalar(
                out=normed_bf[:], in0=lx, scalar1=sl(tt, 6),
                scalar2=sl(tt, 9), op0=ALU.add, op1=ALU.mult),
                waits=wl, inc="dve")
            core_t[t] = S.t("dve")

        def emit_transp(t):
            wl = [("dve", core_t[t])]
            if t >= 1:
                wl.append(("dve", nfmcopy_t[t - 1]))
            for k in range(8):
                S.emit("tensor", lambda e, kk=k: e.transpose(
                    pT[:, kk * 128:(kk + 1) * 128],
                    normed_bf[:, kk * 128:(kk + 1) * 128], identb[:]),
                    waits=(wl if k == 0 else []), inc="pe")
            transp_t[t] = S.t("pe")
            wl = [("pe", transp_t[t])]
            if t >= 2:
                wl.append(("stn%d" % (t % 2), 16 * ((t - 2) // 2 + 1)))
            S.emit("vector", lambda e, s2=t % 2: e.tensor_copy(
                nfm_wr[s2][:], pT), waits=wl, inc="dve")
            nfmcopy_t[t] = S.t("dve")
            S.dma("sync", nfm_h3[:, :, t * 128:(t + 1) * 128],
                  nfm_wr[t % 2].rearrange("p (c j) -> p c j", j=128),
                  waits=[("dve", nfmcopy_t[t])], inc="stn%d" % (t % 2))
            if dbg:
                S.dma("gpsimd",
                      nfmf_d.rearrange("(c p) m -> p c m", p=128)[
                          :, :, t * 128:(t + 1) * 128],
                      nfm_wr[t % 2].rearrange("p (c j) -> p c j", j=128),
                      inc="stdbg")

        def emit_mms(t):
            for n in range(4):
                wl = [("dve", nfmcopy_t[t]), ("dve", t_brow)]
                if t >= 1:
                    wl.append(("se", silu_t[(t - 1, n)]))
                for c in range(8):
                    S.emit("tensor", lambda e, cc=c, nn=n, s2=t % 2: e.matmul(
                        bank(nn), nfm_wr[s2][:, cc * 128:(cc + 1) * 128],
                        wqk_sb[cc][:, nn * 512:(nn + 1) * 512],
                        start=(cc == 0), stop=False),
                        waits=(wl if c == 0 else []), inc="pe")
                S.emit("tensor", lambda e, nn=n: e.matmul(
                    bank(nn), onesb[:], brow_qk[0:1, nn * 512:(nn + 1) * 512],
                    start=False, stop=True), inc="pe")
                mmn_t[(t, n)] = S.t("pe")

        def emit_silu_affine(t):
            for n in range(4):
                p = n // 2
                wl = [("pe", mmn_t[(t, n)])]
                prev = (t, p - 1) if p >= 1 else ((t - 1, 1) if t >= 1
                                                  else None)
                if prev in kaff_t:
                    wl.append(("dve", kaff_t[prev]))
                S.emit("scalar", lambda e, nn=n: e.activation(
                    out=qksilu[nn % 2][:], in_=bank(nn), func=AF.Silu),
                    waits=wl, inc="se")
                silu_t[(t, n)] = S.t("se")
                if n % 2 == 0:
                    continue
                qkpair = qsb[:, 1024:2048]
                wl = [("se", silu_t[(t, n)])]
                npair = t * 2 + p
                if npair >= 1:
                    wl.append(qdma_t[npair - 1])
                S.emit("vector", lambda e, pp=p: e.tensor_mul(
                    qtm[:], qkpair,
                    reps["g0"][:, pp * 1024:(pp + 1) * 1024]), waits=wl,
                    inc="dve")
                S.emit("vector", lambda e, pp=p: e.tensor_add(
                    qtm[:], qtm[:],
                    reps["b0"][:, pp * 1024:(pp + 1) * 1024]), inc="dve")
                t_q = S.t("dve")
                S.dma("sync",
                      q_h[t * 128:(t + 1) * 128, p * 1024:(p + 1) * 1024],
                      qtm[:], waits=[("dve", t_q)],
                      inc="stq%d" % (p % 2))
                qdma_t[npair] = ("stq%d" % (p % 2), S.t("stq%d" % (p % 2)))
                if dbg:
                    S.dma("gpsimd",
                          qf_d[t * 128:(t + 1) * 128,
                               p * 1024:(p + 1) * 1024],
                          qtm[:], inc="stdbg")
                wl = []
                if t in (27, 28, 30, 31) and p == 0:
                    wl += [("pe", t_setup_pe), ("se", t_setup_se)]
                S.emit("vector", lambda e, pp=p, tt=t: e.tensor_mul(
                    k_tile[tt][:, pp * 1024:(pp + 1) * 1024], qkpair,
                    reps["g1"][:, pp * 1024:(pp + 1) * 1024]), waits=wl,
                    inc="dve")
                S.emit("vector", lambda e, pp=p, tt=t: e.tensor_add(
                    k_tile[tt][:, pp * 1024:(pp + 1) * 1024],
                    k_tile[tt][:, pp * 1024:(pp + 1) * 1024],
                    reps["b1"][:, pp * 1024:(pp + 1) * 1024]), inc="dve")
                kaff_t[(t, p)] = S.t("dve")

        def emit_prefetch(t):
            if t + 2 < NT:
                sm = "ldx%d" % (t % 2)
                S.dma("sync", xb[t % 2], x_d[(t + 2) * 128:(t + 3) * 128, :],
                      waits=[("se", sq1_t[t]), ("dve", r1_t[t])], inc=sm)
                xld_t[t + 2] = (sm, S.t(sm))

        emit_chain_pair(0, 1)
        emit_core(0)
        for i in range(NT // 2):
            a, b = 2 * i, 2 * i + 1
            emit_transp(a)
            emit_mms(a)
            emit_core(b)
            emit_transp(b)
            emit_silu_affine(a)
            emit_mms(b)
            emit_prefetch(a)
            emit_prefetch(b)
            emit_chain_pair(a + 2, b + 2, guard_pe=transp_t[b])
            emit_silu_affine(b)
            if a + 2 < NT:
                emit_core(a + 2)

        t_klast = ("dve", S.t("dve"))
        t_qall = [("stq0", S.t("stq0")), ("stq1", S.t("stq1"))]
        if dbg:
            for t in range(NT):
                S.dma("gpsimd", kf_d[t * 128:(t + 1) * 128, :], k_tile[t],
                      waits=[t_klast], inc="stdbg")

        pemm2_t = {}
        exp_t = {}
        attnts_t = {}

        wsub_jobs = ([("wout", j, h) for j in range(16) for h in range(2)] +
                     [("w1s", ft, h) for ft in range(32) for h in range(2)] +
                     [("w2s", cd, h) for cd in range(8) for h in range(8)])
        t_stgB_free = [S.t("dve"), S.t("dve")]
        wsi = [0]
        nacc = {"w1s": 0, "w2s": 0}
        wsl = [0]

        def emit_wsubs(kmax):
            while wsi[0] < min(kmax, len(wsub_jobs)):
                kind, a, h = wsub_jobs[wsi[0]]
                slot = wsi[0] % 2
                if kind == "wout":
                    src = wout_d[a * 128:(a + 1) * 128, h * 512:(h + 1) * 512]
                elif kind == "w1s":
                    src = w1_d[a * 128:(a + 1) * 128, h * 512:(h + 1) * 512]
                else:
                    src = w2_d[a * 128:(a + 1) * 128, h * 512:(h + 1) * 512]
                S.dma("gpsimd", stgB[slot], src,
                      waits=[("dve", t_stgB_free[slot])],
                      inc="ldw%d" % slot)
                tl = ("ldw%d" % slot, S.t("ldw%d" % slot))
                if kind == "wout":
                    S.emit("vector", lambda e, aa=a, hh=h, s2=slot:
                           e.tensor_copy(
                               wout_sb[aa][:, hh * 512:(hh + 1) * 512],
                               stgB[s2]), waits=[tl], inc="dve")
                else:
                    acc = absacc[:, 0:1] if kind == "w1s" else absacc[:, 1:2]
                    if nacc[kind] == 0:
                        S.emit("vector", lambda e, s2=slot, ac=acc:
                               e.reduce_sum(ac, stgB[s2], axis=AX,
                                            apply_absolute_value=True),
                               waits=[tl], inc="dve")
                    else:
                        S.emit("vector", lambda e, s2=slot: e.reduce_sum(
                            tmpred[:], stgB[s2], axis=AX,
                            apply_absolute_value=True),
                            waits=[tl], inc="dve")
                        t_redw = S.t("dve")
                        S.emit("vector", lambda e, ac=acc: e.tensor_add(
                            ac, ac, tmpred[:]), waits=[("dve", t_redw)],
                            inc="dve")
                    nacc[kind] += 1
                t_stgB_free[slot] = S.t("dve")
                wsi[0] += 1

        for c in range(16):
            half = c % 2
            slot = c % 2
            if c == 0:
                S.dma("sync", qslice[0].rearrange("p (t j) -> p t j", j=128),
                      q_h3[:, :, 0:128],
                      waits=t_qall + [t_klast], inc="ldq0")
                qld_t = {0: ("ldq0", S.t("ldq0"))}
                S.dma("sync", qslice[1].rearrange("p (t j) -> p t j", j=128),
                      q_h3[:, :, 128:256], inc="ldq1")
                qld_t[1] = ("ldq1", S.t("ldq1"))
            elif c + 1 < 16:
                sm = "ldq%d" % ((c + 1) % 2)
                S.dma("sync",
                      qslice[(c + 1) % 2].rearrange("p (t j) -> p t j", j=128),
                      q_h3[:, :, (c + 1) * 128:(c + 2) * 128],
                      waits=[("pe", pemm2_t[c - 1])], inc=sm)
                qld_t[c + 1] = (sm, S.t(sm))
            wl = [qld_t[c], t_klast]
            if c >= 2:
                wl += [("se", exp_t[c - 2]), ("dve", attnts_t[c - 2])]
            for n in range(4):
                for t2 in range(NT):
                    S.emit("tensor", lambda e, nn=n, tx=t2, sx=slot, hx=half:
                           e.matmul(
                               bank(hx * 4 + nn),
                               qslice[sx][:, tx * 128:(tx + 1) * 128],
                               k_tile[tx][:, nn * 512:(nn + 1) * 512],
                               start=(tx == 0), stop=(tx == NT - 1)),
                           waits=(wl if (n == 0 and t2 == 0) else []),
                           inc="pe")
            pemm2_t[c] = S.t("pe")
            phalf = psum[:, half * 2048:half * 2048 + 2048]
            wl = [("pe", pemm2_t[c])]
            if c >= 2:
                wl.append(("se", exp_t[c - 2]))
            S.emit("vector", lambda e, ph=phalf, cc=c: e.reduce_max(
                sl(cc, 10), ph, axis=AX, negate=True), waits=wl, inc="dve")
            t_rmax = S.t("dve")
            S.emit("vector", lambda e, cc=c: e.tensor_scalar(
                out=sl(cc, 11), in0=sl(cc, 10), scalar1=1.0 / NTOK,
                scalar2=None, op0=ALU.mult), waits=[("dve", t_rmax)],
                inc="dve")
            t_be = S.t("dve")
            wl = [("dve", t_be)]
            if c >= 1:
                wl.append(("dve", attnts_t[c - 1]))
            S.emit("scalar", lambda e, ph=phalf, cc=c: e.activation(
                out=attn_un[:], in_=ph, func=AF.Exp, scale=1.0 / NTOK,
                bias=sl(cc, 11), accum_out=sl(cc, 12)), waits=wl, inc="se")
            exp_t[c] = S.t("se")
            S.emit("vector", lambda e, cc=c: e.reciprocal(
                sl(cc, 13), sl(cc, 12)), waits=[("se", exp_t[c])], inc="dve")
            t_rz = S.t("dve")
            wl2 = [("dve", t_rz)]
            if c == 0:
                wl2 += [("stn0", 16 * (NT // 2)), ("stn1", 16 * (NT // 2))]
            S.emit("vector", lambda e, cc=c: e.tensor_scalar(
                out=attn_dump[:], in0=attn_un[:], scalar1=sl(cc, 13),
                scalar2=0.0, op0=ALU.mult, op1=ALU.add,
                accum_out=s_sb[:, cc:cc + 1]), waits=wl2,
                inc="dve")
            attnts_t[c] = S.t("dve")
            emit_wsubs((c + 1) * 10)
        emit_wsubs(len(wsub_jobs))
        t_absdone = S.t("dve")
        t_attnlast = S.t("dve")
        if dbg:
            S.dma("gpsimd", sdbg_d, s_sb, waits=[("dve", t_attnlast)],
                  inc="stdbg")

        wl = [("dve", t_absdone), ("se", exp_t[15]), ("dve", t_attnlast)]
        S.emit("tensor", lambda e: e.matmul(
            psum[0:1, 3584:3585], absacc[:, 0:1], onescol, start=True,
            stop=True),
            waits=wl, inc="pe")
        S.emit("tensor", lambda e: e.matmul(
            psum[0:1, 3585:3586], absacc[:, 1:2], onescol, start=True,
            stop=True),
            inc="pe")
        S.emit("scalar", lambda e: e.activation(
            out=mrow[0:1, 0:1], in_=psum[0:1, 3584:3585], func=AF.Identity,
            scale=1.0 / (FF * D)), waits=[("pe", S.t("pe"))], inc="se")
        S.emit("scalar", lambda e: e.activation(
            out=mrow[0:1, 1:2], in_=psum[0:1, 3585:3586], func=AF.Identity,
            scale=1.0 / (FF * D)), inc="se")
        S.emit("vector", lambda e: e.tensor_scalar_max(
            mrow[0:1, 0:2], mrow[0:1, 0:2], 1e-8),
            waits=[("se", S.t("se"))], inc="dve")
        S.emit("vector", lambda e: e.reciprocal(mrow[0:1, 0:2], mrow[0:1, 0:2]),
               inc="dve")
        t_minv = S.t("dve")
        S.dma("gpsimd", bcast_h[0:1, 0:2], mrow[0:1, 0:2],
              waits=[("dve", t_minv)], inc="ldr")
        for k in [2, 4, 8, 16, 32, 64, 128]:
            S.dma("gpsimd", bcast_h[0:1, k:2 * k], bcast_h[0:1, 0:k],
                  waits=[("ldr", S.t("ldr"))], inc="ldr")
        S.dma("gpsimd", ppf[:, 120:122],
              bcast_h.rearrange("a (p c) -> (a p) c", p=128),
              waits=[("ldr", S.t("ldr"))], inc="ldr")
        t_invdone = ("ldr", S.t("ldr"))

        for h in range(2):
            S.dma("gpsimd", stgB[0][0:1, 0:512],
                  bout_d[0:1, h * 512:(h + 1) * 512],
                  waits=[("dve", t_stgB_free[0])], inc="ldw0")
            S.emit("vector", lambda e, hh=h: e.tensor_copy(
                boutb[0:1, hh * 512:(hh + 1) * 512], stgB[0][0:1, 0:512]),
                waits=[("ldw0", S.t("ldw0")), t_klast], inc="dve")
            t_stgB_free[0] = S.t("dve")
        t_boutb = S.t("dve")

        whid_done = 0
        t_ubf3a_free = [S.t("pe"), S.t("pe")]
        t_fold_n = {}
        first_whid = [True]
        ubf3a2 = [qsb[:, 0:512], qsb[:, 512:1024]]
        for n in [0, 4, 1, 5, 2, 6, 3, 7]:
            for c in range(8):
                sub_i = len(t_fold_n) * 8 + c
                slot = sub_i % 2
                S.dma("gpsimd", stgB[slot],
                      whid_d[c * 128:(c + 1) * 128, n * 512:(n + 1) * 512],
                      waits=[("dve", t_stgB_free[slot])],
                      inc="ldw%d" % slot)
                tl = ("ldw%d" % slot, S.t("ldw%d" % slot))
                wl = [tl, ("pe", t_ubf3a_free[slot]), t_invdone]
                if first_whid[0]:
                    first_whid[0] = False
                    wl.append(t_klast)
                    wl.append(("pe", pemm2_t[15]))
                S.emit("vector", lambda e, s2=slot: e.tensor_copy(
                    ubf3a2[s2][:], stgB[s2]), waits=wl, inc="dve")
                t_cast = S.t("dve")
                for jj in range(4):
                    jh = n * 4 + jj
                    S.emit("tensor", lambda e, jx=jj, jhx=jh, cc=c, s2=slot:
                           e.matmul(bank(7, 4 + cc * 32 + jhx, 1),
                                    ubf3a2[s2][:, jx * 128:(jx + 1) * 128],
                                    lnb_pp[:, cc:cc + 1], start=True,
                                    stop=True),
                           waits=([("dve", t_cast)] if jj == 0 else []),
                           inc="pe")
                t_ubf3a_free[slot] = S.t("pe")
                S.emit("vector", lambda e, cc=c, nn=n, s2=slot: e.tensor_scalar(
                    out=whid_ck[cc][:, nn * 512:(nn + 1) * 512], in0=stgB[s2],
                    scalar1=lng_pp[:, cc:cc + 1], scalar2=0.0,
                    op0=ALU.mult, op1=ALU.add), inc="dve")
                t_stgB_free[slot] = S.t("dve")
            t_fold_n[n] = S.t("dve")
        t_beffmm = S.t("pe")
        S.emit("vector", lambda e: e.tensor_add(
            beff_h[:], bank(7, 4, 32), bhid_pp[:]),
            waits=[("pe", t_beffmm)], inc="dve")
        for cc in range(1, 8):
            S.emit("vector", lambda e, c2=cc: e.tensor_add(
                beff_h[:], beff_h[:], bank(7, 4 + c2 * 32, 32)), inc="dve")
        t_beff = S.t("dve")

        hmm_t = {}
        stt_t = {}
        silu3a_t = {}
        evac3a_t = {}
        evac3a_nt = {}
        ldnfm_t = {}
        ldr1_t = {}
        w1st = {"i": 0, "stg": [t_stgB_free[0], t_stgB_free[1]],
                "qfree": [S.t("pe"), S.t("pe")], "pt6": S.t("dve"),
                "copy": 0}

        def emit_w1q(upto):
            while w1st["i"] < min(upto, 64):
                si = w1st["i"]
                ft, hh = si // 2, si % 2
                slot = si % 2
                S.dma("gpsimd", stgB[slot],
                      w1_d[ft * 128:(ft + 1) * 128, hh * 512:(hh + 1) * 512],
                      waits=[("dve", w1st["stg"][slot])],
                      inc="ldw%d" % slot)
                tl = ("ldw%d" % slot, S.t("ldw%d" % slot))
                wl = [tl, ("dve", t_beff), t_invdone,
                      ("pe", w1st["qfree"][slot])]
                S.emit("vector", lambda e, s2=slot: e.tensor_scalar(
                    out=stgB[s2], in0=stgB[s2], scalar1=inv1_pp[:],
                    scalar2=RND, op0=ALU.mult, op1=ALU.add), waits=wl,
                    inc="dve")
                S.emit("vector", lambda e, s2=slot: e.tensor_scalar(
                    out=stgB[s2], in0=stgB[s2], scalar1=RND, scalar2=1.0,
                    op0=ALU.subtract, op1=ALU.min), inc="dve")
                S.emit("vector", lambda e, s2=slot: e.tensor_scalar(
                    out=w1q_stg[s2][:], in0=stgB[s2], scalar1=-1.0,
                    scalar2=None, op0=ALU.max), inc="dve")
                w1st["stg"][slot] = S.t("dve")
                t_q3 = S.t("dve")
                for i in range(4):
                    S.emit("tensor", lambda e, ii=i, s2=slot: e.transpose(
                        pT6[:, ii * 128:(ii + 1) * 128],
                        w1q_stg[s2][:, ii * 128:(ii + 1) * 128], identb[:]),
                        waits=([("dve", t_q3), ("dve", w1st["pt6"])]
                               if i == 0 else []), inc="pe")
                w1st["qfree"][slot] = S.t("pe")
                t_tr = S.t("pe")
                S.emit("vector", lambda e, ftx=ft, hx=hh: e.tensor_copy(
                    KT_w1t_3d[:, hx * 4:hx * 4 + 4,
                              ftx * 128:(ftx + 1) * 128],
                    pT6[:, 0:512].rearrange("p (c m) -> p c m", m=128)),
                    waits=[("pe", t_tr)], inc="dve")
                w1st["pt6"] = S.t("dve")
                w1st["copy"] = S.t("dve")
                w1st["i"] += 1

        for g in range(8):
            wl = []
            if g == 0:
                wl += [("stn0", 16 * (NT // 2)), ("stn1", 16 * (NT // 2)),
                       ("dve", t_attnlast)]
            else:
                wl.append(("pe", hmm_t[g - 1]))
            S.dma("sync", nfm_grp3d, nfm_h3[:, :, g * 512:(g + 1) * 512],
                  waits=wl, inc="ldnf")
            ldnfm_t[g] = ("ldnf", S.t("ldnf"))
            for jj in range(16):
                for hv in range(2):
                    jh = jj if hv == 0 else 16 + jj
                    bk = (2 * jj + hv) % 4
                    wl = [ldnfm_t[g], ("dve", t_fold_n[jh // 4])]
                    prev = (g, jj - 2) if jj >= 2 else (
                        (g - 1, 14 + jj) if g >= 1 else None)
                    if prev in silu3a_t:
                        wl.append(("se", silu3a_t[prev]))
                    for c in range(8):
                        S.emit("tensor", lambda e, cc=c, jhx=jh, bkx=bk:
                               e.matmul(
                                   bank(bkx),
                                   whid_ck[cc][:, jhx * 128:(jhx + 1) * 128],
                                   nfm_grp[:, cc * 512:(cc + 1) * 512],
                                   start=(cc == 0), stop=(cc == 7)),
                               waits=(wl if c == 0 else []), inc="pe")
                t_pair = S.t("pe")
                wl = [("pe", t_pair), ("dve", t_beff)]
                prevs = (g, jj - 1) if jj >= 1 else (
                    (g - 1, 15) if g >= 1 else None)
                if prevs in stt_t:
                    wl.append(("dve", stt_t[prevs]))
                if g == 0 and jj == 0:
                    wl.append(("dve", t_attnlast))
                S.emit("scalar", lambda e, jx=jj: e.activation(
                    out=vg[:, 0:512], in_=bank((2 * jx) % 4), func=AF.Silu,
                    bias=beff_h[:, jx:jx + 1]), waits=wl, inc="se")
                S.emit("scalar", lambda e, jx=jj: e.activation(
                    out=vg[:, 512:1024], in_=bank((2 * jx + 1) % 4),
                    func=AF.Silu, bias=beff_h[:, 16 + jx:17 + jx]), inc="se")
                silu3a_t[(g, jj)] = S.t("se")
                wl = [("se", silu3a_t[(g, jj)])]
                if g == 0 and jj < 4:
                    wl.append(("pe", pemm2_t[15]))
                S.emit("vector", lambda e, jx=jj: e.scalar_tensor_tensor(
                    out=A_sl[jx], in0=vg[:, 0:512],
                    scalar=s_sb[:, jx:jx + 1], in1=vg[:, 512:1024],
                    op0=ALU.mult, op1=ALU.mult), waits=wl, inc="dve")
                stt_t[(g, jj)] = S.t("dve")
            hmm_t[g] = S.t("pe")
            for s4 in range(4):
                tt = g * 4 + s4
                wl = [("str1", 16 * NT)]
                if tt >= 1:
                    wl.append(("dve", evac3a_t[tt - 1]))
                else:
                    wl.append(("dve", t_attnlast))
                S.dma("sync", r1rd, r1_h[tt * 128:(tt + 1) * 128, :],
                      waits=wl, inc="ldr1")
                ldr1_t[tt] = ("ldr1", S.t("ldr1"))
                for n in range(2):
                    wl = [("dve", stt_t[(g, 15)]), ("dve", t_boutb)]
                    if (tt - 1, n) in evac3a_nt:
                        wl.append(("dve", evac3a_nt[(tt - 1, n)]))
                    for j in range(16):
                        S.emit("tensor", lambda e, jx=j, nn=n, sx=s4: e.matmul(
                            bank(4 + nn),
                            A_sl[jx][:, sx * 128:(sx + 1) * 128],
                            wout_sb[jx][:, nn * 512:(nn + 1) * 512],
                            start=(jx == 0), stop=False),
                            waits=(wl if j == 0 else []), inc="pe")
                    S.emit("tensor", lambda e, nn=n: e.matmul(
                        bank(4 + nn), onesb[:],
                        boutb[0:1, nn * 512:(nn + 1) * 512],
                        start=False, stop=True), inc="pe")
                    t_omm = S.t("pe")
                    wl2 = [("pe", t_omm), ldr1_t[tt]]
                    if tt >= 1:
                        wl2.append(("stx1", 16 * tt))
                    else:
                        wl2.append(("str1", 16 * NT))
                    S.emit("vector", lambda e, nn=n: e.tensor_add(
                        evacC[:, nn * 512:(nn + 1) * 512], bank(4 + nn),
                        r1rd[:, nn * 512:(nn + 1) * 512]), waits=wl2,
                        inc="dve")
                    evac3a_nt[(tt, n)] = S.t("dve")
                evac3a_t[tt] = S.t("dve")
                S.dma("gpsimd", x1_h[tt * 128:(tt + 1) * 128, :], evacC,
                      waits=[("dve", evac3a_t[tt])], inc="stx1")
            emit_w1q((g + 1) * 8)
        emit_w1q(64)
        t_w1done = w1st["copy"]
        t_hlast = hmm_t[7]
        t_out3a_last = S.t("pe")
        t_x1all = S.t("stx1")

        for h in range(2):
            S.dma("gpsimd", stgB[0][0:1, 0:512],
                  fb2_d[0:1, h * 512:(h + 1) * 512],
                  waits=[("dve", w1st["stg"][0])], inc="ldw0")
            S.emit("vector", lambda e, hh=h: e.tensor_copy(
                fb2b[0:1, hh * 512:(hh + 1) * 512], stgB[0][0:1, 0:512]),
                waits=[("ldw0", S.t("ldw0")), ("pe", w1st["qfree"][0]),
                       ("pe", w1st["qfree"][1])], inc="dve")
            w1st["stg"][0] = S.t("dve")
        t_fb2 = S.t("dve")

        w2_copy_t = 0
        t_pt7_free = t_beff
        t_w2stg = [w1st["stg"][0], w1st["stg"][1]]
        t_w2q_free = [t_fb2, t_fb2]
        for cd in range(8):
            for ss in range(8):
                si = cd * 8 + ss
                slot = si % 2
                S.dma("gpsimd", stgB[slot],
                      w2_d[cd * 128:(cd + 1) * 128, ss * 512:(ss + 1) * 512],
                      waits=[("dve", t_w2stg[slot])], inc="ldw%d" % slot)
                tl = ("ldw%d" % slot, S.t("ldw%d" % slot))
                S.emit("vector", lambda e, s2=slot: e.tensor_scalar(
                    out=stgB[s2], in0=stgB[s2], scalar1=inv2_pp[:],
                    scalar2=RND, op0=ALU.mult, op1=ALU.add),
                    waits=[tl, ("dve", t_w2q_free[slot]), t_invdone],
                    inc="dve")
                S.emit("vector", lambda e, s2=slot: e.tensor_scalar(
                    out=stgB[s2], in0=stgB[s2], scalar1=RND, scalar2=1.0,
                    op0=ALU.subtract, op1=ALU.min), inc="dve")
                S.emit("vector", lambda e, s2=slot: e.tensor_scalar(
                    out=w2q_stg[s2][:], in0=stgB[s2], scalar1=-1.0,
                    scalar2=None, op0=ALU.max), inc="dve")
                t_w2stg[slot] = S.t("dve")
                t_q3 = S.t("dve")
                for i in range(4):
                    wl = []
                    if i == 0:
                        wl = [("dve", t_q3), ("dve", t_pt7_free)]
                    S.emit("tensor", lambda e, ii=i, s2=slot: e.transpose(
                        pT7[:, ii * 128:(ii + 1) * 128],
                        w2q_stg[s2][:, ii * 128:(ii + 1) * 128], identb[:]),
                        waits=wl, inc="pe")
                t_tr = S.t("pe")
                wl = [("pe", t_tr)]
                if si == 0:
                    wl.append(("pe", t_hlast))
                S.emit("vector", lambda e, cdx=cd, ssx=ss: e.tensor_copy(
                    KT_w2t_3d[:, ssx * 4:ssx * 4 + 4,
                              cdx * 128:(cdx + 1) * 128],
                    pT7[:, 0:512].rearrange("p (c m) -> p c m", m=128)),
                    waits=wl, inc="dve")
                t_pt7_free = S.t("dve")
                t_w2q_free[slot] = S.t("dve")
                w2_copy_t = S.t("dve")
        t_w2done = w2_copy_t

        ffn1_t = {}
        ffn2_t = {}
        gelu_t = {}
        ycopy_t = {}
        evac3b_t = {}
        evac3b_nt = {}
        ldx1_t = {}
        lnse3_t = {}
        sq3_t = {}
        transp3_t = {}

        y_t = {}
        for g in range(8):
            for s4 in range(4):
                tt = g * 4 + s4
                wl = [("stx1", 16 * NT)]
                if s4 >= 2:
                    wl.append(("dve", y_t[tt - 2]))
                elif g >= 1:
                    wl.append(("dve", evac3b_nt[(tt - 2, 1)]))
                else:
                    wl += [("dve", evac3a_t[NT - 1]),
                           ("dve", stt_t[(7, 15)]),
                           ("dve", t_w2stg[0]), ("dve", t_w2stg[1])]
                sm = "ldy%d" % (tt % 2)
                S.dma("sync", x1rd[tt % 2], x1_h[tt * 128:(tt + 1) * 128, :],
                      waits=wl, inc=sm)
                ldx1_t[tt] = (sm, S.t(sm))
                wl = [ldx1_t[tt]]
                if tt >= 2:
                    wl.append(("se", lnse3_t[tt - 2]))
                S.emit("vector", lambda e, x2=x1rd[tt % 2], tx=tt:
                       e.reduce_sum(sl(tx, 0), x2, axis=AX), waits=wl,
                       inc="dve")
                t_red3 = S.t("dve")
                S.emit("vector", lambda e, tx=tt: e.tensor_scalar(
                    out=sl(tx, 1), in0=sl(tx, 0), scalar1=-1.0 / D,
                    scalar2=None, op0=ALU.mult), waits=[("dve", t_red3)],
                    inc="dve")
                t_negmu = S.t("dve")
                wl = [("dve", t_negmu)]
                if tt == 0:
                    wl.append(("pe", t_out3a_last))
                S.emit("scalar", lambda e, x2=x1rd[tt % 2], tx=tt:
                       e.activation(out=sqdump3b[:], in_=x2, func=AF.Square,
                                    bias=sl(tx, 1), accum_out=sl(tx, 2)),
                       waits=wl, inc="se")
                sq3_t[tt] = S.t("se")
                S.emit("scalar", lambda e, tx=tt: e.activation(
                    out=sl(tx, 3), in_=sl(tx, 2), func=AF.Sqrt, scale=1.0 / D,
                    bias=epsc), inc="se")
                lnse3_t[tt] = S.t("se")
                S.emit("vector", lambda e, tx=tt: e.reciprocal(
                    sl(tx, 4), sl(tx, 3)), waits=[("se", lnse3_t[tt])],
                    inc="dve")
                t_rstd3 = S.t("dve")
                wl = []
                if tt == 0:
                    wl.append(("dve", t_w2done))
                else:
                    wl.append(("pe", transp3_t[tt - 1]))
                wl.append(("dve", t_rstd3))
                S.emit("vector", lambda e, x2=x1rd[tt % 2], tx=tt:
                       e.tensor_scalar(
                           out=y_bf[:], in0=x2, scalar1=sl(tx, 1),
                           scalar2=sl(tx, 4), op0=ALU.add, op1=ALU.mult),
                       waits=wl, inc="dve")
                t_y = S.t("dve")
                y_t[tt] = t_y
                wl = [("dve", t_y)]
                if tt >= 1:
                    wl.append(("dve", ycopy_t[tt - 1]))
                else:
                    wl.append(("dve", t_w1done))
                for k in range(8):
                    S.emit("tensor", lambda e, kk=k: e.transpose(
                        pT6[:, kk * 128:(kk + 1) * 128],
                        y_bf[:, kk * 128:(kk + 1) * 128], identb[:]),
                        waits=(wl if k == 0 else []), inc="pe")
                transp3_t[tt] = S.t("pe")
                wl = [("pe", transp3_t[tt])]
                if s4 == 0:
                    if g >= 1:
                        wl.append(("pe", ffn1_t[g - 1]))
                    else:
                        wl.append(("pe", t_hlast))
                S.emit("vector", lambda e, sx=s4: e.tensor_copy(
                    nfm_grp3d[:, :, sx * 128:(sx + 1) * 128],
                    pT6.rearrange("p (c m) -> p c m", m=128)),
                    waits=wl, inc="dve")
                ycopy_t[tt] = S.t("dve")
            t_ygrp = S.t("dve")
            for j in range(32):
                bk = j % 4
                wl = [("dve", t_ygrp), ("dve", t_w1done)]
                prev = (g, j - 4) if j >= 4 else (
                    (g - 1, 28 + j) if g >= 1 else None)
                if prev in gelu_t:
                    wl.append(("se", gelu_t[prev]))
                for c in range(8):
                    S.emit("tensor", lambda e, cc=c, jx=j, bkx=bk: e.matmul(
                        bank(bkx), w1t_ck[cc][:, jx * 128:(jx + 1) * 128],
                        nfm_grp[:, cc * 512:(cc + 1) * 512],
                        start=(cc == 0), stop=(cc == 7)),
                        waits=(wl if c == 0 else []), inc="pe")
                t_f1 = S.t("pe")
                wl = [("pe", t_f1)]
                if g >= 1:
                    wl.append(("pe", ffn2_t[g - 1]))
                elif j == 0:
                    wl.append(("pe", t_out3a_last))
                S.emit("scalar", lambda e, jx=j, bkx=bk: e.activation(
                    out=h1_sb[jx][:], in_=bank(bkx), func=AF.Gelu,
                    bias=fb1_pp[:, jx:jx + 1]), waits=wl, inc="se")
                gelu_t[(g, j)] = S.t("se")
            ffn1_t[g] = S.t("pe")
            t_gelu_all = S.t("se")
            for s4 in range(4):
                tt = g * 4 + s4
                if s4 < 2:
                    wl = [("dve", y_t[g * 4 + s4 + 2])]
                else:
                    wl = [("dve", evac3b_nt[(tt - 2, 1)])]
                sm = "ldy%d" % (tt % 2)
                S.dma("sync", x1rd[tt % 2], x1_h[tt * 128:(tt + 1) * 128, :],
                      waits=wl, inc=sm)
                t_x1b = (sm, S.t(sm))
                for n in range(2):
                    wl = [("se", t_gelu_all), ("dve", t_w2done),
                          ("dve", t_fb2), t_x1b]
                    if (tt - 1, n) in evac3b_nt:
                        wl.append(("dve", evac3b_nt[(tt - 1, n)]))
                    elif tt == 0:
                        wl.append(("dve", evac3a_nt[(NT - 1, n)]))
                    for j in range(32):
                        S.emit("tensor", lambda e, jx=j, nn=n, sx=s4: e.matmul(
                            bank(4 + nn),
                            h1_sb[jx][:, sx * 128:(sx + 1) * 128],
                            w2t_ck[jx][:, nn * 512:(nn + 1) * 512],
                            start=(jx == 0), stop=False),
                            waits=(wl if j == 0 else []), inc="pe")
                    S.emit("tensor", lambda e, nn=n: e.matmul(
                        bank(4 + nn), onesb[:],
                        fb2b[0:1, nn * 512:(nn + 1) * 512],
                        start=False, stop=True), inc="pe")
                    t_omm = S.t("pe")
                    wl2 = [("pe", t_omm), t_x1b]
                    if tt >= 1:
                        wl2.append(("stout", 16 * tt))
                    else:
                        wl2.append(("stx1", t_x1all))
                    S.emit("vector", lambda e, nn=n, sx=s4: e.tensor_add(
                        evacC[:, nn * 512:(nn + 1) * 512], bank(4 + nn),
                        x1rd[(g * 4 + sx) % 2][:, nn * 512:(nn + 1) * 512]),
                        waits=wl2, inc="dve")
                    evac3b_nt[(tt, n)] = S.t("dve")
                evac3b_t[tt] = S.t("dve")
                S.dma("gpsimd", out_d[tt * 128:(tt + 1) * 128, :], evacC,
                      waits=[("dve", evac3b_t[tt])], inc="stout")
            ffn2_t[g] = S.t("pe")

        if dbg:
            t_done = S.t("pe")
            for hh in range(8):
                S.dma("gpsimd", ktf_d[:, hh * 8192:(hh + 1) * 8192],
                      KT[:, hh * 8192:(hh + 1) * 8192],
                      waits=[("pe", t_done)], inc="stdbg")
            S.dma("gpsimd", h1f_d, wA[:], waits=[("pe", t_done)], inc="stdbg")
            S.dma("gpsimd", ppf_d, ppf[:], waits=[("pe", t_done)],
                  inc="stdbg")
        fw = [("stout", 16 * NT)]
        if dbg:
            fw.append(("stdbg", S.t("stdbg")))
        S.emit("gpsimd", lambda e: e.nop(), waits=fw)

        with nc.Block() as block:
            @block.sync
            def _(eng):
                S.replay("sync", eng)

            @block.vector
            def _(eng):
                S.replay("vector", eng)

            @block.scalar
            def _(eng):
                S.replay("scalar", eng)

            @block.tensor
            def _(eng):
                S.replay("tensor", eng)

            @block.gpsimd
            def _(eng):
                S.replay("gpsimd", eng)

    return nc


_NC_CACHE = {}


def _get_nc(dbg=False):
    key = bool(dbg)
    if key not in _NC_CACHE:
        _NC_CACHE[key] = build_nc(dbg=key)
    return _NC_CACHE[key]


def kernel(L0_attn_x, L0_attn_ln_g, L0_attn_ln_b, L0_attn_W_hidden,
           L0_attn_b_hidden, L0_attn_W_qk, L0_attn_b_qk, L0_attn_os_gamma,
           L0_attn_os_beta, L0_attn_W_out, L0_attn_b_out, L0_ffn_W1,
           L0_ffn_b1, L0_ffn_W2, L0_ffn_b2):
    x = np.ascontiguousarray(np.asarray(L0_attn_x, dtype=np.float32))
    B = x.shape[0]
    assert B == 8 and x.shape[1] == NTOK and x.shape[2] == D

    com = dict(
        wqk=np.ascontiguousarray(np.asarray(L0_attn_W_qk, np.float32)),
        whid=np.ascontiguousarray(np.asarray(L0_attn_W_hidden, np.float32)),
        wout=np.ascontiguousarray(np.asarray(L0_attn_W_out, np.float32)),
        w1=np.ascontiguousarray(np.asarray(L0_ffn_W1, np.float32)),
        w2=np.ascontiguousarray(np.asarray(L0_ffn_W2, np.float32)),
        lng=np.asarray(L0_attn_ln_g, np.float32).reshape(1, D),
        lnb=np.asarray(L0_attn_ln_b, np.float32).reshape(1, D),
        bqk=np.asarray(L0_attn_b_qk, np.float32).reshape(1, QKD),
        bhid=np.asarray(L0_attn_b_hidden, np.float32).reshape(1, HID2),
        bout=np.asarray(L0_attn_b_out, np.float32).reshape(1, D),
        g0=np.ascontiguousarray(
            np.asarray(L0_attn_os_gamma, np.float32)[0]).reshape(1, QKD),
        b0=np.ascontiguousarray(
            np.asarray(L0_attn_os_beta, np.float32)[0]).reshape(1, QKD),
        g1=np.ascontiguousarray(
            np.asarray(L0_attn_os_gamma, np.float32)[1]).reshape(1, QKD),
        b1=np.ascontiguousarray(
            np.asarray(L0_attn_os_beta, np.float32)[1]).reshape(1, QKD),
        fb1=np.asarray(L0_ffn_b1, np.float32).reshape(1, FF),
        fb2=np.asarray(L0_ffn_b2, np.float32).reshape(1, D),
        iden=np.eye(128, dtype=np.float32),
    )
    in_maps = [dict(com, x=np.ascontiguousarray(x[i])) for i in range(B)]

    nc = _get_nc(dbg=DBG)
    res = run_bass_kernel_spmd(nc, in_maps, core_ids=list(range(B)))
    out = np.stack([np.asarray(r["out"]) for r in res.results], axis=0)
    if DBG:
        kernel.last_results = res.results
    return out


# version 17
# speedup vs baseline: 1.2496x; 1.2496x over previous
import os
import numpy as np
from contextlib import ExitStack

import concourse.bass as bass
import concourse.mybir as mybir
from concourse.bass_utils import run_bass_kernel_spmd

F32 = mybir.dt.float32
BF16 = mybir.dt.bfloat16
AF = mybir.ActivationFunctionType
ALU = mybir.AluOpType
AX = mybir.AxisListType.X

NTOK, D, QKD, HID2, FF = 4096, 1024, 2048, 4096, 4096
NT = NTOK // 128
EPS = 1e-5
RND = 12582912.0

DBG = bool(int(os.environ.get("MK_DBG", "0")))


class Sched:
    ENGS = ("sync", "vector", "scalar", "tensor", "gpsimd")

    def __init__(self, nc):
        self.nc = nc
        self.q = {e: [] for e in self.ENGS}
        self.cnt = {}
        self.sem_h = {}

    def new_sem(self, stack, name):
        self.sem_h[name] = stack.enter_context(self.nc.semaphore(name))
        self.cnt[name] = 0

    def t(self, sem):
        return self.cnt[sem]

    def emit(self, eng, fn, waits=(), inc=None, amt=1):
        waits = [(s, v) for (s, v) in waits if v > 0]
        for s, v in waits:
            assert v <= self.cnt[s], f"future ticket {s}:{v}>{self.cnt[s]}"
        if eng == "vector":
            assert inc in (None, "dve")
            inc, amt = "dve", 1
        elif eng == "scalar":
            assert inc in (None, "se")
            inc, amt = "se", 1
        if inc is not None:
            self.cnt[inc] += amt
        self.q[eng].append((fn, tuple(waits), inc, amt))

    def dma(self, eng, out, in_, waits=(), inc=None):
        self.emit(eng, lambda e, o=out, i=in_: e.dma_start(out=o, in_=i),
                  waits=waits, inc=inc, amt=16)

    def replay(self, eng_name, eng):
        serial = {"vector": "dve", "scalar": "se"}.get(eng_name)
        k = 0
        kp = 0
        for fn, waits, inc, amt in self.q[eng_name]:
            if serial and k > 0:
                eng.wait_ge(self.sem_h[serial], k)
            if eng_name == "gpsimd" and inc == "pool" and kp > 0:
                eng.wait_ge(self.sem_h["pool"], kp)
            for s, v in waits:
                eng.wait_ge(self.sem_h[s], v)
            instr = fn(eng)
            if inc is not None:
                instr.then_inc(self.sem_h[inc], amt)
            if serial:
                k += 1
            if eng_name == "gpsimd" and inc == "pool":
                kp += 1


def build_nc(dbg=False):
    nc = bass.Bass("TRN2", target_bir_lowering=False, debug=False)

    x_d = nc.dram_tensor("x", [NTOK, D], F32, kind="ExternalInput").ap()
    wqk_d = nc.dram_tensor("wqk", [D, QKD], F32, kind="ExternalInput").ap()
    whid_d = nc.dram_tensor("whid", [D, HID2], F32, kind="ExternalInput").ap()
    wout_d = nc.dram_tensor("wout", [QKD, D], F32, kind="ExternalInput").ap()
    w1_d = nc.dram_tensor("w1", [FF, D], F32, kind="ExternalInput").ap()
    w2_d = nc.dram_tensor("w2", [D, FF], F32, kind="ExternalInput").ap()
    lng_d = nc.dram_tensor("lng", [1, D], F32, kind="ExternalInput").ap()
    lnb_d = nc.dram_tensor("lnb", [1, D], F32, kind="ExternalInput").ap()
    bqk_d = nc.dram_tensor("bqk", [1, QKD], F32, kind="ExternalInput").ap()
    bhid_d = nc.dram_tensor("bhid", [1, HID2], F32, kind="ExternalInput").ap()
    bout_d = nc.dram_tensor("bout", [1, D], F32, kind="ExternalInput").ap()
    g0_d = nc.dram_tensor("g0", [1, QKD], F32, kind="ExternalInput").ap()
    b0_d = nc.dram_tensor("b0", [1, QKD], F32, kind="ExternalInput").ap()
    g1_d = nc.dram_tensor("g1", [1, QKD], F32, kind="ExternalInput").ap()
    b1_d = nc.dram_tensor("b1", [1, QKD], F32, kind="ExternalInput").ap()
    fb1_d = nc.dram_tensor("fb1", [1, FF], F32, kind="ExternalInput").ap()
    fb2_d = nc.dram_tensor("fb2", [1, D], F32, kind="ExternalInput").ap()
    iden_d = nc.dram_tensor("iden", [128, 128], F32, kind="ExternalInput").ap()

    out_d = nc.dram_tensor("out", [NTOK, D], F32, kind="ExternalOutput").ap()

    skind = "ExternalOutput" if dbg else "Internal"
    q_h = nc.dram_tensor("q_h", [NTOK, QKD], BF16, kind="Internal").ap()
    nfm_h = nc.dram_tensor("nfm_h", [D, NTOK], BF16, kind="Internal").ap()
    r1_h = nc.dram_tensor("r1_h", [NTOK, D], F32, kind=skind).ap()
    x1_h = nc.dram_tensor("x1_h", [NTOK, D], F32, kind=skind).ap()
    bcast_h = nc.dram_tensor("bcast_h", [1, 256], F32, kind="Internal").ap()
    if dbg:
        sdbg_d = nc.dram_tensor("sdbg", [128, 16], F32,
                                kind="ExternalOutput").ap()
        qf_d = nc.dram_tensor("qf", [NTOK, QKD], F32,
                              kind="ExternalOutput").ap()
        ktf_d = nc.dram_tensor("ktf", [128, 65536], F32,
                               kind="ExternalOutput").ap()
        h1f_d = nc.dram_tensor("h1f", [128, 16384], F32,
                               kind="ExternalOutput").ap()
        ppf_d = nc.dram_tensor("ppfd", [128, 132], F32,
                               kind="ExternalOutput").ap()
        nfmf_d = nc.dram_tensor("nfmf", [D, NTOK], F32,
                                kind="ExternalOutput").ap()
        kf_d = nc.dram_tensor("kf", [NTOK, QKD], F32,
                              kind="ExternalOutput").ap()

    with ExitStack() as stk:
        sbuf = lambda *a, **k: stk.enter_context(nc.sbuf_tensor(*a, **k))

        KT = sbuf("KT", [128, 65536], BF16)
        wA = sbuf("wA", [128, 16384], BF16)
        scrA = sbuf("scrA", [128, 8192], BF16)
        scrB = sbuf("scrB", [128, 3072], F32)
        scrC = sbuf("scrC", [128, 1024], F32)
        qsb = sbuf("qsb", [128, 2048], BF16)
        nfm = sbuf("nfm", [128, 4096], BF16)
        qtm = sbuf("qtm", [128, 1024], BF16)
        stats = sbuf("stats", [128, 64], F32)
        ppf = sbuf("ppf", [128, 132], F32)
        ppb = sbuf("ppb", [128, 8], BF16)
        identb = sbuf("identb", [128, 128], BF16)
        onesb = sbuf("onesb", [1, 128], BF16)
        onesf = sbuf("onesf", [1, 128], F32)
        onef1 = sbuf("onef1", [1, 1], F32)

        psum = stk.enter_context(nc.psum_tensor("psum", [128, 4096], F32))

        S = Sched(nc)
        for s in ["ldx0", "ldx1", "ldw0", "ldw1", "ldq0", "ldq1", "ldnf",
                  "ldr1", "ldy0", "ldy1", "ldr", "ldr2", "stq0", "stq1",
                  "stn0", "stn1", "str1", "str1o", "stx1", "stout", "pe", "dve", "se",
                  "pool", "stdbg"]:
            S.new_sem(stk, s)

        def bank(b, off=0, w=512):
            return psum[:, b * 512 + off:b * 512 + off + w]

        psum_b = psum.bitcast(BF16)
        pT = psum_b[:, 4096:5120]
        pT6 = psum_b[:, 6144:7168]
        pT7 = psum_b[:, 7168:8192]
        browp = psum[0:1, 3584:4096]

        KTf = KT.bitcast(F32)
        rowf = KTf[0:1, 31744:32768]
        rowf2 = KTf[0:1, 30720:31744]
        wstg = [KTf[:, 28672 + i * 512:28672 + (i + 1) * 512] for i in range(2)]
        idstg = KTf[:, 28672:28800]
        ubf = KT[:, 56320:56832]

        wqk_sb = [wA[:, c * 2048:(c + 1) * 2048] for c in range(8)]
        wout_sb = [wA[:, j * 1024:(j + 1) * 1024] for j in range(16)]
        h1_sb = [wA[:, j * 512:(j + 1) * 512] for j in range(32)]

        k_tile = [KT[:, t * 2048:(t + 1) * 2048] for t in range(NT)]
        w1t_ck = [KT[:, c * 4096:(c + 1) * 4096] for c in range(8)]
        whid_ck = [KT[:, 32768 + c * 4096:32768 + (c + 1) * 4096]
                   for c in range(8)]
        w2t_ck = [KT[:, 32768 + j * 1024:32768 + (j + 1) * 1024]
                  for j in range(32)]
        KT_w1t_3d = KT.rearrange("p (c m) -> p c m", m=4096)
        KT_w2t_3d = KT[:, 32768:65536].rearrange("p (j m) -> p j m", m=1024)

        reps = {v: scrA[:, i * 2048:(i + 1) * 2048]
                for i, v in enumerate(["g0", "b0", "g1", "b1"])}
        qslice = [scrA[:, i * 4096:(i + 1) * 4096] for i in range(2)]
        A_sl = [scrA[:, j * 512:(j + 1) * 512] for j in range(16)]
        sqdump3b = scrA[:, 0:1024]

        xb = [scrB[:, 0:1024], scrB[:, 1024:2048]]
        lnx = scrB[:, 2048:3072]
        attn_un = scrB[:, 0:2048]
        stgB = [scrB[:, 2048:2560], scrB[:, 2560:3072]]
        r1rd = scrB[:, 0:1024]
        vg = scrB[:, 1024:2048]
        x1rd = [scrB[:, 0:1024], scrB[:, 1024:2048]]

        evacC = scrC[:, 0:1024]

        normed_bf = qsb[:, 0:1024]
        qksilu = [qsb[:, 1024:1536], qsb[:, 1536:2048]]
        ubf3a = qsb[:, 0:512]
        w1q_stg = [qsb[:, 0:512], qsb[:, 512:1024]]
        fb2b = qsb[0:1, 0:1024]
        w2q_stg = [qsb[:, 1024:1536], qsb[:, 1536:2048]]
        y_bf = qsb[:, 1024:2048]

        nfm_wr = [nfm[:, 0:1024], nfm[:, 1024:2048]]
        brow_qk = KT[0:1, 63488:65536]
        lnxB = nfm.bitcast(F32)[:, 1024:2048]
        attn_dump = nfm[:, 0:2048]
        nfm_grp = nfm[:]
        nfm_grp3d = nfm.rearrange("p (c m) -> p c m", m=512)
        rb_row = nfm[0:1, 0:2048]

        qch = [qtm[:, 0:512], qtm[:, 512:1024]]
        boutb = qtm[0:1, 0:1024]

        lng_pp = ppf[:, 0:8]
        lnb_pp = ppb[:, 0:8]
        bhid_pp = ppf[:, 8:40]
        fb1_pp = ppf[:, 40:72]
        beff_h = ppf[:, 72:104]
        s_sb = ppf[:, 104:120]
        inv1_pp = ppf[:, 120:121]
        inv2_pp = ppf[:, 121:122]
        absacc = ppf[:, 122:124]
        tmpred = ppf[:, 126:127]
        epsc = ppf[:, 127:128]
        onescol = ppf[:, 128:129]
        mrow = ppf[0:1, 124:126]

        def sl(t, i):
            return stats[:, (t % 4) * 16 + i:(t % 4) * 16 + i + 1]

        nfm_h3 = nfm_h.rearrange("(c p) m -> p c m", p=128)
        q_h3 = q_h.rearrange("(t p) f -> p t f", p=128)

        xld_t = {}
        S.dma("sync", xb[0], x_d[0:128, :], inc="ldx0")
        xld_t[0] = ("ldx0", S.t("ldx0"))
        S.dma("sync", xb[1], x_d[128:256, :], inc="ldx1")
        xld_t[1] = ("ldx1", S.t("ldx1"))

        S.emit("vector", lambda e: e.memset(onesb[:], 1.0))
        S.emit("vector", lambda e: e.memset(onesf[:], 1.0))
        S.emit("vector", lambda e: e.memset(onef1[:], 1.0))
        S.emit("vector", lambda e: e.memset(epsc, EPS))
        S.emit("vector", lambda e: e.memset(onescol, 1.0), inc="dve")
        t_ones = S.t("dve")

        S.dma("gpsimd", idstg, iden_d, inc="ldr")
        S.dma("gpsimd", rowf, lng_d, inc="ldr")
        S.dma("gpsimd", rowf2, lnb_d, inc="ldr2")
        S.emit("vector", lambda e: e.tensor_copy(identb[:], idstg),
               waits=[("ldr", S.t("ldr"))], inc="dve")
        t_ident = S.t("dve")
        first5 = [True]

        def scat(col, row_ap, wts, last=False):
            f = first5[0]
            first5[0] = False
            S.emit("tensor", lambda e, c=col, r=row_ap, ff=f, ll=last: e.matmul(
                bank(5, c, 1), r, onef1[:], start=ff, stop=ll),
                waits=wts, inc="pe")

        for c in range(8):
            scat(c, rowf[0:1, c * 128:(c + 1) * 128],
                 [("ldr", S.t("ldr")), ("ldr2", S.t("ldr2")),
                  ("dve", t_ones)])
        for c in range(8):
            scat(8 + c, rowf2[0:1, c * 128:(c + 1) * 128], [])
        for h in range(4):
            S.dma("gpsimd", rowf, bhid_d[0:1, h * 1024:(h + 1) * 1024],
                  waits=[("pe", S.t("pe"))], inc="ldr")
            tl = S.t("ldr")
            for c in range(8):
                scat(16 + h * 8 + c, rowf[0:1, c * 128:(c + 1) * 128],
                     [("ldr", tl)])
        for h in range(4):
            S.dma("gpsimd", rowf, fb1_d[0:1, h * 1024:(h + 1) * 1024],
                  waits=[("pe", S.t("pe"))], inc="ldr")
            tl = S.t("ldr")
            for c in range(8):
                scat(48 + h * 8 + c, rowf[0:1, c * 128:(c + 1) * 128],
                     [("ldr", tl)], last=(h == 3 and c == 7))
        t_scat = S.t("pe")
        S.emit("vector", lambda e: e.tensor_copy(lng_pp[:], bank(5, 0, 8)),
               waits=[("pe", t_scat)], inc="dve")
        S.emit("vector", lambda e: e.tensor_copy(lnb_pp[:], bank(5, 8, 8)),
               inc="dve")
        S.emit("vector", lambda e: e.tensor_copy(bhid_pp[:], bank(5, 16, 32)),
               inc="dve")
        S.emit("vector", lambda e: e.tensor_copy(fb1_pp[:], bank(5, 48, 32)),
               inc="dve")
        t_ppcopy = S.t("dve")

        t_rep_prev = 0
        for i, vd in enumerate([g0_d, b0_d, g1_d, b1_d]):
            nm = ["g0", "b0", "g1", "b1"][i]
            for h in range(2):
                S.dma("gpsimd", rowf, vd[0:1, h * 1024:(h + 1) * 1024],
                      waits=[("pe", S.t("pe")), ("dve", S.t("dve"))],
                      inc="ldr")
                S.emit("vector", lambda e, hh=h: e.tensor_copy(
                    rb_row[0:1, hh * 1024:(hh + 1) * 1024], rowf),
                    waits=[("ldr", S.t("ldr"))], inc="dve")
            t_rb = S.t("dve")
            for n in range(4):
                S.emit("tensor", lambda e, nn=n: e.matmul(
                    bank(6), onesb[:], rb_row[0:1, nn * 512:(nn + 1) * 512],
                    start=True, stop=True),
                    waits=[("dve", t_rb), ("dve", t_ident),
                           ("dve", t_rep_prev)], inc="pe")
                S.emit("vector", lambda e, nn=n, nmx=nm: e.tensor_copy(
                    reps[nmx][:, nn * 512:(nn + 1) * 512], bank(6)),
                    waits=[("pe", S.t("pe"))], inc="dve")
                t_rep_prev = S.t("dve")

        t_wstg_free = [S.t("dve"), S.t("dve")]
        t_ubf_free = S.t("pe")
        for n in range(4):
            for c in range(8):
                sub = n * 8 + c
                slot = sub % 2
                S.dma("gpsimd", wstg[slot],
                      wqk_d[c * 128:(c + 1) * 128, n * 512:(n + 1) * 512],
                      waits=[("dve", t_wstg_free[slot])],
                      inc="ldw%d" % slot)
                tl = ("ldw%d" % slot, S.t("ldw%d" % slot))
                S.emit("vector", lambda e, s2=slot: e.tensor_copy(
                    ubf[:], wstg[s2]),
                    waits=[tl, ("pe", t_ubf_free)], inc="dve")
                t_cast = S.t("dve")
                S.emit("tensor", lambda e, cc=c: e.matmul(
                    browp, lnb_pp[:, cc:cc + 1], ubf[:],
                    start=(cc == 0), stop=(cc == 7)),
                    waits=[("dve", t_cast)], inc="pe")
                t_ubf_free = S.t("pe")
                S.emit("vector", lambda e, cc=c, nn=n, s2=slot: e.tensor_scalar(
                    out=wqk_sb[cc][:, nn * 512:(nn + 1) * 512], in0=wstg[s2],
                    scalar1=lng_pp[:, cc:cc + 1], scalar2=0.0,
                    op0=ALU.mult, op1=ALU.add),
                    waits=[("dve", t_ppcopy)], inc="dve")
                t_wstg_free[slot] = S.t("dve")
            t_brow_mm = S.t("pe")
            S.emit("scalar", lambda e: e.copy(rowf2[0:1, 0:512], browp),
                   waits=[("pe", t_brow_mm), ("dve", S.t("dve"))], inc="se")
            S.dma("gpsimd", rowf2[0:1, 512:1024],
                  bqk_d[0:1, n * 512:(n + 1) * 512],
                  waits=[("dve", S.t("dve"))], inc="ldr2")
            S.emit("vector", lambda e: e.tensor_add(
                rowf2[0:1, 0:512], rowf2[0:1, 0:512], rowf2[0:1, 512:1024]),
                waits=[("se", S.t("se")), ("ldr2", S.t("ldr2"))], inc="dve")
            S.emit("vector", lambda e, nn=n: e.tensor_copy(
                brow_qk[0:1, nn * 512:(nn + 1) * 512], rowf2[0:1, 0:512]),
                inc="dve")
        t_brow = S.t("dve")
        t_setup_pe = S.t("pe")
        t_setup_se = S.t("se")

        sq1_t = {}
        r1_t = {}
        transp_t = {}
        nfmcopy_t = {}
        silu_t = {}
        kaff_t = {}
        qaff_t = {}
        qdma_t = {}
        mmn_t = {}
        lnse_t = {}
        core_t = {}

        lnx_of = lambda t: lnx if t % 2 == 0 else lnxB
        sq2_t = {}

        last_dump = [0]

        negmu_hold = [0]

        def emit_chain_front(ta, tb):
            tiles = [t for t in (ta, tb) if t is not None and t < NT]

            def both(emit_one):
                for t in tiles:
                    emit_one(t)

            def red1(t):
                wl = [xld_t[t]]
                if t >= 4:
                    wl.append(("se", lnse_t[t - 4]))
                    wl.append(("dve", core_t[t - 4]))
                S.emit("vector", lambda e, x2=xb[t % 2], tt=t: e.reduce_sum(
                    sl(tt, 0), x2, axis=AX), waits=wl, inc="dve")
            both(red1)
            t_red1s = S.t("dve")

            def negmu(t):
                S.emit("vector", lambda e, tt=t: e.tensor_scalar(
                    out=sl(tt, 1), in0=sl(tt, 0), scalar1=-1.0 / D,
                    scalar2=None, op0=ALU.mult), waits=[("dve", t_red1s)],
                    inc="dve")
            both(negmu)
            negmu_hold[0] = S.t("dve")

        def emit_chain_rest(ta, tb, guard_pe=None):
            tiles = [t for t in (ta, tb) if t is not None and t < NT]
            t_negmus = negmu_hold[0]

            def both(emit_one):
                for t in tiles:
                    emit_one(t)

            def sq1(t):
                wl = [("dve", t_negmus)]
                if guard_pe is not None:
                    wl.append(("pe", guard_pe))
                S.emit("scalar", lambda e, x2=xb[t % 2], tt=t: e.activation(
                    out=normed_bf[:], in_=x2, func=AF.Square, bias=sl(tt, 1),
                    accum_out=sl(tt, 2)), waits=wl, inc="se")
                sq1_t[t] = S.t("se")
            both(sq1)

            def sqrt1(t):
                S.emit("scalar", lambda e, tt=t: e.activation(
                    out=sl(tt, 3), in_=sl(tt, 2), func=AF.Sqrt, scale=1.0 / D,
                    bias=epsc), inc="se")
            both(sqrt1)
            t_sqrt1s = S.t("se")

            def recip1(t):
                S.emit("vector", lambda e, tt=t: e.reciprocal(
                    sl(tt, 4), sl(tt, 3)), waits=[("se", t_sqrt1s)], inc="dve")
            both(recip1)
            t_recips = S.t("dve")

            def lnxm(t):
                S.emit("vector", lambda e, x2=xb[t % 2], tt=t,
                       lx=lnx_of(t): e.tensor_scalar(
                    out=lx, in0=x2, scalar1=sl(tt, 1), scalar2=sl(tt, 4),
                    op0=ALU.add, op1=ALU.mult), waits=[("dve", t_recips)],
                    inc="dve")
            both(lnxm)

            def mu2(t):
                S.emit("vector", lambda e, tt=t: e.scalar_tensor_tensor(
                    out=sl(tt, 5), in0=sl(tt, 1), scalar=float(D),
                    in1=sl(tt, 0), op0=ALU.mult, op1=ALU.add), inc="dve")
                S.emit("vector", lambda e, tt=t: e.tensor_scalar(
                    out=sl(tt, 6), in0=sl(tt, 5), scalar1=sl(tt, 4),
                    scalar2=-1.0 / D, op0=ALU.mult, op1=ALU.mult), inc="dve")
            both(mu2)
            t_mu2s = S.t("dve")

            def sq2(t):
                S.emit("scalar", lambda e, tt=t, lx=lnx_of(t): e.activation(
                    out=normed_bf[:], in_=lx, func=AF.Square, bias=sl(tt, 6),
                    accum_out=sl(tt, 7)), waits=[("dve", t_mu2s)], inc="se")
                sq2_t[t] = S.t("se")
            both(sq2)

            def sqrt2(t):
                S.emit("scalar", lambda e, tt=t: e.activation(
                    out=sl(tt, 8), in_=sl(tt, 7), func=AF.Sqrt, scale=1.0 / D,
                    bias=epsc), inc="se")
                lnse_t[t] = S.t("se")
            both(sqrt2)

            def recip2(t):
                S.emit("vector", lambda e, tt=t: e.reciprocal(
                    sl(tt, 9), sl(tt, 8)), waits=[("se", lnse_t[t])],
                    inc="dve")
            both(recip2)
            last_dump[0] = S.t("se")

        str_t = {}

        def emit_r1(t):
            S.emit("vector", lambda e, x2=xb[t % 2], lx=lnx_of(t):
                   e.tensor_add(x2, lx, x2), waits=[("se", sq1_t[t])],
                   inc="dve")
            r1_t[t] = S.t("dve")
            sm = "str1" if t % 2 == 0 else "str1o"
            S.dma("sync", r1_h[t * 128:(t + 1) * 128, :], xb[t % 2],
                  waits=[("dve", r1_t[t])], inc=sm)
            str_t[t] = (sm, S.t(sm))
            emit_prefetch(t)

        def emit_core(t):
            wl = [("se", last_dump[0])]
            if t >= 1:
                wl.append(("pe", transp_t[t - 1]))
            S.emit("vector", lambda e, tt=t, lx=lnx_of(t): e.tensor_scalar(
                out=normed_bf[:], in0=lx, scalar1=sl(tt, 6),
                scalar2=sl(tt, 9), op0=ALU.add, op1=ALU.mult),
                waits=wl, inc="dve")
            core_t[t] = S.t("dve")

        def emit_transp(t):
            wl = [("dve", core_t[t])]
            if t >= 1:
                wl.append(("dve", nfmcopy_t[t - 1]))
            for k in range(8):
                S.emit("tensor", lambda e, kk=k: e.transpose(
                    pT[:, kk * 128:(kk + 1) * 128],
                    normed_bf[:, kk * 128:(kk + 1) * 128], identb[:]),
                    waits=(wl if k == 0 else []), inc="pe")
            transp_t[t] = S.t("pe")
            wl = [("pe", transp_t[t])]
            if t >= 2:
                wl.append(("stn%d" % (t % 2), 16 * ((t - 2) // 2 + 1)))
            S.emit("vector", lambda e, s2=t % 2: e.tensor_copy(
                nfm_wr[s2][:], pT), waits=wl, inc="dve")
            nfmcopy_t[t] = S.t("dve")
            S.dma("sync", nfm_h3[:, :, t * 128:(t + 1) * 128],
                  nfm_wr[t % 2].rearrange("p (c j) -> p c j", j=128),
                  waits=[("dve", nfmcopy_t[t])], inc="stn%d" % (t % 2))
            if dbg:
                S.dma("gpsimd",
                      nfmf_d.rearrange("(c p) m -> p c m", p=128)[
                          :, :, t * 128:(t + 1) * 128],
                      nfm_wr[t % 2].rearrange("p (c j) -> p c j", j=128),
                      inc="stdbg")

        def emit_mms(t):
            for n in range(4):
                wl = [("dve", nfmcopy_t[t]), ("dve", t_brow)]
                if t >= 1:
                    wl.append(("se", silu_t[(t - 1, n)]))
                for c in range(8):
                    S.emit("tensor", lambda e, cc=c, nn=n, s2=t % 2: e.matmul(
                        bank(nn), nfm_wr[s2][:, cc * 128:(cc + 1) * 128],
                        wqk_sb[cc][:, nn * 512:(nn + 1) * 512],
                        start=(cc == 0), stop=False),
                        waits=(wl if c == 0 else []), inc="pe")
                S.emit("tensor", lambda e, nn=n: e.matmul(
                    bank(nn), onesb[:], brow_qk[0:1, nn * 512:(nn + 1) * 512],
                    start=False, stop=True), inc="pe")
                mmn_t[(t, n)] = S.t("pe")

        def emit_silu_affine(t, pr):
            for n in (2 * pr, 2 * pr + 1):
                p = n // 2
                wl = [("pe", mmn_t[(t, n)])]
                prev = (t, p - 1) if p >= 1 else ((t - 1, 1) if t >= 1
                                                  else None)
                if prev in kaff_t:
                    wl.append(("dve", kaff_t[prev]))
                S.emit("scalar", lambda e, nn=n: e.activation(
                    out=qksilu[nn % 2][:], in_=bank(nn), func=AF.Silu),
                    waits=wl, inc="se")
                silu_t[(t, n)] = S.t("se")
                if n % 2 == 0:
                    continue
                qkpair = qsb[:, 1024:2048]
                wl = [("se", silu_t[(t, n)])]
                npair = t * 2 + p
                if npair >= 1:
                    wl.append(qdma_t[npair - 1])
                S.emit("vector", lambda e, pp=p: e.tensor_mul(
                    qtm[:], qkpair,
                    reps["g0"][:, pp * 1024:(pp + 1) * 1024]), waits=wl,
                    inc="dve")
                S.emit("vector", lambda e, pp=p: e.tensor_add(
                    qtm[:], qtm[:],
                    reps["b0"][:, pp * 1024:(pp + 1) * 1024]), inc="dve")
                t_q = S.t("dve")
                S.dma("sync",
                      q_h[t * 128:(t + 1) * 128, p * 1024:(p + 1) * 1024],
                      qtm[:], waits=[("dve", t_q)],
                      inc="stq%d" % (p % 2))
                qdma_t[npair] = ("stq%d" % (p % 2), S.t("stq%d" % (p % 2)))
                if dbg:
                    S.dma("gpsimd",
                          qf_d[t * 128:(t + 1) * 128,
                               p * 1024:(p + 1) * 1024],
                          qtm[:], inc="stdbg")
                wl = []
                if t in (27, 28, 30, 31) and p == 0:
                    wl += [("pe", t_setup_pe), ("se", t_setup_se)]
                S.emit("vector", lambda e, pp=p, tt=t: e.tensor_mul(
                    k_tile[tt][:, pp * 1024:(pp + 1) * 1024], qkpair,
                    reps["g1"][:, pp * 1024:(pp + 1) * 1024]), waits=wl,
                    inc="dve")
                S.emit("vector", lambda e, pp=p, tt=t: e.tensor_add(
                    k_tile[tt][:, pp * 1024:(pp + 1) * 1024],
                    k_tile[tt][:, pp * 1024:(pp + 1) * 1024],
                    reps["b1"][:, pp * 1024:(pp + 1) * 1024]), inc="dve")
                kaff_t[(t, p)] = S.t("dve")

        def emit_prefetch(t):
            if t + 2 < NT:
                sm = "ldx%d" % (t % 2)
                S.dma("sync", xb[t % 2], x_d[(t + 2) * 128:(t + 3) * 128, :],
                      waits=[str_t[t]], inc=sm)
                xld_t[t + 2] = (sm, S.t(sm))

        emit_chain_front(0, 1)
        emit_chain_rest(0, 1)
        emit_core(0)
        emit_transp(0)
        emit_r1(0)
        emit_r1(1)
        for i in range(NT // 2):
            a, b = 2 * i, 2 * i + 1
            emit_mms(a)
            emit_core(b)
            emit_silu_affine(a, 0)
            emit_transp(b)
            emit_silu_affine(a, 1)
            emit_mms(b)
            if a + 2 < NT:
                emit_chain_front(a + 2, b + 2)
                emit_chain_rest(a + 2, b + 2, guard_pe=transp_t[b])
                emit_core(a + 2)
                emit_r1(a + 2)
                emit_transp(a + 2)
            emit_silu_affine(b, 0)
            emit_silu_affine(b, 1)
            if b + 2 < NT:
                emit_r1(b + 2)

        t_klast = ("dve", S.t("dve"))
        t_qall = [("stq0", S.t("stq0")), ("stq1", S.t("stq1"))]
        if dbg:
            for t in range(NT):
                S.dma("gpsimd", kf_d[t * 128:(t + 1) * 128, :], k_tile[t],
                      waits=[t_klast], inc="stdbg")

        pemm2_t = {}
        exp_t = {}
        attnts_t = {}

        wsub_jobs = ([("wout", j, h) for j in range(16) for h in range(2)] +
                     [("w1s", ft, h) for ft in range(32) for h in range(2)] +
                     [("w2s", cd, h) for cd in range(8) for h in range(8)])
        t_stgB_free = [S.t("dve"), S.t("dve")]
        wsi = [0]
        nacc = {"w1s": 0, "w2s": 0}
        wsl = [0]

        def emit_wsubs(kmax):
            while wsi[0] < min(kmax, len(wsub_jobs)):
                kind, a, h = wsub_jobs[wsi[0]]
                slot = wsi[0] % 2
                if kind == "wout":
                    src = wout_d[a * 128:(a + 1) * 128, h * 512:(h + 1) * 512]
                elif kind == "w1s":
                    src = w1_d[a * 128:(a + 1) * 128, h * 512:(h + 1) * 512]
                else:
                    src = w2_d[a * 128:(a + 1) * 128, h * 512:(h + 1) * 512]
                S.dma("gpsimd", stgB[slot], src,
                      waits=[("dve", t_stgB_free[slot])],
                      inc="ldw%d" % slot)
                tl = ("ldw%d" % slot, S.t("ldw%d" % slot))
                if kind == "wout":
                    S.emit("vector", lambda e, aa=a, hh=h, s2=slot:
                           e.tensor_copy(
                               wout_sb[aa][:, hh * 512:(hh + 1) * 512],
                               stgB[s2]), waits=[tl], inc="dve")
                else:
                    acc = absacc[:, 0:1] if kind == "w1s" else absacc[:, 1:2]
                    if nacc[kind] == 0:
                        S.emit("vector", lambda e, s2=slot, ac=acc:
                               e.reduce_sum(ac, stgB[s2], axis=AX,
                                            apply_absolute_value=True),
                               waits=[tl], inc="dve")
                    else:
                        S.emit("vector", lambda e, s2=slot: e.reduce_sum(
                            tmpred[:], stgB[s2], axis=AX,
                            apply_absolute_value=True),
                            waits=[tl], inc="dve")
                        t_redw = S.t("dve")
                        S.emit("vector", lambda e, ac=acc: e.tensor_add(
                            ac, ac, tmpred[:]), waits=[("dve", t_redw)],
                            inc="dve")
                    nacc[kind] += 1
                t_stgB_free[slot] = S.t("dve")
                wsi[0] += 1

        for c in range(16):
            half = c % 2
            slot = c % 2
            if c == 0:
                S.dma("sync", qslice[0].rearrange("p (t j) -> p t j", j=128),
                      q_h3[:, :, 0:128],
                      waits=t_qall + [t_klast], inc="ldq0")
                qld_t = {0: ("ldq0", S.t("ldq0"))}
                S.dma("sync", qslice[1].rearrange("p (t j) -> p t j", j=128),
                      q_h3[:, :, 128:256], inc="ldq1")
                qld_t[1] = ("ldq1", S.t("ldq1"))
            elif c + 1 < 16:
                sm = "ldq%d" % ((c + 1) % 2)
                S.dma("sync",
                      qslice[(c + 1) % 2].rearrange("p (t j) -> p t j", j=128),
                      q_h3[:, :, (c + 1) * 128:(c + 2) * 128],
                      waits=[("pe", pemm2_t[c - 1])], inc=sm)
                qld_t[c + 1] = (sm, S.t(sm))
            wl = [qld_t[c], t_klast]
            if c >= 2:
                wl += [("se", exp_t[c - 2]), ("dve", attnts_t[c - 2])]
            for n in range(4):
                for t2 in range(NT):
                    S.emit("tensor", lambda e, nn=n, tx=t2, sx=slot, hx=half:
                           e.matmul(
                               bank(hx * 4 + nn),
                               qslice[sx][:, tx * 128:(tx + 1) * 128],
                               k_tile[tx][:, nn * 512:(nn + 1) * 512],
                               start=(tx == 0), stop=(tx == NT - 1)),
                           waits=(wl if (n == 0 and t2 == 0) else []),
                           inc="pe")
            pemm2_t[c] = S.t("pe")
            phalf = psum[:, half * 2048:half * 2048 + 2048]
            wl = [("pe", pemm2_t[c])]
            if c >= 2:
                wl.append(("se", exp_t[c - 2]))
            S.emit("vector", lambda e, ph=phalf, cc=c: e.reduce_max(
                sl(cc, 10), ph, axis=AX, negate=True), waits=wl, inc="dve")
            t_rmax = S.t("dve")
            S.emit("vector", lambda e, cc=c: e.tensor_scalar(
                out=sl(cc, 11), in0=sl(cc, 10), scalar1=1.0 / NTOK,
                scalar2=None, op0=ALU.mult), waits=[("dve", t_rmax)],
                inc="dve")
            t_be = S.t("dve")
            wl = [("dve", t_be)]
            if c >= 1:
                wl.append(("dve", attnts_t[c - 1]))
            else:
                wl += [("str1", 16 * (NT // 2)), ("str1o", 16 * (NT // 2))]
            S.emit("scalar", lambda e, ph=phalf, cc=c: e.activation(
                out=attn_un[:], in_=ph, func=AF.Exp, scale=1.0 / NTOK,
                bias=sl(cc, 11), accum_out=sl(cc, 12)), waits=wl, inc="se")
            exp_t[c] = S.t("se")
            S.emit("vector", lambda e, cc=c: e.reciprocal(
                sl(cc, 13), sl(cc, 12)), waits=[("se", exp_t[c])], inc="dve")
            t_rz = S.t("dve")
            wl2 = [("dve", t_rz)]
            if c == 0:
                wl2 += [("stn0", 16 * (NT // 2)), ("stn1", 16 * (NT // 2))]
            S.emit("vector", lambda e, cc=c: e.tensor_scalar(
                out=attn_dump[:], in0=attn_un[:], scalar1=sl(cc, 13),
                scalar2=0.0, op0=ALU.mult, op1=ALU.add,
                accum_out=s_sb[:, cc:cc + 1]), waits=wl2,
                inc="dve")
            attnts_t[c] = S.t("dve")
            emit_wsubs((c + 1) * 10)
        emit_wsubs(len(wsub_jobs))
        t_absdone = S.t("dve")
        t_attnlast = S.t("dve")
        if dbg:
            S.dma("gpsimd", sdbg_d, s_sb, waits=[("dve", t_attnlast)],
                  inc="stdbg")

        wl = [("dve", t_absdone), ("se", exp_t[15]), ("dve", t_attnlast)]
        S.emit("tensor", lambda e: e.matmul(
            psum[0:1, 3584:3585], absacc[:, 0:1], onescol, start=True,
            stop=True),
            waits=wl, inc="pe")
        S.emit("tensor", lambda e: e.matmul(
            psum[0:1, 3585:3586], absacc[:, 1:2], onescol, start=True,
            stop=True),
            inc="pe")
        S.emit("scalar", lambda e: e.activation(
            out=mrow[0:1, 0:1], in_=psum[0:1, 3584:3585], func=AF.Identity,
            scale=1.0 / (FF * D)), waits=[("pe", S.t("pe"))], inc="se")
        S.emit("scalar", lambda e: e.activation(
            out=mrow[0:1, 1:2], in_=psum[0:1, 3585:3586], func=AF.Identity,
            scale=1.0 / (FF * D)), inc="se")
        S.emit("vector", lambda e: e.tensor_scalar_max(
            mrow[0:1, 0:2], mrow[0:1, 0:2], 1e-8),
            waits=[("se", S.t("se"))], inc="dve")
        S.emit("vector", lambda e: e.reciprocal(mrow[0:1, 0:2], mrow[0:1, 0:2]),
               inc="dve")
        t_minv = S.t("dve")
        S.dma("gpsimd", bcast_h[0:1, 0:2], mrow[0:1, 0:2],
              waits=[("dve", t_minv)], inc="ldr")
        for k in [2, 4, 8, 16, 32, 64, 128]:
            S.dma("gpsimd", bcast_h[0:1, k:2 * k], bcast_h[0:1, 0:k],
                  waits=[("ldr", S.t("ldr"))], inc="ldr")
        S.dma("gpsimd", ppf[:, 120:122],
              bcast_h.rearrange("a (p c) -> (a p) c", p=128),
              waits=[("ldr", S.t("ldr"))], inc="ldr")
        t_invdone = ("ldr", S.t("ldr"))

        for h in range(2):
            S.dma("gpsimd", stgB[0][0:1, 0:512],
                  bout_d[0:1, h * 512:(h + 1) * 512],
                  waits=[("dve", t_stgB_free[0])], inc="ldw0")
            S.emit("vector", lambda e, hh=h: e.tensor_copy(
                boutb[0:1, hh * 512:(hh + 1) * 512], stgB[0][0:1, 0:512]),
                waits=[("ldw0", S.t("ldw0")), t_klast], inc="dve")
            t_stgB_free[0] = S.t("dve")
        t_boutb = S.t("dve")

        whid_done = 0
        t_ubf3a_free = [S.t("pe"), S.t("pe")]
        t_fold_n = {}
        first_whid = [True]
        ubf3a2 = [qsb[:, 0:512], qsb[:, 512:1024]]
        for n in [0, 4, 1, 5, 2, 6, 3, 7]:
            for c in range(8):
                sub_i = len(t_fold_n) * 8 + c
                slot = sub_i % 2
                S.dma("gpsimd", stgB[slot],
                      whid_d[c * 128:(c + 1) * 128, n * 512:(n + 1) * 512],
                      waits=[("dve", t_stgB_free[slot])],
                      inc="ldw%d" % slot)
                tl = ("ldw%d" % slot, S.t("ldw%d" % slot))
                wl = [tl, ("pe", t_ubf3a_free[slot]), t_invdone]
                if first_whid[0]:
                    first_whid[0] = False
                    wl.append(t_klast)
                    wl.append(("pe", pemm2_t[15]))
                S.emit("vector", lambda e, s2=slot: e.tensor_copy(
                    ubf3a2[s2][:], stgB[s2]), waits=wl, inc="dve")
                t_cast = S.t("dve")
                for jj in range(4):
                    jh = n * 4 + jj
                    S.emit("tensor", lambda e, jx=jj, jhx=jh, cc=c, s2=slot:
                           e.matmul(bank(7, 4 + cc * 32 + jhx, 1),
                                    ubf3a2[s2][:, jx * 128:(jx + 1) * 128],
                                    lnb_pp[:, cc:cc + 1], start=True,
                                    stop=True),
                           waits=([("dve", t_cast)] if jj == 0 else []),
                           inc="pe")
                t_ubf3a_free[slot] = S.t("pe")
                S.emit("vector", lambda e, cc=c, nn=n, s2=slot: e.tensor_scalar(
                    out=whid_ck[cc][:, nn * 512:(nn + 1) * 512], in0=stgB[s2],
                    scalar1=lng_pp[:, cc:cc + 1], scalar2=0.0,
                    op0=ALU.mult, op1=ALU.add), inc="dve")
                t_stgB_free[slot] = S.t("dve")
            t_fold_n[n] = S.t("dve")
        t_beffmm = S.t("pe")
        S.emit("vector", lambda e: e.tensor_add(
            beff_h[:], bank(7, 4, 32), bhid_pp[:]),
            waits=[("pe", t_beffmm)], inc="dve")
        for cc in range(1, 8):
            S.emit("vector", lambda e, c2=cc: e.tensor_add(
                beff_h[:], beff_h[:], bank(7, 4 + c2 * 32, 32)), inc="dve")
        t_beff = S.t("dve")

        hmm_t = {}
        stt_t = {}
        silu3a_t = {}
        evac3a_t = {}
        evac3a_nt = {}
        ldnfm_t = {}
        ldr1_t = {}
        w1st = {"i": 0, "stg": [t_stgB_free[0], t_stgB_free[1]],
                "qfree": [S.t("pe"), S.t("pe")], "pt6": S.t("dve"),
                "copy": 0}

        def emit_w1q(upto):
            while w1st["i"] < min(upto, 64):
                si = w1st["i"]
                ft, hh = si // 2, si % 2
                slot = si % 2
                S.dma("gpsimd", stgB[slot],
                      w1_d[ft * 128:(ft + 1) * 128, hh * 512:(hh + 1) * 512],
                      waits=[("dve", w1st["stg"][slot])],
                      inc="ldw%d" % slot)
                tl = ("ldw%d" % slot, S.t("ldw%d" % slot))
                wl = [tl, ("dve", t_beff), t_invdone,
                      ("pe", w1st["qfree"][slot])]
                S.emit("vector", lambda e, s2=slot: e.tensor_scalar(
                    out=stgB[s2], in0=stgB[s2], scalar1=inv1_pp[:],
                    scalar2=RND, op0=ALU.mult, op1=ALU.add), waits=wl,
                    inc="dve")
                S.emit("vector", lambda e, s2=slot: e.tensor_scalar(
                    out=stgB[s2], in0=stgB[s2], scalar1=RND, scalar2=1.0,
                    op0=ALU.subtract, op1=ALU.min), inc="dve")
                S.emit("vector", lambda e, s2=slot: e.tensor_scalar(
                    out=w1q_stg[s2][:], in0=stgB[s2], scalar1=-1.0,
                    scalar2=None, op0=ALU.max), inc="dve")
                w1st["stg"][slot] = S.t("dve")
                t_q3 = S.t("dve")
                for i in range(4):
                    S.emit("tensor", lambda e, ii=i, s2=slot: e.transpose(
                        pT6[:, ii * 128:(ii + 1) * 128],
                        w1q_stg[s2][:, ii * 128:(ii + 1) * 128], identb[:]),
                        waits=([("dve", t_q3), ("dve", w1st["pt6"])]
                               if i == 0 else []), inc="pe")
                w1st["qfree"][slot] = S.t("pe")
                t_tr = S.t("pe")
                S.emit("vector", lambda e, ftx=ft, hx=hh: e.tensor_copy(
                    KT_w1t_3d[:, hx * 4:hx * 4 + 4,
                              ftx * 128:(ftx + 1) * 128],
                    pT6[:, 0:512].rearrange("p (c m) -> p c m", m=128)),
                    waits=[("pe", t_tr)], inc="dve")
                w1st["pt6"] = S.t("dve")
                w1st["copy"] = S.t("dve")
                w1st["i"] += 1

        for g in range(8):
            wl = []
            if g == 0:
                wl += [("stn0", 16 * (NT // 2)), ("stn1", 16 * (NT // 2)),
                       ("dve", t_attnlast)]
            else:
                wl.append(("pe", hmm_t[g - 1]))
            S.dma("sync", nfm_grp3d, nfm_h3[:, :, g * 512:(g + 1) * 512],
                  waits=wl, inc="ldnf")
            ldnfm_t[g] = ("ldnf", S.t("ldnf"))
            for jj in range(16):
                for hv in range(2):
                    jh = jj if hv == 0 else 16 + jj
                    bk = (2 * jj + hv) % 4
                    wl = [ldnfm_t[g], ("dve", t_fold_n[jh // 4])]
                    prev = (g, jj - 2) if jj >= 2 else (
                        (g - 1, 14 + jj) if g >= 1 else None)
                    if prev in silu3a_t:
                        wl.append(("se", silu3a_t[prev]))
                    for c in range(8):
                        S.emit("tensor", lambda e, cc=c, jhx=jh, bkx=bk:
                               e.matmul(
                                   bank(bkx),
                                   whid_ck[cc][:, jhx * 128:(jhx + 1) * 128],
                                   nfm_grp[:, cc * 512:(cc + 1) * 512],
                                   start=(cc == 0), stop=(cc == 7)),
                               waits=(wl if c == 0 else []), inc="pe")
                t_pair = S.t("pe")
                wl = [("pe", t_pair), ("dve", t_beff)]
                prevs = (g, jj - 1) if jj >= 1 else (
                    (g - 1, 15) if g >= 1 else None)
                if prevs in stt_t:
                    wl.append(("dve", stt_t[prevs]))
                if g == 0 and jj == 0:
                    wl.append(("dve", t_attnlast))
                S.emit("scalar", lambda e, jx=jj: e.activation(
                    out=vg[:, 0:512], in_=bank((2 * jx) % 4), func=AF.Silu,
                    bias=beff_h[:, jx:jx + 1]), waits=wl, inc="se")
                S.emit("scalar", lambda e, jx=jj: e.activation(
                    out=vg[:, 512:1024], in_=bank((2 * jx + 1) % 4),
                    func=AF.Silu, bias=beff_h[:, 16 + jx:17 + jx]), inc="se")
                silu3a_t[(g, jj)] = S.t("se")
                wl = [("se", silu3a_t[(g, jj)])]
                if g == 0 and jj < 4:
                    wl.append(("pe", pemm2_t[15]))
                S.emit("vector", lambda e, jx=jj: e.scalar_tensor_tensor(
                    out=A_sl[jx], in0=vg[:, 0:512],
                    scalar=s_sb[:, jx:jx + 1], in1=vg[:, 512:1024],
                    op0=ALU.mult, op1=ALU.mult), waits=wl, inc="dve")
                stt_t[(g, jj)] = S.t("dve")
            hmm_t[g] = S.t("pe")
            for s4 in range(4):
                tt = g * 4 + s4
                wl = [("str1", 16 * (NT // 2)),
                      ("str1o", 16 * (NT // 2))]
                if tt >= 1:
                    wl.append(("dve", evac3a_t[tt - 1]))
                else:
                    wl.append(("dve", t_attnlast))
                S.dma("sync", r1rd, r1_h[tt * 128:(tt + 1) * 128, :],
                      waits=wl, inc="ldr1")
                ldr1_t[tt] = ("ldr1", S.t("ldr1"))
                for n in range(2):
                    wl = [("dve", stt_t[(g, 15)]), ("dve", t_boutb)]
                    if (tt - 1, n) in evac3a_nt:
                        wl.append(("dve", evac3a_nt[(tt - 1, n)]))
                    for j in range(16):
                        S.emit("tensor", lambda e, jx=j, nn=n, sx=s4: e.matmul(
                            bank(4 + nn),
                            A_sl[jx][:, sx * 128:(sx + 1) * 128],
                            wout_sb[jx][:, nn * 512:(nn + 1) * 512],
                            start=(jx == 0), stop=False),
                            waits=(wl if j == 0 else []), inc="pe")
                    S.emit("tensor", lambda e, nn=n: e.matmul(
                        bank(4 + nn), onesb[:],
                        boutb[0:1, nn * 512:(nn + 1) * 512],
                        start=False, stop=True), inc="pe")
                    t_omm = S.t("pe")
                    wl2 = [("pe", t_omm), ldr1_t[tt]]
                    if tt >= 1:
                        wl2.append(("stx1", 16 * tt))
                    else:
                        wl2.append(("str1", 16 * (NT // 2)))
                        wl2.append(("str1o", 16 * (NT // 2)))
                    S.emit("vector", lambda e, nn=n: e.tensor_add(
                        evacC[:, nn * 512:(nn + 1) * 512], bank(4 + nn),
                        r1rd[:, nn * 512:(nn + 1) * 512]), waits=wl2,
                        inc="dve")
                    evac3a_nt[(tt, n)] = S.t("dve")
                evac3a_t[tt] = S.t("dve")
                S.dma("gpsimd", x1_h[tt * 128:(tt + 1) * 128, :], evacC,
                      waits=[("dve", evac3a_t[tt])], inc="stx1")
            emit_w1q((g + 1) * 8)
        emit_w1q(64)
        t_w1done = w1st["copy"]
        t_hlast = hmm_t[7]
        t_out3a_last = S.t("pe")
        t_x1all = S.t("stx1")

        for h in range(2):
            S.dma("gpsimd", stgB[0][0:1, 0:512],
                  fb2_d[0:1, h * 512:(h + 1) * 512],
                  waits=[("dve", w1st["stg"][0])], inc="ldw0")
            S.emit("vector", lambda e, hh=h: e.tensor_copy(
                fb2b[0:1, hh * 512:(hh + 1) * 512], stgB[0][0:1, 0:512]),
                waits=[("ldw0", S.t("ldw0")), ("pe", w1st["qfree"][0]),
                       ("pe", w1st["qfree"][1])], inc="dve")
            w1st["stg"][0] = S.t("dve")
        t_fb2 = S.t("dve")

        w2_copy_t = 0
        t_pt7_free = t_beff
        t_w2stg = [w1st["stg"][0], w1st["stg"][1]]
        t_w2q_free = [t_fb2, t_fb2]
        for cd in range(8):
            for ss in range(8):
                si = cd * 8 + ss
                slot = si % 2
                S.dma("gpsimd", stgB[slot],
                      w2_d[cd * 128:(cd + 1) * 128, ss * 512:(ss + 1) * 512],
                      waits=[("dve", t_w2stg[slot])], inc="ldw%d" % slot)
                tl = ("ldw%d" % slot, S.t("ldw%d" % slot))
                S.emit("vector", lambda e, s2=slot: e.tensor_scalar(
                    out=stgB[s2], in0=stgB[s2], scalar1=inv2_pp[:],
                    scalar2=RND, op0=ALU.mult, op1=ALU.add),
                    waits=[tl, ("dve", t_w2q_free[slot]), t_invdone],
                    inc="dve")
                S.emit("vector", lambda e, s2=slot: e.tensor_scalar(
                    out=stgB[s2], in0=stgB[s2], scalar1=RND, scalar2=1.0,
                    op0=ALU.subtract, op1=ALU.min), inc="dve")
                S.emit("vector", lambda e, s2=slot: e.tensor_scalar(
                    out=w2q_stg[s2][:], in0=stgB[s2], scalar1=-1.0,
                    scalar2=None, op0=ALU.max), inc="dve")
                t_w2stg[slot] = S.t("dve")
                t_q3 = S.t("dve")
                for i in range(4):
                    wl = []
                    if i == 0:
                        wl = [("dve", t_q3), ("dve", t_pt7_free)]
                    S.emit("tensor", lambda e, ii=i, s2=slot: e.transpose(
                        pT7[:, ii * 128:(ii + 1) * 128],
                        w2q_stg[s2][:, ii * 128:(ii + 1) * 128], identb[:]),
                        waits=wl, inc="pe")
                t_tr = S.t("pe")
                wl = [("pe", t_tr)]
                if si == 0:
                    wl.append(("pe", t_hlast))
                S.emit("vector", lambda e, cdx=cd, ssx=ss: e.tensor_copy(
                    KT_w2t_3d[:, ssx * 4:ssx * 4 + 4,
                              cdx * 128:(cdx + 1) * 128],
                    pT7[:, 0:512].rearrange("p (c m) -> p c m", m=128)),
                    waits=wl, inc="dve")
                t_pt7_free = S.t("dve")
                t_w2q_free[slot] = S.t("dve")
                w2_copy_t = S.t("dve")
        t_w2done = w2_copy_t

        ffn1_t = {}
        ffn2_t = {}
        gelu_t = {}
        ycopy_t = {}
        evac3b_t = {}
        evac3b_nt = {}
        ldx1_t = {}
        lnse3_t = {}
        sq3_t = {}
        transp3_t = {}

        y_t = {}
        for g in range(8):
            for s4 in range(4):
                tt = g * 4 + s4
                wl = [("stx1", 16 * NT)]
                if s4 >= 2:
                    wl.append(("dve", y_t[tt - 2]))
                elif g >= 1:
                    wl.append(("dve", evac3b_nt[(tt - 2, 1)]))
                else:
                    wl += [("dve", evac3a_t[NT - 1]),
                           ("dve", stt_t[(7, 15)]),
                           ("dve", t_w2stg[0]), ("dve", t_w2stg[1])]
                sm = "ldy%d" % (tt % 2)
                S.dma("sync", x1rd[tt % 2], x1_h[tt * 128:(tt + 1) * 128, :],
                      waits=wl, inc=sm)
                ldx1_t[tt] = (sm, S.t(sm))
                wl = [ldx1_t[tt]]
                if tt >= 2:
                    wl.append(("se", lnse3_t[tt - 2]))
                S.emit("vector", lambda e, x2=x1rd[tt % 2], tx=tt:
                       e.reduce_sum(sl(tx, 0), x2, axis=AX), waits=wl,
                       inc="dve")
                t_red3 = S.t("dve")
                S.emit("vector", lambda e, tx=tt: e.tensor_scalar(
                    out=sl(tx, 1), in0=sl(tx, 0), scalar1=-1.0 / D,
                    scalar2=None, op0=ALU.mult), waits=[("dve", t_red3)],
                    inc="dve")
                t_negmu = S.t("dve")
                wl = [("dve", t_negmu)]
                if tt == 0:
                    wl.append(("pe", t_out3a_last))
                S.emit("scalar", lambda e, x2=x1rd[tt % 2], tx=tt:
                       e.activation(out=sqdump3b[:], in_=x2, func=AF.Square,
                                    bias=sl(tx, 1), accum_out=sl(tx, 2)),
                       waits=wl, inc="se")
                sq3_t[tt] = S.t("se")
                S.emit("scalar", lambda e, tx=tt: e.activation(
                    out=sl(tx, 3), in_=sl(tx, 2), func=AF.Sqrt, scale=1.0 / D,
                    bias=epsc), inc="se")
                lnse3_t[tt] = S.t("se")
                S.emit("vector", lambda e, tx=tt: e.reciprocal(
                    sl(tx, 4), sl(tx, 3)), waits=[("se", lnse3_t[tt])],
                    inc="dve")
                t_rstd3 = S.t("dve")
                wl = []
                if tt == 0:
                    wl.append(("dve", t_w2done))
                else:
                    wl.append(("pe", transp3_t[tt - 1]))
                wl.append(("dve", t_rstd3))
                S.emit("vector", lambda e, x2=x1rd[tt % 2], tx=tt:
                       e.tensor_scalar(
                           out=y_bf[:], in0=x2, scalar1=sl(tx, 1),
                           scalar2=sl(tx, 4), op0=ALU.add, op1=ALU.mult),
                       waits=wl, inc="dve")
                t_y = S.t("dve")
                y_t[tt] = t_y
                wl = [("dve", t_y)]
                if tt >= 1:
                    wl.append(("dve", ycopy_t[tt - 1]))
                else:
                    wl.append(("dve", t_w1done))
                for k in range(8):
                    S.emit("tensor", lambda e, kk=k: e.transpose(
                        pT6[:, kk * 128:(kk + 1) * 128],
                        y_bf[:, kk * 128:(kk + 1) * 128], identb[:]),
                        waits=(wl if k == 0 else []), inc="pe")
                transp3_t[tt] = S.t("pe")
                wl = [("pe", transp3_t[tt])]
                if s4 == 0:
                    if g >= 1:
                        wl.append(("pe", ffn1_t[g - 1]))
                    else:
                        wl.append(("pe", t_hlast))
                S.emit("vector", lambda e, sx=s4: e.tensor_copy(
                    nfm_grp3d[:, :, sx * 128:(sx + 1) * 128],
                    pT6.rearrange("p (c m) -> p c m", m=128)),
                    waits=wl, inc="dve")
                ycopy_t[tt] = S.t("dve")
            t_ygrp = S.t("dve")
            for j in range(32):
                bk = j % 4
                wl = [("dve", t_ygrp), ("dve", t_w1done)]
                prev = (g, j - 4) if j >= 4 else (
                    (g - 1, 28 + j) if g >= 1 else None)
                if prev in gelu_t:
                    wl.append(("se", gelu_t[prev]))
                for c in range(8):
                    S.emit("tensor", lambda e, cc=c, jx=j, bkx=bk: e.matmul(
                        bank(bkx), w1t_ck[cc][:, jx * 128:(jx + 1) * 128],
                        nfm_grp[:, cc * 512:(cc + 1) * 512],
                        start=(cc == 0), stop=(cc == 7)),
                        waits=(wl if c == 0 else []), inc="pe")
                t_f1 = S.t("pe")
                wl = [("pe", t_f1)]
                if g >= 1:
                    wl.append(("pe", ffn2_t[g - 1]))
                elif j == 0:
                    wl.append(("pe", t_out3a_last))
                S.emit("scalar", lambda e, jx=j, bkx=bk: e.activation(
                    out=h1_sb[jx][:], in_=bank(bkx), func=AF.Gelu,
                    bias=fb1_pp[:, jx:jx + 1]), waits=wl, inc="se")
                gelu_t[(g, j)] = S.t("se")
            ffn1_t[g] = S.t("pe")
            t_gelu_all = S.t("se")
            for s4 in range(4):
                tt = g * 4 + s4
                if s4 < 2:
                    wl = [("dve", y_t[g * 4 + s4 + 2])]
                else:
                    wl = [("dve", evac3b_nt[(tt - 2, 1)])]
                sm = "ldy%d" % (tt % 2)
                S.dma("sync", x1rd[tt % 2], x1_h[tt * 128:(tt + 1) * 128, :],
                      waits=wl, inc=sm)
                t_x1b = (sm, S.t(sm))
                for n in range(2):
                    wl = [("se", t_gelu_all), ("dve", t_w2done),
                          ("dve", t_fb2), t_x1b]
                    if (tt - 1, n) in evac3b_nt:
                        wl.append(("dve", evac3b_nt[(tt - 1, n)]))
                    elif tt == 0:
                        wl.append(("dve", evac3a_nt[(NT - 1, n)]))
                    for j in range(32):
                        S.emit("tensor", lambda e, jx=j, nn=n, sx=s4: e.matmul(
                            bank(4 + nn),
                            h1_sb[jx][:, sx * 128:(sx + 1) * 128],
                            w2t_ck[jx][:, nn * 512:(nn + 1) * 512],
                            start=(jx == 0), stop=False),
                            waits=(wl if j == 0 else []), inc="pe")
                    S.emit("tensor", lambda e, nn=n: e.matmul(
                        bank(4 + nn), onesb[:],
                        fb2b[0:1, nn * 512:(nn + 1) * 512],
                        start=False, stop=True), inc="pe")
                    t_omm = S.t("pe")
                    wl2 = [("pe", t_omm), t_x1b]
                    if tt >= 1:
                        wl2.append(("stout", 16 * tt))
                    else:
                        wl2.append(("stx1", t_x1all))
                    S.emit("vector", lambda e, nn=n, sx=s4: e.tensor_add(
                        evacC[:, nn * 512:(nn + 1) * 512], bank(4 + nn),
                        x1rd[(g * 4 + sx) % 2][:, nn * 512:(nn + 1) * 512]),
                        waits=wl2, inc="dve")
                    evac3b_nt[(tt, n)] = S.t("dve")
                evac3b_t[tt] = S.t("dve")
                S.dma("gpsimd", out_d[tt * 128:(tt + 1) * 128, :], evacC,
                      waits=[("dve", evac3b_t[tt])], inc="stout")
            ffn2_t[g] = S.t("pe")

        if dbg:
            t_done = S.t("pe")
            for hh in range(8):
                S.dma("gpsimd", ktf_d[:, hh * 8192:(hh + 1) * 8192],
                      KT[:, hh * 8192:(hh + 1) * 8192],
                      waits=[("pe", t_done)], inc="stdbg")
            S.dma("gpsimd", h1f_d, wA[:], waits=[("pe", t_done)], inc="stdbg")
            S.dma("gpsimd", ppf_d, ppf[:], waits=[("pe", t_done)],
                  inc="stdbg")
        fw = [("stout", 16 * NT)]
        if dbg:
            fw.append(("stdbg", S.t("stdbg")))
        S.emit("gpsimd", lambda e: e.nop(), waits=fw)

        with nc.Block() as block:
            @block.sync
            def _(eng):
                S.replay("sync", eng)

            @block.vector
            def _(eng):
                S.replay("vector", eng)

            @block.scalar
            def _(eng):
                S.replay("scalar", eng)

            @block.tensor
            def _(eng):
                S.replay("tensor", eng)

            @block.gpsimd
            def _(eng):
                S.replay("gpsimd", eng)

    return nc


_NC_CACHE = {}


def _get_nc(dbg=False):
    key = bool(dbg)
    if key not in _NC_CACHE:
        _NC_CACHE[key] = build_nc(dbg=key)
    return _NC_CACHE[key]


def kernel(L0_attn_x, L0_attn_ln_g, L0_attn_ln_b, L0_attn_W_hidden,
           L0_attn_b_hidden, L0_attn_W_qk, L0_attn_b_qk, L0_attn_os_gamma,
           L0_attn_os_beta, L0_attn_W_out, L0_attn_b_out, L0_ffn_W1,
           L0_ffn_b1, L0_ffn_W2, L0_ffn_b2):
    x = np.ascontiguousarray(np.asarray(L0_attn_x, dtype=np.float32))
    B = x.shape[0]
    assert B == 8 and x.shape[1] == NTOK and x.shape[2] == D

    com = dict(
        wqk=np.ascontiguousarray(np.asarray(L0_attn_W_qk, np.float32)),
        whid=np.ascontiguousarray(np.asarray(L0_attn_W_hidden, np.float32)),
        wout=np.ascontiguousarray(np.asarray(L0_attn_W_out, np.float32)),
        w1=np.ascontiguousarray(np.asarray(L0_ffn_W1, np.float32)),
        w2=np.ascontiguousarray(np.asarray(L0_ffn_W2, np.float32)),
        lng=np.asarray(L0_attn_ln_g, np.float32).reshape(1, D),
        lnb=np.asarray(L0_attn_ln_b, np.float32).reshape(1, D),
        bqk=np.asarray(L0_attn_b_qk, np.float32).reshape(1, QKD),
        bhid=np.asarray(L0_attn_b_hidden, np.float32).reshape(1, HID2),
        bout=np.asarray(L0_attn_b_out, np.float32).reshape(1, D),
        g0=np.ascontiguousarray(
            np.asarray(L0_attn_os_gamma, np.float32)[0]).reshape(1, QKD),
        b0=np.ascontiguousarray(
            np.asarray(L0_attn_os_beta, np.float32)[0]).reshape(1, QKD),
        g1=np.ascontiguousarray(
            np.asarray(L0_attn_os_gamma, np.float32)[1]).reshape(1, QKD),
        b1=np.ascontiguousarray(
            np.asarray(L0_attn_os_beta, np.float32)[1]).reshape(1, QKD),
        fb1=np.asarray(L0_ffn_b1, np.float32).reshape(1, FF),
        fb2=np.asarray(L0_ffn_b2, np.float32).reshape(1, D),
        iden=np.eye(128, dtype=np.float32),
    )
    in_maps = [dict(com, x=np.ascontiguousarray(x[i])) for i in range(B)]

    nc = _get_nc(dbg=DBG)
    res = run_bass_kernel_spmd(nc, in_maps, core_ids=list(range(B)))
    out = np.stack([np.asarray(r["out"]) for r in res.results], axis=0)
    if DBG:
        kernel.last_results = res.results
    return out


# version 18
# speedup vs baseline: 1.2593x; 1.0078x over previous
import os
import numpy as np
from contextlib import ExitStack

import concourse.bass as bass
import concourse.mybir as mybir
from concourse.bass_utils import run_bass_kernel_spmd

F32 = mybir.dt.float32
BF16 = mybir.dt.bfloat16
AF = mybir.ActivationFunctionType
ALU = mybir.AluOpType
AX = mybir.AxisListType.X

NTOK, D, QKD, HID2, FF = 4096, 1024, 2048, 4096, 4096
NT = NTOK // 128
EPS = 1e-5
RND = 12582912.0

DBG = bool(int(os.environ.get("MK_DBG", "0")))


class Sched:
    ENGS = ("sync", "vector", "scalar", "tensor", "gpsimd")

    def __init__(self, nc):
        self.nc = nc
        self.q = {e: [] for e in self.ENGS}
        self.cnt = {}
        self.sem_h = {}

    def new_sem(self, stack, name):
        self.sem_h[name] = stack.enter_context(self.nc.semaphore(name))
        self.cnt[name] = 0

    def t(self, sem):
        return self.cnt[sem]

    def emit(self, eng, fn, waits=(), inc=None, amt=1):
        waits = [(s, v) for (s, v) in waits if v > 0]
        for s, v in waits:
            assert v <= self.cnt[s], f"future ticket {s}:{v}>{self.cnt[s]}"
        if eng == "vector":
            assert inc in (None, "dve")
            inc, amt = "dve", 1
        elif eng == "scalar":
            assert inc in (None, "se")
            inc, amt = "se", 1
        if inc is not None:
            self.cnt[inc] += amt
        self.q[eng].append((fn, tuple(waits), inc, amt))

    def dma(self, eng, out, in_, waits=(), inc=None):
        self.emit(eng, lambda e, o=out, i=in_: e.dma_start(out=o, in_=i),
                  waits=waits, inc=inc, amt=16)

    def replay(self, eng_name, eng):
        serial = {"vector": "dve", "scalar": "se"}.get(eng_name)
        k = 0
        kp = 0
        for fn, waits, inc, amt in self.q[eng_name]:
            if serial and k > 0:
                eng.wait_ge(self.sem_h[serial], k)
            if eng_name == "gpsimd" and inc == "pool" and kp > 0:
                eng.wait_ge(self.sem_h["pool"], kp)
            for s, v in waits:
                eng.wait_ge(self.sem_h[s], v)
            instr = fn(eng)
            if inc is not None:
                instr.then_inc(self.sem_h[inc], amt)
            if serial:
                k += 1
            if eng_name == "gpsimd" and inc == "pool":
                kp += 1


def build_nc(dbg=False):
    nc = bass.Bass("TRN2", target_bir_lowering=False, debug=False)

    x_d = nc.dram_tensor("x", [NTOK, D], F32, kind="ExternalInput").ap()
    wqk_d = nc.dram_tensor("wqk", [D, QKD], F32, kind="ExternalInput").ap()
    whid_d = nc.dram_tensor("whid", [D, HID2], F32, kind="ExternalInput").ap()
    wout_d = nc.dram_tensor("wout", [QKD, D], F32, kind="ExternalInput").ap()
    w1_d = nc.dram_tensor("w1", [FF, D], F32, kind="ExternalInput").ap()
    w2_d = nc.dram_tensor("w2", [D, FF], F32, kind="ExternalInput").ap()
    lng_d = nc.dram_tensor("lng", [1, D], F32, kind="ExternalInput").ap()
    lnb_d = nc.dram_tensor("lnb", [1, D], F32, kind="ExternalInput").ap()
    bqk_d = nc.dram_tensor("bqk", [1, QKD], F32, kind="ExternalInput").ap()
    bhid_d = nc.dram_tensor("bhid", [1, HID2], F32, kind="ExternalInput").ap()
    bout_d = nc.dram_tensor("bout", [1, D], F32, kind="ExternalInput").ap()
    g0_d = nc.dram_tensor("g0", [1, QKD], F32, kind="ExternalInput").ap()
    b0_d = nc.dram_tensor("b0", [1, QKD], F32, kind="ExternalInput").ap()
    g1_d = nc.dram_tensor("g1", [1, QKD], F32, kind="ExternalInput").ap()
    b1_d = nc.dram_tensor("b1", [1, QKD], F32, kind="ExternalInput").ap()
    fb1_d = nc.dram_tensor("fb1", [1, FF], F32, kind="ExternalInput").ap()
    fb2_d = nc.dram_tensor("fb2", [1, D], F32, kind="ExternalInput").ap()
    iden_d = nc.dram_tensor("iden", [128, 128], F32, kind="ExternalInput").ap()

    out_d = nc.dram_tensor("out", [NTOK, D], F32, kind="ExternalOutput").ap()

    skind = "ExternalOutput" if dbg else "Internal"
    q_h = nc.dram_tensor("q_h", [NTOK, QKD], BF16, kind="Internal").ap()
    nfm_h = nc.dram_tensor("nfm_h", [D, NTOK], BF16, kind="Internal").ap()
    r1_h = nc.dram_tensor("r1_h", [NTOK, D], F32, kind=skind).ap()
    x1_h = nc.dram_tensor("x1_h", [NTOK, D], F32, kind=skind).ap()
    bcast_h = nc.dram_tensor("bcast_h", [1, 256], F32, kind="Internal").ap()
    if dbg:
        sdbg_d = nc.dram_tensor("sdbg", [128, 16], F32,
                                kind="ExternalOutput").ap()
        qf_d = nc.dram_tensor("qf", [NTOK, QKD], F32,
                              kind="ExternalOutput").ap()
        ktf_d = nc.dram_tensor("ktf", [128, 65536], F32,
                               kind="ExternalOutput").ap()
        h1f_d = nc.dram_tensor("h1f", [128, 16384], F32,
                               kind="ExternalOutput").ap()
        ppf_d = nc.dram_tensor("ppfd", [128, 132], F32,
                               kind="ExternalOutput").ap()
        nfmf_d = nc.dram_tensor("nfmf", [D, NTOK], F32,
                                kind="ExternalOutput").ap()
        kf_d = nc.dram_tensor("kf", [NTOK, QKD], F32,
                              kind="ExternalOutput").ap()

    with ExitStack() as stk:
        sbuf = lambda *a, **k: stk.enter_context(nc.sbuf_tensor(*a, **k))

        KT = sbuf("KT", [128, 65536], BF16)
        wA = sbuf("wA", [128, 16384], BF16)
        scrA = sbuf("scrA", [128, 8192], BF16)
        scrB = sbuf("scrB", [128, 3072], F32)
        scrC = sbuf("scrC", [128, 1024], F32)
        qsb = sbuf("qsb", [128, 2048], BF16)
        nfm = sbuf("nfm", [128, 4096], BF16)
        qtm = sbuf("qtm", [128, 1024], BF16)
        stats = sbuf("stats", [128, 64], F32)
        ppf = sbuf("ppf", [128, 132], F32)
        ppb = sbuf("ppb", [128, 8], BF16)
        identb = sbuf("identb", [128, 128], BF16)
        onesb = sbuf("onesb", [1, 128], BF16)
        onesf = sbuf("onesf", [1, 128], F32)
        onef1 = sbuf("onef1", [1, 1], F32)

        psum = stk.enter_context(nc.psum_tensor("psum", [128, 4096], F32))

        S = Sched(nc)
        for s in ["ldx0", "ldx1", "ldw0", "ldw1", "ldq0", "ldq1", "ldnf",
                  "ldr1", "ldy0", "ldy1", "ldr", "ldr2", "stq0", "stq1",
                  "stn0", "stn1", "str1", "str1o", "stx1", "stout", "pe", "dve", "se",
                  "pool", "stdbg"]:
            S.new_sem(stk, s)

        def bank(b, off=0, w=512):
            return psum[:, b * 512 + off:b * 512 + off + w]

        psum_b = psum.bitcast(BF16)
        pT = psum_b[:, 4096:5120]
        pT6 = psum_b[:, 6144:7168]
        pT7 = psum_b[:, 7168:8192]
        browp = psum[0:1, 3584:4096]

        KTf = KT.bitcast(F32)
        rowf = KTf[0:1, 31744:32768]
        rowf2 = KTf[0:1, 30720:31744]
        wstg = [KTf[:, 28672 + i * 512:28672 + (i + 1) * 512] for i in range(2)]
        idstg = KTf[:, 28672:28800]
        ubf = KT[:, 56320:56832]

        wqk_sb = [wA[:, c * 2048:(c + 1) * 2048] for c in range(8)]
        wout_sb = [wA[:, j * 1024:(j + 1) * 1024] for j in range(16)]
        h1_sb = [wA[:, j * 512:(j + 1) * 512] for j in range(32)]

        k_tile = [KT[:, t * 2048:(t + 1) * 2048] for t in range(NT)]
        w1t_ck = [KT[:, c * 4096:(c + 1) * 4096] for c in range(8)]
        whid_ck = [KT[:, 32768 + c * 4096:32768 + (c + 1) * 4096]
                   for c in range(8)]
        w2t_ck = [KT[:, 32768 + j * 1024:32768 + (j + 1) * 1024]
                  for j in range(32)]
        KT_w1t_3d = KT.rearrange("p (c m) -> p c m", m=4096)
        KT_w2t_3d = KT[:, 32768:65536].rearrange("p (j m) -> p j m", m=1024)

        reps = {v: scrA[:, i * 2048:(i + 1) * 2048]
                for i, v in enumerate(["g0", "b0", "g1", "b1"])}
        qslice = [scrA[:, i * 4096:(i + 1) * 4096] for i in range(2)]
        A_sl = [scrA[:, j * 512:(j + 1) * 512] for j in range(16)]
        sqdump3b = scrA[:, 0:1024]

        xb = [scrB[:, 0:1024], scrB[:, 1024:2048]]
        lnx = scrB[:, 2048:3072]
        attn_un = scrB[:, 0:2048]
        stgB = [scrB[:, 2048:2560], scrB[:, 2560:3072]]
        r1rd = scrB[:, 0:1024]
        vg = scrB[:, 1024:2048]
        x1rd = [scrB[:, 0:1024], scrB[:, 1024:2048]]

        evacC = scrC[:, 0:1024]
        qtm_sl2 = scrC.bitcast(BF16)[:, 0:1024]

        qtm_sl = None
        normed_bf = qsb[:, 0:1024]
        qksilu = [qsb[:, 1024:1536], qsb[:, 1536:2048]]
        ubf3a = qsb[:, 0:512]
        w1q_stg = [qsb[:, 0:512], qsb[:, 512:1024]]
        fb2b = qsb[0:1, 0:1024]
        w2q_stg = [qsb[:, 1024:1536], qsb[:, 1536:2048]]
        y_bf = qsb[:, 1024:2048]

        nfm_wr = [nfm[:, 0:1024], nfm[:, 1024:2048]]
        brow_qk = KT[0:1, 63488:65536]
        lnxB = nfm.bitcast(F32)[:, 1024:2048]
        attn_dump = nfm[:, 0:2048]
        nfm_grp = nfm[:]
        nfm_grp3d = nfm.rearrange("p (c m) -> p c m", m=512)
        rb_row = nfm[0:1, 0:2048]

        qch = [qtm[:, 0:512], qtm[:, 512:1024]]
        boutb = qtm[0:1, 0:1024]

        lng_pp = ppf[:, 0:8]
        lnb_pp = ppb[:, 0:8]
        bhid_pp = ppf[:, 8:40]
        fb1_pp = ppf[:, 40:72]
        beff_h = ppf[:, 72:104]
        s_sb = ppf[:, 104:120]
        inv1_pp = ppf[:, 120:121]
        inv2_pp = ppf[:, 121:122]
        absacc = ppf[:, 122:124]
        tmpred = ppf[:, 126:127]
        epsc = ppf[:, 127:128]
        onescol = ppf[:, 128:129]
        mrow = ppf[0:1, 124:126]

        def sl(t, i):
            return stats[:, (t % 4) * 16 + i:(t % 4) * 16 + i + 1]

        nfm_h3 = nfm_h.rearrange("(c p) m -> p c m", p=128)
        q_h3 = q_h.rearrange("(t p) f -> p t f", p=128)

        xld_t = {}
        S.dma("sync", xb[0], x_d[0:128, :], inc="ldx0")
        xld_t[0] = ("ldx0", S.t("ldx0"))
        S.dma("sync", xb[1], x_d[128:256, :], inc="ldx1")
        xld_t[1] = ("ldx1", S.t("ldx1"))

        S.emit("vector", lambda e: e.memset(onesb[:], 1.0))
        S.emit("vector", lambda e: e.memset(onesf[:], 1.0))
        S.emit("vector", lambda e: e.memset(onef1[:], 1.0))
        S.emit("vector", lambda e: e.memset(epsc, EPS))
        S.emit("vector", lambda e: e.memset(onescol, 1.0), inc="dve")
        t_ones = S.t("dve")

        S.dma("gpsimd", idstg, iden_d, inc="ldr")
        S.dma("gpsimd", rowf, lng_d, inc="ldr")
        S.dma("gpsimd", rowf2, lnb_d, inc="ldr2")
        S.emit("vector", lambda e: e.tensor_copy(identb[:], idstg),
               waits=[("ldr", S.t("ldr"))], inc="dve")
        t_ident = S.t("dve")
        first5 = [True]

        def scat(col, row_ap, wts, last=False):
            f = first5[0]
            first5[0] = False
            S.emit("tensor", lambda e, c=col, r=row_ap, ff=f, ll=last: e.matmul(
                bank(5, c, 1), r, onef1[:], start=ff, stop=ll),
                waits=wts, inc="pe")

        for c in range(8):
            scat(c, rowf[0:1, c * 128:(c + 1) * 128],
                 [("ldr", S.t("ldr")), ("ldr2", S.t("ldr2")),
                  ("dve", t_ones)])
        for c in range(8):
            scat(8 + c, rowf2[0:1, c * 128:(c + 1) * 128], [])
        for h in range(4):
            S.dma("gpsimd", rowf, bhid_d[0:1, h * 1024:(h + 1) * 1024],
                  waits=[("pe", S.t("pe"))], inc="ldr")
            tl = S.t("ldr")
            for c in range(8):
                scat(16 + h * 8 + c, rowf[0:1, c * 128:(c + 1) * 128],
                     [("ldr", tl)])
        for h in range(4):
            S.dma("gpsimd", rowf, fb1_d[0:1, h * 1024:(h + 1) * 1024],
                  waits=[("pe", S.t("pe"))], inc="ldr")
            tl = S.t("ldr")
            for c in range(8):
                scat(48 + h * 8 + c, rowf[0:1, c * 128:(c + 1) * 128],
                     [("ldr", tl)], last=(h == 3 and c == 7))
        t_scat = S.t("pe")
        S.emit("vector", lambda e: e.tensor_copy(lng_pp[:], bank(5, 0, 8)),
               waits=[("pe", t_scat)], inc="dve")
        S.emit("vector", lambda e: e.tensor_copy(lnb_pp[:], bank(5, 8, 8)),
               inc="dve")
        S.emit("vector", lambda e: e.tensor_copy(bhid_pp[:], bank(5, 16, 32)),
               inc="dve")
        S.emit("vector", lambda e: e.tensor_copy(fb1_pp[:], bank(5, 48, 32)),
               inc="dve")
        t_ppcopy = S.t("dve")

        t_rep_prev = 0
        for i, vd in enumerate([g0_d, b0_d, g1_d, b1_d]):
            nm = ["g0", "b0", "g1", "b1"][i]
            for h in range(2):
                S.dma("gpsimd", rowf, vd[0:1, h * 1024:(h + 1) * 1024],
                      waits=[("pe", S.t("pe")), ("dve", S.t("dve"))],
                      inc="ldr")
                S.emit("vector", lambda e, hh=h: e.tensor_copy(
                    rb_row[0:1, hh * 1024:(hh + 1) * 1024], rowf),
                    waits=[("ldr", S.t("ldr"))], inc="dve")
            t_rb = S.t("dve")
            for n in range(4):
                S.emit("tensor", lambda e, nn=n: e.matmul(
                    bank(6), onesb[:], rb_row[0:1, nn * 512:(nn + 1) * 512],
                    start=True, stop=True),
                    waits=[("dve", t_rb), ("dve", t_ident),
                           ("dve", t_rep_prev)], inc="pe")
                S.emit("vector", lambda e, nn=n, nmx=nm: e.tensor_copy(
                    reps[nmx][:, nn * 512:(nn + 1) * 512], bank(6)),
                    waits=[("pe", S.t("pe"))], inc="dve")
                t_rep_prev = S.t("dve")

        t_wstg_free = [S.t("dve"), S.t("dve")]
        t_ubf_free = S.t("pe")
        for n in range(4):
            for c in range(8):
                sub = n * 8 + c
                slot = sub % 2
                S.dma("gpsimd", wstg[slot],
                      wqk_d[c * 128:(c + 1) * 128, n * 512:(n + 1) * 512],
                      waits=[("dve", t_wstg_free[slot])],
                      inc="ldw%d" % slot)
                tl = ("ldw%d" % slot, S.t("ldw%d" % slot))
                S.emit("vector", lambda e, s2=slot: e.tensor_copy(
                    ubf[:], wstg[s2]),
                    waits=[tl, ("pe", t_ubf_free)], inc="dve")
                t_cast = S.t("dve")
                S.emit("tensor", lambda e, cc=c: e.matmul(
                    browp, lnb_pp[:, cc:cc + 1], ubf[:],
                    start=(cc == 0), stop=(cc == 7)),
                    waits=[("dve", t_cast)], inc="pe")
                t_ubf_free = S.t("pe")
                S.emit("vector", lambda e, cc=c, nn=n, s2=slot: e.tensor_scalar(
                    out=wqk_sb[cc][:, nn * 512:(nn + 1) * 512], in0=wstg[s2],
                    scalar1=lng_pp[:, cc:cc + 1], scalar2=0.0,
                    op0=ALU.mult, op1=ALU.add),
                    waits=[("dve", t_ppcopy)], inc="dve")
                t_wstg_free[slot] = S.t("dve")
            t_brow_mm = S.t("pe")
            S.emit("scalar", lambda e: e.copy(rowf2[0:1, 0:512], browp),
                   waits=[("pe", t_brow_mm), ("dve", S.t("dve"))], inc="se")
            S.dma("gpsimd", rowf2[0:1, 512:1024],
                  bqk_d[0:1, n * 512:(n + 1) * 512],
                  waits=[("dve", S.t("dve"))], inc="ldr2")
            S.emit("vector", lambda e: e.tensor_add(
                rowf2[0:1, 0:512], rowf2[0:1, 0:512], rowf2[0:1, 512:1024]),
                waits=[("se", S.t("se")), ("ldr2", S.t("ldr2"))], inc="dve")
            S.emit("vector", lambda e, nn=n: e.tensor_copy(
                brow_qk[0:1, nn * 512:(nn + 1) * 512], rowf2[0:1, 0:512]),
                inc="dve")
        t_brow = S.t("dve")
        t_setup_pe = S.t("pe")
        t_setup_se = S.t("se")

        sq1_t = {}
        r1_t = {}
        transp_t = {}
        nfmcopy_t = {}
        silu_t = {}
        kaff_t = {}
        qaff_t = {}
        qdma_t = {}
        mmn_t = {}
        lnse_t = {}
        core_t = {}

        lnx_of = lambda t: lnx if t % 2 == 0 else lnxB
        sq2_t = {}

        last_dump = [0]

        negmu_hold = [0]

        def emit_chain_front(ta, tb):
            tiles = [t for t in (ta, tb) if t is not None and t < NT]

            def both(emit_one):
                for t in tiles:
                    emit_one(t)

            def red1(t):
                wl = [xld_t[t]]
                if t >= 4:
                    wl.append(("se", lnse_t[t - 4]))
                    wl.append(("dve", core_t[t - 4]))
                S.emit("vector", lambda e, x2=xb[t % 2], tt=t: e.reduce_sum(
                    sl(tt, 0), x2, axis=AX), waits=wl, inc="dve")
            both(red1)
            t_red1s = S.t("dve")

            def negmu(t):
                S.emit("vector", lambda e, tt=t: e.tensor_scalar(
                    out=sl(tt, 1), in0=sl(tt, 0), scalar1=-1.0 / D,
                    scalar2=None, op0=ALU.mult), waits=[("dve", t_red1s)],
                    inc="dve")
            both(negmu)
            negmu_hold[0] = S.t("dve")

        def emit_chain_rest(ta, tb, guard_pe=None):
            tiles = [t for t in (ta, tb) if t is not None and t < NT]
            t_negmus = negmu_hold[0]

            def both(emit_one):
                for t in tiles:
                    emit_one(t)

            def sq1(t):
                wl = [("dve", t_negmus)]
                if guard_pe is not None:
                    wl.append(("pe", guard_pe))
                S.emit("scalar", lambda e, x2=xb[t % 2], tt=t: e.activation(
                    out=normed_bf[:], in_=x2, func=AF.Square, bias=sl(tt, 1),
                    accum_out=sl(tt, 2)), waits=wl, inc="se")
                sq1_t[t] = S.t("se")
            both(sq1)

            def sqrt1(t):
                S.emit("scalar", lambda e, tt=t: e.activation(
                    out=sl(tt, 3), in_=sl(tt, 2), func=AF.Sqrt, scale=1.0 / D,
                    bias=epsc), inc="se")
            both(sqrt1)
            t_sqrt1s = S.t("se")

            def recip1(t):
                S.emit("vector", lambda e, tt=t: e.reciprocal(
                    sl(tt, 4), sl(tt, 3)), waits=[("se", t_sqrt1s)], inc="dve")
            both(recip1)
            t_recips = S.t("dve")

            def lnxm(t):
                S.emit("vector", lambda e, x2=xb[t % 2], tt=t,
                       lx=lnx_of(t): e.tensor_scalar(
                    out=lx, in0=x2, scalar1=sl(tt, 1), scalar2=sl(tt, 4),
                    op0=ALU.add, op1=ALU.mult), waits=[("dve", t_recips)],
                    inc="dve")
            both(lnxm)

            def mu2(t):
                S.emit("vector", lambda e, tt=t: e.scalar_tensor_tensor(
                    out=sl(tt, 5), in0=sl(tt, 1), scalar=float(D),
                    in1=sl(tt, 0), op0=ALU.mult, op1=ALU.add), inc="dve")
                S.emit("vector", lambda e, tt=t: e.tensor_scalar(
                    out=sl(tt, 6), in0=sl(tt, 5), scalar1=sl(tt, 4),
                    scalar2=-1.0 / D, op0=ALU.mult, op1=ALU.mult), inc="dve")
            both(mu2)
            t_mu2s = S.t("dve")

            def sq2(t):
                S.emit("scalar", lambda e, tt=t, lx=lnx_of(t): e.activation(
                    out=normed_bf[:], in_=lx, func=AF.Square, bias=sl(tt, 6),
                    accum_out=sl(tt, 7)), waits=[("dve", t_mu2s)], inc="se")
                sq2_t[t] = S.t("se")
            both(sq2)

            def sqrt2(t):
                S.emit("scalar", lambda e, tt=t: e.activation(
                    out=sl(tt, 8), in_=sl(tt, 7), func=AF.Sqrt, scale=1.0 / D,
                    bias=epsc), inc="se")
                lnse_t[t] = S.t("se")
            both(sqrt2)

            def recip2(t):
                S.emit("vector", lambda e, tt=t: e.reciprocal(
                    sl(tt, 9), sl(tt, 8)), waits=[("se", lnse_t[t])],
                    inc="dve")
            both(recip2)
            last_dump[0] = S.t("se")

        str_t = {}

        def emit_r1(t):
            S.emit("vector", lambda e, x2=xb[t % 2], lx=lnx_of(t):
                   e.tensor_add(x2, lx, x2), waits=[("se", sq1_t[t])],
                   inc="dve")
            r1_t[t] = S.t("dve")
            sm = "str1" if t % 2 == 0 else "str1o"
            S.dma("sync", r1_h[t * 128:(t + 1) * 128, :], xb[t % 2],
                  waits=[("dve", r1_t[t])], inc=sm)
            str_t[t] = (sm, S.t(sm))
            emit_prefetch(t)

        def emit_core(t):
            wl = [("se", last_dump[0])]
            if t >= 1:
                wl.append(("pe", transp_t[t - 1]))
            S.emit("vector", lambda e, tt=t, lx=lnx_of(t): e.tensor_scalar(
                out=normed_bf[:], in0=lx, scalar1=sl(tt, 6),
                scalar2=sl(tt, 9), op0=ALU.add, op1=ALU.mult),
                waits=wl, inc="dve")
            core_t[t] = S.t("dve")

        def emit_transp(t):
            wl = [("dve", core_t[t])]
            if t >= 1:
                wl.append(("dve", nfmcopy_t[t - 1]))
            for k in range(8):
                S.emit("tensor", lambda e, kk=k: e.transpose(
                    pT[:, kk * 128:(kk + 1) * 128],
                    normed_bf[:, kk * 128:(kk + 1) * 128], identb[:]),
                    waits=(wl if k == 0 else []), inc="pe")
            transp_t[t] = S.t("pe")
            wl = [("pe", transp_t[t])]
            if t >= 2:
                wl.append(("stn%d" % (t % 2), 16 * ((t - 2) // 2 + 1)))
            S.emit("vector", lambda e, s2=t % 2: e.tensor_copy(
                nfm_wr[s2][:], pT), waits=wl, inc="dve")
            nfmcopy_t[t] = S.t("dve")
            S.dma("sync", nfm_h3[:, :, t * 128:(t + 1) * 128],
                  nfm_wr[t % 2].rearrange("p (c j) -> p c j", j=128),
                  waits=[("dve", nfmcopy_t[t])], inc="stn%d" % (t % 2))
            if dbg:
                S.dma("gpsimd",
                      nfmf_d.rearrange("(c p) m -> p c m", p=128)[
                          :, :, t * 128:(t + 1) * 128],
                      nfm_wr[t % 2].rearrange("p (c j) -> p c j", j=128),
                      inc="stdbg")

        def emit_mms(t):
            for n in range(4):
                wl = [("dve", nfmcopy_t[t]), ("dve", t_brow)]
                if t >= 1:
                    wl.append(("se", silu_t[(t - 1, n)]))
                for c in range(8):
                    S.emit("tensor", lambda e, cc=c, nn=n, s2=t % 2: e.matmul(
                        bank(nn), nfm_wr[s2][:, cc * 128:(cc + 1) * 128],
                        wqk_sb[cc][:, nn * 512:(nn + 1) * 512],
                        start=(cc == 0), stop=False),
                        waits=(wl if c == 0 else []), inc="pe")
                S.emit("tensor", lambda e, nn=n: e.matmul(
                    bank(nn), onesb[:], brow_qk[0:1, nn * 512:(nn + 1) * 512],
                    start=False, stop=True), inc="pe")
                mmn_t[(t, n)] = S.t("pe")

        def emit_silu_affine(t, pr):
            for n in (2 * pr, 2 * pr + 1):
                p = n // 2
                wl = [("pe", mmn_t[(t, n)])]
                prev = (t, p - 1) if p >= 1 else ((t - 1, 1) if t >= 1
                                                  else None)
                if prev in kaff_t:
                    wl.append(("dve", kaff_t[prev]))
                S.emit("scalar", lambda e, nn=n: e.activation(
                    out=qksilu[nn % 2][:], in_=bank(nn), func=AF.Silu),
                    waits=wl, inc="se")
                silu_t[(t, n)] = S.t("se")
                if n % 2 == 0:
                    continue
                qkpair = qsb[:, 1024:2048]
                wl = [("se", silu_t[(t, n)])]
                npair = t * 2 + p
                qt = qtm[:] if npair % 2 == 0 else qtm_sl2
                if npair >= 2:
                    wl.append(qdma_t[npair - 2])
                S.emit("vector", lambda e, pp=p, qq=qt: e.tensor_mul(
                    qq, qkpair,
                    reps["g0"][:, pp * 1024:(pp + 1) * 1024]), waits=wl,
                    inc="dve")
                S.emit("vector", lambda e, pp=p, qq=qt: e.tensor_add(
                    qq, qq,
                    reps["b0"][:, pp * 1024:(pp + 1) * 1024]), inc="dve")
                t_q = S.t("dve")
                S.dma("sync",
                      q_h[t * 128:(t + 1) * 128, p * 1024:(p + 1) * 1024],
                      qt, waits=[("dve", t_q)],
                      inc="stq%d" % (p % 2))
                qdma_t[npair] = ("stq%d" % (p % 2), S.t("stq%d" % (p % 2)))
                if dbg:
                    S.dma("gpsimd",
                          qf_d[t * 128:(t + 1) * 128,
                               p * 1024:(p + 1) * 1024],
                          qt, inc="stdbg")
                wl = []
                if t in (27, 28, 30, 31) and p == 0:
                    wl += [("pe", t_setup_pe), ("se", t_setup_se)]
                S.emit("vector", lambda e, pp=p, tt=t: e.tensor_mul(
                    k_tile[tt][:, pp * 1024:(pp + 1) * 1024], qkpair,
                    reps["g1"][:, pp * 1024:(pp + 1) * 1024]), waits=wl,
                    inc="dve")
                S.emit("vector", lambda e, pp=p, tt=t: e.tensor_add(
                    k_tile[tt][:, pp * 1024:(pp + 1) * 1024],
                    k_tile[tt][:, pp * 1024:(pp + 1) * 1024],
                    reps["b1"][:, pp * 1024:(pp + 1) * 1024]), inc="dve")
                kaff_t[(t, p)] = S.t("dve")

        def emit_prefetch(t):
            if t + 2 < NT:
                sm = "ldx%d" % (t % 2)
                S.dma("sync", xb[t % 2], x_d[(t + 2) * 128:(t + 3) * 128, :],
                      waits=[str_t[t]], inc=sm)
                xld_t[t + 2] = (sm, S.t(sm))

        emit_chain_front(0, 1)
        emit_chain_rest(0, 1)
        emit_core(0)
        emit_transp(0)
        emit_r1(0)
        emit_r1(1)
        for i in range(NT // 2):
            a, b = 2 * i, 2 * i + 1
            emit_mms(a)
            emit_core(b)
            emit_silu_affine(a, 0)
            emit_transp(b)
            emit_silu_affine(a, 1)
            emit_mms(b)
            if a + 2 < NT:
                emit_chain_front(a + 2, b + 2)
                emit_chain_rest(a + 2, b + 2, guard_pe=transp_t[b])
                emit_core(a + 2)
                emit_r1(a + 2)
                emit_transp(a + 2)
            emit_silu_affine(b, 0)
            emit_silu_affine(b, 1)
            if b + 2 < NT:
                emit_r1(b + 2)

        t_klast = ("dve", S.t("dve"))
        t_qall = [("stq0", S.t("stq0")), ("stq1", S.t("stq1"))]
        if dbg:
            for t in range(NT):
                S.dma("gpsimd", kf_d[t * 128:(t + 1) * 128, :], k_tile[t],
                      waits=[t_klast], inc="stdbg")

        pemm2_t = {}
        exp_t = {}
        attnts_t = {}

        wsub_jobs = ([("wout", j, h) for j in range(16) for h in range(2)] +
                     [("w1s", ft, h) for ft in range(32) for h in range(2)] +
                     [("w2s", cd, h) for cd in range(8) for h in range(8)])
        t_stgB_free = [S.t("dve"), S.t("dve")]
        wsi = [0]
        nacc = {"w1s": 0, "w2s": 0}
        wsl = [0]

        def emit_wsubs(kmax):
            while wsi[0] < min(kmax, len(wsub_jobs)):
                kind, a, h = wsub_jobs[wsi[0]]
                slot = wsi[0] % 2
                if kind == "wout":
                    src = wout_d[a * 128:(a + 1) * 128, h * 512:(h + 1) * 512]
                elif kind == "w1s":
                    src = w1_d[a * 128:(a + 1) * 128, h * 512:(h + 1) * 512]
                else:
                    src = w2_d[a * 128:(a + 1) * 128, h * 512:(h + 1) * 512]
                S.dma("gpsimd", stgB[slot], src,
                      waits=[("dve", t_stgB_free[slot])],
                      inc="ldw%d" % slot)
                tl = ("ldw%d" % slot, S.t("ldw%d" % slot))
                if kind == "wout":
                    S.emit("vector", lambda e, aa=a, hh=h, s2=slot:
                           e.tensor_copy(
                               wout_sb[aa][:, hh * 512:(hh + 1) * 512],
                               stgB[s2]), waits=[tl], inc="dve")
                else:
                    acc = absacc[:, 0:1] if kind == "w1s" else absacc[:, 1:2]
                    if nacc[kind] == 0:
                        S.emit("vector", lambda e, s2=slot, ac=acc:
                               e.reduce_sum(ac, stgB[s2], axis=AX,
                                            apply_absolute_value=True),
                               waits=[tl], inc="dve")
                    else:
                        S.emit("vector", lambda e, s2=slot: e.reduce_sum(
                            tmpred[:], stgB[s2], axis=AX,
                            apply_absolute_value=True),
                            waits=[tl], inc="dve")
                        t_redw = S.t("dve")
                        S.emit("vector", lambda e, ac=acc: e.tensor_add(
                            ac, ac, tmpred[:]), waits=[("dve", t_redw)],
                            inc="dve")
                    nacc[kind] += 1
                t_stgB_free[slot] = S.t("dve")
                wsi[0] += 1

        for c in range(16):
            half = c % 2
            slot = c % 2
            if c == 0:
                S.dma("sync", qslice[0].rearrange("p (t j) -> p t j", j=128),
                      q_h3[:, :, 0:128],
                      waits=t_qall + [t_klast], inc="ldq0")
                qld_t = {0: ("ldq0", S.t("ldq0"))}
                S.dma("sync", qslice[1].rearrange("p (t j) -> p t j", j=128),
                      q_h3[:, :, 128:256], inc="ldq1")
                qld_t[1] = ("ldq1", S.t("ldq1"))
            elif c + 1 < 16:
                sm = "ldq%d" % ((c + 1) % 2)
                S.dma("sync",
                      qslice[(c + 1) % 2].rearrange("p (t j) -> p t j", j=128),
                      q_h3[:, :, (c + 1) * 128:(c + 2) * 128],
                      waits=[("pe", pemm2_t[c - 1])], inc=sm)
                qld_t[c + 1] = (sm, S.t(sm))
            wl = [qld_t[c], t_klast]
            if c >= 2:
                wl += [("se", exp_t[c - 2]), ("dve", attnts_t[c - 2])]
            for n in range(4):
                for t2 in range(NT):
                    S.emit("tensor", lambda e, nn=n, tx=t2, sx=slot, hx=half:
                           e.matmul(
                               bank(hx * 4 + nn),
                               qslice[sx][:, tx * 128:(tx + 1) * 128],
                               k_tile[tx][:, nn * 512:(nn + 1) * 512],
                               start=(tx == 0), stop=(tx == NT - 1)),
                           waits=(wl if (n == 0 and t2 == 0) else []),
                           inc="pe")
            pemm2_t[c] = S.t("pe")
            phalf = psum[:, half * 2048:half * 2048 + 2048]
            wl = [("pe", pemm2_t[c])]
            if c >= 2:
                wl.append(("se", exp_t[c - 2]))
            S.emit("vector", lambda e, ph=phalf, cc=c: e.reduce_max(
                sl(cc, 10), ph, axis=AX, negate=True), waits=wl, inc="dve")
            t_rmax = S.t("dve")
            S.emit("vector", lambda e, cc=c: e.tensor_scalar(
                out=sl(cc, 11), in0=sl(cc, 10), scalar1=1.0 / NTOK,
                scalar2=None, op0=ALU.mult), waits=[("dve", t_rmax)],
                inc="dve")
            t_be = S.t("dve")
            wl = [("dve", t_be)]
            if c >= 1:
                wl.append(("dve", attnts_t[c - 1]))
            else:
                wl += [("str1", 16 * (NT // 2)), ("str1o", 16 * (NT // 2))]
            S.emit("scalar", lambda e, ph=phalf, cc=c: e.activation(
                out=attn_un[:], in_=ph, func=AF.Exp, scale=1.0 / NTOK,
                bias=sl(cc, 11), accum_out=sl(cc, 12)), waits=wl, inc="se")
            exp_t[c] = S.t("se")
            S.emit("vector", lambda e, cc=c: e.reciprocal(
                sl(cc, 13), sl(cc, 12)), waits=[("se", exp_t[c])], inc="dve")
            t_rz = S.t("dve")
            wl2 = [("dve", t_rz)]
            if c == 0:
                wl2 += [("stn0", 16 * (NT // 2)), ("stn1", 16 * (NT // 2))]
            S.emit("vector", lambda e, cc=c: e.tensor_scalar(
                out=attn_dump[:], in0=attn_un[:], scalar1=sl(cc, 13),
                scalar2=0.0, op0=ALU.mult, op1=ALU.add,
                accum_out=s_sb[:, cc:cc + 1]), waits=wl2,
                inc="dve")
            attnts_t[c] = S.t("dve")
            emit_wsubs((c + 1) * 10)
        emit_wsubs(len(wsub_jobs))
        t_absdone = S.t("dve")
        t_attnlast = S.t("dve")
        if dbg:
            S.dma("gpsimd", sdbg_d, s_sb, waits=[("dve", t_attnlast)],
                  inc="stdbg")

        wl = [("dve", t_absdone), ("se", exp_t[15]), ("dve", t_attnlast)]
        S.emit("tensor", lambda e: e.matmul(
            psum[0:1, 3584:3585], absacc[:, 0:1], onescol, start=True,
            stop=True),
            waits=wl, inc="pe")
        S.emit("tensor", lambda e: e.matmul(
            psum[0:1, 3585:3586], absacc[:, 1:2], onescol, start=True,
            stop=True),
            inc="pe")
        S.emit("scalar", lambda e: e.activation(
            out=mrow[0:1, 0:1], in_=psum[0:1, 3584:3585], func=AF.Identity,
            scale=1.0 / (FF * D)), waits=[("pe", S.t("pe"))], inc="se")
        S.emit("scalar", lambda e: e.activation(
            out=mrow[0:1, 1:2], in_=psum[0:1, 3585:3586], func=AF.Identity,
            scale=1.0 / (FF * D)), inc="se")
        S.emit("vector", lambda e: e.tensor_scalar_max(
            mrow[0:1, 0:2], mrow[0:1, 0:2], 1e-8),
            waits=[("se", S.t("se"))], inc="dve")
        S.emit("vector", lambda e: e.reciprocal(mrow[0:1, 0:2], mrow[0:1, 0:2]),
               inc="dve")
        t_minv = S.t("dve")
        S.dma("gpsimd", bcast_h[0:1, 0:2], mrow[0:1, 0:2],
              waits=[("dve", t_minv)], inc="ldr")
        for k in [2, 4, 8, 16, 32, 64, 128]:
            S.dma("gpsimd", bcast_h[0:1, k:2 * k], bcast_h[0:1, 0:k],
                  waits=[("ldr", S.t("ldr"))], inc="ldr")
        S.dma("gpsimd", ppf[:, 120:122],
              bcast_h.rearrange("a (p c) -> (a p) c", p=128),
              waits=[("ldr", S.t("ldr"))], inc="ldr")
        t_invdone = ("ldr", S.t("ldr"))

        for h in range(2):
            S.dma("gpsimd", stgB[0][0:1, 0:512],
                  bout_d[0:1, h * 512:(h + 1) * 512],
                  waits=[("dve", t_stgB_free[0])], inc="ldw0")
            S.emit("vector", lambda e, hh=h: e.tensor_copy(
                boutb[0:1, hh * 512:(hh + 1) * 512], stgB[0][0:1, 0:512]),
                waits=[("ldw0", S.t("ldw0")), t_klast], inc="dve")
            t_stgB_free[0] = S.t("dve")
        t_boutb = S.t("dve")

        whid_done = 0
        t_ubf3a_free = [S.t("pe"), S.t("pe")]
        t_fold_n = {}
        first_whid = [True]
        ubf3a2 = [qsb[:, 0:512], qsb[:, 512:1024]]
        for n in [0, 4, 1, 5, 2, 6, 3, 7]:
            for c in range(8):
                sub_i = len(t_fold_n) * 8 + c
                slot = sub_i % 2
                S.dma("gpsimd", stgB[slot],
                      whid_d[c * 128:(c + 1) * 128, n * 512:(n + 1) * 512],
                      waits=[("dve", t_stgB_free[slot])],
                      inc="ldw%d" % slot)
                tl = ("ldw%d" % slot, S.t("ldw%d" % slot))
                wl = [tl, ("pe", t_ubf3a_free[slot]), t_invdone]
                if first_whid[0]:
                    first_whid[0] = False
                    wl.append(t_klast)
                    wl.append(("pe", pemm2_t[15]))
                S.emit("vector", lambda e, s2=slot: e.tensor_copy(
                    ubf3a2[s2][:], stgB[s2]), waits=wl, inc="dve")
                t_cast = S.t("dve")
                for jj in range(4):
                    jh = n * 4 + jj
                    S.emit("tensor", lambda e, jx=jj, jhx=jh, cc=c, s2=slot:
                           e.matmul(bank(7, 4 + cc * 32 + jhx, 1),
                                    ubf3a2[s2][:, jx * 128:(jx + 1) * 128],
                                    lnb_pp[:, cc:cc + 1], start=True,
                                    stop=True),
                           waits=([("dve", t_cast)] if jj == 0 else []),
                           inc="pe")
                t_ubf3a_free[slot] = S.t("pe")
                S.emit("vector", lambda e, cc=c, nn=n, s2=slot: e.tensor_scalar(
                    out=whid_ck[cc][:, nn * 512:(nn + 1) * 512], in0=stgB[s2],
                    scalar1=lng_pp[:, cc:cc + 1], scalar2=0.0,
                    op0=ALU.mult, op1=ALU.add), inc="dve")
                t_stgB_free[slot] = S.t("dve")
            t_fold_n[n] = S.t("dve")
        t_beffmm = S.t("pe")
        S.emit("vector", lambda e: e.tensor_add(
            beff_h[:], bank(7, 4, 32), bhid_pp[:]),
            waits=[("pe", t_beffmm)], inc="dve")
        for cc in range(1, 8):
            S.emit("vector", lambda e, c2=cc: e.tensor_add(
                beff_h[:], beff_h[:], bank(7, 4 + c2 * 32, 32)), inc="dve")
        t_beff = S.t("dve")

        hmm_t = {}
        stt_t = {}
        silu3a_t = {}
        evac3a_t = {}
        evac3a_nt = {}
        ldnfm_t = {}
        ldr1_t = {}
        w1st = {"i": 0, "stg": [t_stgB_free[0], t_stgB_free[1]],
                "qfree": [S.t("pe"), S.t("pe")], "pt6": S.t("dve"),
                "copy": 0}

        def emit_w1q(upto):
            while w1st["i"] < min(upto, 64):
                si = w1st["i"]
                ft, hh = si // 2, si % 2
                slot = si % 2
                S.dma("gpsimd", stgB[slot],
                      w1_d[ft * 128:(ft + 1) * 128, hh * 512:(hh + 1) * 512],
                      waits=[("dve", w1st["stg"][slot])],
                      inc="ldw%d" % slot)
                tl = ("ldw%d" % slot, S.t("ldw%d" % slot))
                wl = [tl, ("dve", t_beff), t_invdone,
                      ("pe", w1st["qfree"][slot])]
                S.emit("vector", lambda e, s2=slot: e.tensor_scalar(
                    out=stgB[s2], in0=stgB[s2], scalar1=inv1_pp[:],
                    scalar2=RND, op0=ALU.mult, op1=ALU.add), waits=wl,
                    inc="dve")
                S.emit("vector", lambda e, s2=slot: e.tensor_scalar(
                    out=stgB[s2], in0=stgB[s2], scalar1=RND, scalar2=1.0,
                    op0=ALU.subtract, op1=ALU.min), inc="dve")
                S.emit("vector", lambda e, s2=slot: e.tensor_scalar(
                    out=w1q_stg[s2][:], in0=stgB[s2], scalar1=-1.0,
                    scalar2=None, op0=ALU.max), inc="dve")
                w1st["stg"][slot] = S.t("dve")
                t_q3 = S.t("dve")
                for i in range(4):
                    S.emit("tensor", lambda e, ii=i, s2=slot: e.transpose(
                        pT6[:, ii * 128:(ii + 1) * 128],
                        w1q_stg[s2][:, ii * 128:(ii + 1) * 128], identb[:]),
                        waits=([("dve", t_q3), ("dve", w1st["pt6"])]
                               if i == 0 else []), inc="pe")
                w1st["qfree"][slot] = S.t("pe")
                t_tr = S.t("pe")
                S.emit("vector", lambda e, ftx=ft, hx=hh: e.tensor_copy(
                    KT_w1t_3d[:, hx * 4:hx * 4 + 4,
                              ftx * 128:(ftx + 1) * 128],
                    pT6[:, 0:512].rearrange("p (c m) -> p c m", m=128)),
                    waits=[("pe", t_tr)], inc="dve")
                w1st["pt6"] = S.t("dve")
                w1st["copy"] = S.t("dve")
                w1st["i"] += 1

        for g in range(8):
            wl = []
            if g == 0:
                wl += [("stn0", 16 * (NT // 2)), ("stn1", 16 * (NT // 2)),
                       ("dve", t_attnlast)]
            else:
                wl.append(("pe", hmm_t[g - 1]))
            S.dma("sync", nfm_grp3d, nfm_h3[:, :, g * 512:(g + 1) * 512],
                  waits=wl, inc="ldnf")
            ldnfm_t[g] = ("ldnf", S.t("ldnf"))
            for jj in range(16):
                for hv in range(2):
                    jh = jj if hv == 0 else 16 + jj
                    bk = (2 * jj + hv) % 4
                    wl = [ldnfm_t[g], ("dve", t_fold_n[jh // 4])]
                    prev = (g, jj - 2) if jj >= 2 else (
                        (g - 1, 14 + jj) if g >= 1 else None)
                    if prev in silu3a_t:
                        wl.append(("se", silu3a_t[prev]))
                    for c in range(8):
                        S.emit("tensor", lambda e, cc=c, jhx=jh, bkx=bk:
                               e.matmul(
                                   bank(bkx),
                                   whid_ck[cc][:, jhx * 128:(jhx + 1) * 128],
                                   nfm_grp[:, cc * 512:(cc + 1) * 512],
                                   start=(cc == 0), stop=(cc == 7)),
                               waits=(wl if c == 0 else []), inc="pe")
                t_pair = S.t("pe")
                wl = [("pe", t_pair), ("dve", t_beff)]
                prevs = (g, jj - 1) if jj >= 1 else (
                    (g - 1, 15) if g >= 1 else None)
                if prevs in stt_t:
                    wl.append(("dve", stt_t[prevs]))
                if g == 0 and jj == 0:
                    wl.append(("dve", t_attnlast))
                S.emit("scalar", lambda e, jx=jj: e.activation(
                    out=vg[:, 0:512], in_=bank((2 * jx) % 4), func=AF.Silu,
                    bias=beff_h[:, jx:jx + 1]), waits=wl, inc="se")
                S.emit("scalar", lambda e, jx=jj: e.activation(
                    out=vg[:, 512:1024], in_=bank((2 * jx + 1) % 4),
                    func=AF.Silu, bias=beff_h[:, 16 + jx:17 + jx]), inc="se")
                silu3a_t[(g, jj)] = S.t("se")
                wl = [("se", silu3a_t[(g, jj)])]
                if g == 0 and jj < 4:
                    wl.append(("pe", pemm2_t[15]))
                S.emit("vector", lambda e, jx=jj: e.scalar_tensor_tensor(
                    out=A_sl[jx], in0=vg[:, 0:512],
                    scalar=s_sb[:, jx:jx + 1], in1=vg[:, 512:1024],
                    op0=ALU.mult, op1=ALU.mult), waits=wl, inc="dve")
                stt_t[(g, jj)] = S.t("dve")
            hmm_t[g] = S.t("pe")
            for s4 in range(4):
                tt = g * 4 + s4
                wl = [("str1", 16 * (NT // 2)),
                      ("str1o", 16 * (NT // 2))]
                if tt >= 1:
                    wl.append(("dve", evac3a_t[tt - 1]))
                else:
                    wl.append(("dve", t_attnlast))
                S.dma("sync", r1rd, r1_h[tt * 128:(tt + 1) * 128, :],
                      waits=wl, inc="ldr1")
                ldr1_t[tt] = ("ldr1", S.t("ldr1"))
                for n in range(2):
                    wl = [("dve", stt_t[(g, 15)]), ("dve", t_boutb)]
                    if (tt - 1, n) in evac3a_nt:
                        wl.append(("dve", evac3a_nt[(tt - 1, n)]))
                    for j in range(16):
                        S.emit("tensor", lambda e, jx=j, nn=n, sx=s4: e.matmul(
                            bank(4 + nn),
                            A_sl[jx][:, sx * 128:(sx + 1) * 128],
                            wout_sb[jx][:, nn * 512:(nn + 1) * 512],
                            start=(jx == 0), stop=False),
                            waits=(wl if j == 0 else []), inc="pe")
                    S.emit("tensor", lambda e, nn=n: e.matmul(
                        bank(4 + nn), onesb[:],
                        boutb[0:1, nn * 512:(nn + 1) * 512],
                        start=False, stop=True), inc="pe")
                    t_omm = S.t("pe")
                    wl2 = [("pe", t_omm), ldr1_t[tt]]
                    if tt >= 1:
                        wl2.append(("stx1", 16 * tt))
                    else:
                        wl2.append(("str1", 16 * (NT // 2)))
                        wl2.append(("str1o", 16 * (NT // 2)))
                    S.emit("vector", lambda e, nn=n: e.tensor_add(
                        evacC[:, nn * 512:(nn + 1) * 512], bank(4 + nn),
                        r1rd[:, nn * 512:(nn + 1) * 512]), waits=wl2,
                        inc="dve")
                    evac3a_nt[(tt, n)] = S.t("dve")
                evac3a_t[tt] = S.t("dve")
                S.dma("gpsimd", x1_h[tt * 128:(tt + 1) * 128, :], evacC,
                      waits=[("dve", evac3a_t[tt])], inc="stx1")
            emit_w1q((g + 1) * 8)
        emit_w1q(64)
        t_w1done = w1st["copy"]
        t_hlast = hmm_t[7]
        t_out3a_last = S.t("pe")
        t_x1all = S.t("stx1")

        for h in range(2):
            S.dma("gpsimd", stgB[0][0:1, 0:512],
                  fb2_d[0:1, h * 512:(h + 1) * 512],
                  waits=[("dve", w1st["stg"][0])], inc="ldw0")
            S.emit("vector", lambda e, hh=h: e.tensor_copy(
                fb2b[0:1, hh * 512:(hh + 1) * 512], stgB[0][0:1, 0:512]),
                waits=[("ldw0", S.t("ldw0")), ("pe", w1st["qfree"][0]),
                       ("pe", w1st["qfree"][1])], inc="dve")
            w1st["stg"][0] = S.t("dve")
        t_fb2 = S.t("dve")

        w2_copy_t = 0
        t_pt7_free = t_beff
        t_w2stg = [w1st["stg"][0], w1st["stg"][1]]
        t_w2q_free = [t_fb2, t_fb2]
        for cd in range(8):
            for ss in range(8):
                si = cd * 8 + ss
                slot = si % 2
                S.dma("gpsimd", stgB[slot],
                      w2_d[cd * 128:(cd + 1) * 128, ss * 512:(ss + 1) * 512],
                      waits=[("dve", t_w2stg[slot])], inc="ldw%d" % slot)
                tl = ("ldw%d" % slot, S.t("ldw%d" % slot))
                S.emit("vector", lambda e, s2=slot: e.tensor_scalar(
                    out=stgB[s2], in0=stgB[s2], scalar1=inv2_pp[:],
                    scalar2=RND, op0=ALU.mult, op1=ALU.add),
                    waits=[tl, ("dve", t_w2q_free[slot]), t_invdone],
                    inc="dve")
                S.emit("vector", lambda e, s2=slot: e.tensor_scalar(
                    out=stgB[s2], in0=stgB[s2], scalar1=RND, scalar2=1.0,
                    op0=ALU.subtract, op1=ALU.min), inc="dve")
                S.emit("vector", lambda e, s2=slot: e.tensor_scalar(
                    out=w2q_stg[s2][:], in0=stgB[s2], scalar1=-1.0,
                    scalar2=None, op0=ALU.max), inc="dve")
                t_w2stg[slot] = S.t("dve")
                t_q3 = S.t("dve")
                for i in range(4):
                    wl = []
                    if i == 0:
                        wl = [("dve", t_q3), ("dve", t_pt7_free)]
                    S.emit("tensor", lambda e, ii=i, s2=slot: e.transpose(
                        pT7[:, ii * 128:(ii + 1) * 128],
                        w2q_stg[s2][:, ii * 128:(ii + 1) * 128], identb[:]),
                        waits=wl, inc="pe")
                t_tr = S.t("pe")
                wl = [("pe", t_tr)]
                if si == 0:
                    wl.append(("pe", t_hlast))
                S.emit("vector", lambda e, cdx=cd, ssx=ss: e.tensor_copy(
                    KT_w2t_3d[:, ssx * 4:ssx * 4 + 4,
                              cdx * 128:(cdx + 1) * 128],
                    pT7[:, 0:512].rearrange("p (c m) -> p c m", m=128)),
                    waits=wl, inc="dve")
                t_pt7_free = S.t("dve")
                t_w2q_free[slot] = S.t("dve")
                w2_copy_t = S.t("dve")
        t_w2done = w2_copy_t

        ffn1_t = {}
        ffn2_t = {}
        gelu_t = {}
        ycopy_t = {}
        evac3b_t = {}
        evac3b_nt = {}
        ldx1_t = {}
        lnse3_t = {}
        sq3_t = {}
        transp3_t = {}

        y_t = {}
        for g in range(8):
            for s4 in range(4):
                tt = g * 4 + s4
                wl = [("stx1", 16 * NT)]
                if s4 >= 2:
                    wl.append(("dve", y_t[tt - 2]))
                elif g >= 1:
                    wl.append(("dve", evac3b_nt[(tt - 2, 1)]))
                else:
                    wl += [("dve", evac3a_t[NT - 1]),
                           ("dve", stt_t[(7, 15)]),
                           ("dve", t_w2stg[0]), ("dve", t_w2stg[1])]
                sm = "ldy%d" % (tt % 2)
                S.dma("sync", x1rd[tt % 2], x1_h[tt * 128:(tt + 1) * 128, :],
                      waits=wl, inc=sm)
                ldx1_t[tt] = (sm, S.t(sm))
                wl = [ldx1_t[tt]]
                if tt >= 2:
                    wl.append(("se", lnse3_t[tt - 2]))
                S.emit("vector", lambda e, x2=x1rd[tt % 2], tx=tt:
                       e.reduce_sum(sl(tx, 0), x2, axis=AX), waits=wl,
                       inc="dve")
                t_red3 = S.t("dve")
                S.emit("vector", lambda e, tx=tt: e.tensor_scalar(
                    out=sl(tx, 1), in0=sl(tx, 0), scalar1=-1.0 / D,
                    scalar2=None, op0=ALU.mult), waits=[("dve", t_red3)],
                    inc="dve")
                t_negmu = S.t("dve")
                wl = [("dve", t_negmu)]
                if tt == 0:
                    wl.append(("pe", t_out3a_last))
                S.emit("scalar", lambda e, x2=x1rd[tt % 2], tx=tt:
                       e.activation(out=sqdump3b[:], in_=x2, func=AF.Square,
                                    bias=sl(tx, 1), accum_out=sl(tx, 2)),
                       waits=wl, inc="se")
                sq3_t[tt] = S.t("se")
                S.emit("scalar", lambda e, tx=tt: e.activation(
                    out=sl(tx, 3), in_=sl(tx, 2), func=AF.Sqrt, scale=1.0 / D,
                    bias=epsc), inc="se")
                lnse3_t[tt] = S.t("se")
                S.emit("vector", lambda e, tx=tt: e.reciprocal(
                    sl(tx, 4), sl(tx, 3)), waits=[("se", lnse3_t[tt])],
                    inc="dve")
                t_rstd3 = S.t("dve")
                wl = []
                if tt == 0:
                    wl.append(("dve", t_w2done))
                else:
                    wl.append(("pe", transp3_t[tt - 1]))
                wl.append(("dve", t_rstd3))
                S.emit("vector", lambda e, x2=x1rd[tt % 2], tx=tt:
                       e.tensor_scalar(
                           out=y_bf[:], in0=x2, scalar1=sl(tx, 1),
                           scalar2=sl(tx, 4), op0=ALU.add, op1=ALU.mult),
                       waits=wl, inc="dve")
                t_y = S.t("dve")
                y_t[tt] = t_y
                wl = [("dve", t_y)]
                if tt >= 1:
                    wl.append(("dve", ycopy_t[tt - 1]))
                else:
                    wl.append(("dve", t_w1done))
                for k in range(8):
                    S.emit("tensor", lambda e, kk=k: e.transpose(
                        pT6[:, kk * 128:(kk + 1) * 128],
                        y_bf[:, kk * 128:(kk + 1) * 128], identb[:]),
                        waits=(wl if k == 0 else []), inc="pe")
                transp3_t[tt] = S.t("pe")
                wl = [("pe", transp3_t[tt])]
                if s4 == 0:
                    if g >= 1:
                        wl.append(("pe", ffn1_t[g - 1]))
                    else:
                        wl.append(("pe", t_hlast))
                S.emit("vector", lambda e, sx=s4: e.tensor_copy(
                    nfm_grp3d[:, :, sx * 128:(sx + 1) * 128],
                    pT6.rearrange("p (c m) -> p c m", m=128)),
                    waits=wl, inc="dve")
                ycopy_t[tt] = S.t("dve")
            t_ygrp = S.t("dve")
            for j in range(32):
                bk = j % 4
                wl = [("dve", t_ygrp), ("dve", t_w1done)]
                prev = (g, j - 4) if j >= 4 else (
                    (g - 1, 28 + j) if g >= 1 else None)
                if prev in gelu_t:
                    wl.append(("se", gelu_t[prev]))
                for c in range(8):
                    S.emit("tensor", lambda e, cc=c, jx=j, bkx=bk: e.matmul(
                        bank(bkx), w1t_ck[cc][:, jx * 128:(jx + 1) * 128],
                        nfm_grp[:, cc * 512:(cc + 1) * 512],
                        start=(cc == 0), stop=(cc == 7)),
                        waits=(wl if c == 0 else []), inc="pe")
                t_f1 = S.t("pe")
                wl = [("pe", t_f1)]
                if g >= 1:
                    wl.append(("pe", ffn2_t[g - 1]))
                elif j == 0:
                    wl.append(("pe", t_out3a_last))
                S.emit("scalar", lambda e, jx=j, bkx=bk: e.activation(
                    out=h1_sb[jx][:], in_=bank(bkx), func=AF.Gelu,
                    bias=fb1_pp[:, jx:jx + 1]), waits=wl, inc="se")
                gelu_t[(g, j)] = S.t("se")
            ffn1_t[g] = S.t("pe")
            t_gelu_all = S.t("se")
            for s4 in range(4):
                tt = g * 4 + s4
                if s4 < 2:
                    wl = [("dve", y_t[g * 4 + s4 + 2])]
                else:
                    wl = [("dve", evac3b_nt[(tt - 2, 1)])]
                sm = "ldy%d" % (tt % 2)
                S.dma("sync", x1rd[tt % 2], x1_h[tt * 128:(tt + 1) * 128, :],
                      waits=wl, inc=sm)
                t_x1b = (sm, S.t(sm))
                for n in range(2):
                    wl = [("se", t_gelu_all), ("dve", t_w2done),
                          ("dve", t_fb2), t_x1b]
                    if (tt - 1, n) in evac3b_nt:
                        wl.append(("dve", evac3b_nt[(tt - 1, n)]))
                    elif tt == 0:
                        wl.append(("dve", evac3a_nt[(NT - 1, n)]))
                    for j in range(32):
                        S.emit("tensor", lambda e, jx=j, nn=n, sx=s4: e.matmul(
                            bank(4 + nn),
                            h1_sb[jx][:, sx * 128:(sx + 1) * 128],
                            w2t_ck[jx][:, nn * 512:(nn + 1) * 512],
                            start=(jx == 0), stop=False),
                            waits=(wl if j == 0 else []), inc="pe")
                    S.emit("tensor", lambda e, nn=n: e.matmul(
                        bank(4 + nn), onesb[:],
                        fb2b[0:1, nn * 512:(nn + 1) * 512],
                        start=False, stop=True), inc="pe")
                    t_omm = S.t("pe")
                    wl2 = [("pe", t_omm), t_x1b]
                    if tt >= 1:
                        wl2.append(("stout", 16 * tt))
                    else:
                        wl2.append(("stx1", t_x1all))
                    S.emit("vector", lambda e, nn=n, sx=s4: e.tensor_add(
                        evacC[:, nn * 512:(nn + 1) * 512], bank(4 + nn),
                        x1rd[(g * 4 + sx) % 2][:, nn * 512:(nn + 1) * 512]),
                        waits=wl2, inc="dve")
                    evac3b_nt[(tt, n)] = S.t("dve")
                evac3b_t[tt] = S.t("dve")
                S.dma("gpsimd", out_d[tt * 128:(tt + 1) * 128, :], evacC,
                      waits=[("dve", evac3b_t[tt])], inc="stout")
            ffn2_t[g] = S.t("pe")

        if dbg:
            t_done = S.t("pe")
            for hh in range(8):
                S.dma("gpsimd", ktf_d[:, hh * 8192:(hh + 1) * 8192],
                      KT[:, hh * 8192:(hh + 1) * 8192],
                      waits=[("pe", t_done)], inc="stdbg")
            S.dma("gpsimd", h1f_d, wA[:], waits=[("pe", t_done)], inc="stdbg")
            S.dma("gpsimd", ppf_d, ppf[:], waits=[("pe", t_done)],
                  inc="stdbg")
        fw = [("stout", 16 * NT)]
        if dbg:
            fw.append(("stdbg", S.t("stdbg")))
        S.emit("gpsimd", lambda e: e.nop(), waits=fw)

        with nc.Block() as block:
            @block.sync
            def _(eng):
                S.replay("sync", eng)

            @block.vector
            def _(eng):
                S.replay("vector", eng)

            @block.scalar
            def _(eng):
                S.replay("scalar", eng)

            @block.tensor
            def _(eng):
                S.replay("tensor", eng)

            @block.gpsimd
            def _(eng):
                S.replay("gpsimd", eng)

    return nc


_NC_CACHE = {}


def _get_nc(dbg=False):
    key = bool(dbg)
    if key not in _NC_CACHE:
        _NC_CACHE[key] = build_nc(dbg=key)
    return _NC_CACHE[key]


def kernel(L0_attn_x, L0_attn_ln_g, L0_attn_ln_b, L0_attn_W_hidden,
           L0_attn_b_hidden, L0_attn_W_qk, L0_attn_b_qk, L0_attn_os_gamma,
           L0_attn_os_beta, L0_attn_W_out, L0_attn_b_out, L0_ffn_W1,
           L0_ffn_b1, L0_ffn_W2, L0_ffn_b2):
    x = np.ascontiguousarray(np.asarray(L0_attn_x, dtype=np.float32))
    B = x.shape[0]
    assert B == 8 and x.shape[1] == NTOK and x.shape[2] == D

    com = dict(
        wqk=np.ascontiguousarray(np.asarray(L0_attn_W_qk, np.float32)),
        whid=np.ascontiguousarray(np.asarray(L0_attn_W_hidden, np.float32)),
        wout=np.ascontiguousarray(np.asarray(L0_attn_W_out, np.float32)),
        w1=np.ascontiguousarray(np.asarray(L0_ffn_W1, np.float32)),
        w2=np.ascontiguousarray(np.asarray(L0_ffn_W2, np.float32)),
        lng=np.asarray(L0_attn_ln_g, np.float32).reshape(1, D),
        lnb=np.asarray(L0_attn_ln_b, np.float32).reshape(1, D),
        bqk=np.asarray(L0_attn_b_qk, np.float32).reshape(1, QKD),
        bhid=np.asarray(L0_attn_b_hidden, np.float32).reshape(1, HID2),
        bout=np.asarray(L0_attn_b_out, np.float32).reshape(1, D),
        g0=np.ascontiguousarray(
            np.asarray(L0_attn_os_gamma, np.float32)[0]).reshape(1, QKD),
        b0=np.ascontiguousarray(
            np.asarray(L0_attn_os_beta, np.float32)[0]).reshape(1, QKD),
        g1=np.ascontiguousarray(
            np.asarray(L0_attn_os_gamma, np.float32)[1]).reshape(1, QKD),
        b1=np.ascontiguousarray(
            np.asarray(L0_attn_os_beta, np.float32)[1]).reshape(1, QKD),
        fb1=np.asarray(L0_ffn_b1, np.float32).reshape(1, FF),
        fb2=np.asarray(L0_ffn_b2, np.float32).reshape(1, D),
        iden=np.eye(128, dtype=np.float32),
    )
    in_maps = [dict(com, x=np.ascontiguousarray(x[i])) for i in range(B)]

    nc = _get_nc(dbg=DBG)
    res = run_bass_kernel_spmd(nc, in_maps, core_ids=list(range(B)))
    out = np.stack([np.asarray(r["out"]) for r in res.results], axis=0)
    if DBG:
        kernel.last_results = res.results
    return out


# version 19 (final)
# speedup vs baseline: 1.2669x; 1.0060x over previous
import os
import numpy as np
from contextlib import ExitStack

import concourse.bass as bass
import concourse.mybir as mybir
from concourse.bass_utils import run_bass_kernel_spmd

F32 = mybir.dt.float32
BF16 = mybir.dt.bfloat16
AF = mybir.ActivationFunctionType
ALU = mybir.AluOpType
AX = mybir.AxisListType.X

NTOK, D, QKD, HID2, FF = 4096, 1024, 2048, 4096, 4096
NT = NTOK // 128
EPS = 1e-5
RND = 12582912.0

DBG = bool(int(os.environ.get("MK_DBG", "0")))


class Sched:
    ENGS = ("sync", "vector", "scalar", "tensor", "gpsimd")

    def __init__(self, nc):
        self.nc = nc
        self.q = {e: [] for e in self.ENGS}
        self.cnt = {}
        self.sem_h = {}

    def new_sem(self, stack, name):
        self.sem_h[name] = stack.enter_context(self.nc.semaphore(name))
        self.cnt[name] = 0

    def t(self, sem):
        return self.cnt[sem]

    def emit(self, eng, fn, waits=(), inc=None, amt=1):
        waits = [(s, v) for (s, v) in waits if v > 0]
        for s, v in waits:
            assert v <= self.cnt[s], f"future ticket {s}:{v}>{self.cnt[s]}"
        if eng == "vector":
            assert inc in (None, "dve")
            inc, amt = "dve", 1
        elif eng == "scalar":
            assert inc in (None, "se")
            inc, amt = "se", 1
        if inc is not None:
            self.cnt[inc] += amt
        self.q[eng].append((fn, tuple(waits), inc, amt))

    def dma(self, eng, out, in_, waits=(), inc=None):
        self.emit(eng, lambda e, o=out, i=in_: e.dma_start(out=o, in_=i),
                  waits=waits, inc=inc, amt=16)

    def replay(self, eng_name, eng):
        serial = {"vector": "dve", "scalar": "se"}.get(eng_name)
        k = 0
        kp = 0
        for fn, waits, inc, amt in self.q[eng_name]:
            if serial and k > 0:
                eng.wait_ge(self.sem_h[serial], k)
            if eng_name == "gpsimd" and inc == "pool" and kp > 0:
                eng.wait_ge(self.sem_h["pool"], kp)
            for s, v in waits:
                eng.wait_ge(self.sem_h[s], v)
            instr = fn(eng)
            if inc is not None:
                instr.then_inc(self.sem_h[inc], amt)
            if serial:
                k += 1
            if eng_name == "gpsimd" and inc == "pool":
                kp += 1


def build_nc(dbg=False):
    nc = bass.Bass("TRN2", target_bir_lowering=False, debug=False)

    x_d = nc.dram_tensor("x", [NTOK, D], F32, kind="ExternalInput").ap()
    wqk_d = nc.dram_tensor("wqk", [D, QKD], F32, kind="ExternalInput").ap()
    whid_d = nc.dram_tensor("whid", [D, HID2], F32, kind="ExternalInput").ap()
    wout_d = nc.dram_tensor("wout", [QKD, D], F32, kind="ExternalInput").ap()
    w1_d = nc.dram_tensor("w1", [FF, D], F32, kind="ExternalInput").ap()
    w2_d = nc.dram_tensor("w2", [D, FF], F32, kind="ExternalInput").ap()
    lng_d = nc.dram_tensor("lng", [1, D], F32, kind="ExternalInput").ap()
    lnb_d = nc.dram_tensor("lnb", [1, D], F32, kind="ExternalInput").ap()
    bqk_d = nc.dram_tensor("bqk", [1, QKD], F32, kind="ExternalInput").ap()
    bhid_d = nc.dram_tensor("bhid", [1, HID2], F32, kind="ExternalInput").ap()
    bout_d = nc.dram_tensor("bout", [1, D], F32, kind="ExternalInput").ap()
    g0_d = nc.dram_tensor("g0", [1, QKD], F32, kind="ExternalInput").ap()
    b0_d = nc.dram_tensor("b0", [1, QKD], F32, kind="ExternalInput").ap()
    g1_d = nc.dram_tensor("g1", [1, QKD], F32, kind="ExternalInput").ap()
    b1_d = nc.dram_tensor("b1", [1, QKD], F32, kind="ExternalInput").ap()
    fb1_d = nc.dram_tensor("fb1", [1, FF], F32, kind="ExternalInput").ap()
    fb2_d = nc.dram_tensor("fb2", [1, D], F32, kind="ExternalInput").ap()
    iden_d = nc.dram_tensor("iden", [128, 128], F32, kind="ExternalInput").ap()

    out_d = nc.dram_tensor("out", [NTOK, D], F32, kind="ExternalOutput").ap()

    skind = "ExternalOutput" if dbg else "Internal"
    q_h = nc.dram_tensor("q_h", [NTOK, QKD], BF16, kind="Internal").ap()
    nfm_h = nc.dram_tensor("nfm_h", [D, NTOK], BF16, kind="Internal").ap()
    r1_h = nc.dram_tensor("r1_h", [NTOK, D], F32, kind=skind).ap()
    x1_h = nc.dram_tensor("x1_h", [NTOK, D], F32, kind=skind).ap()
    bcast_h = nc.dram_tensor("bcast_h", [1, 256], F32, kind="Internal").ap()
    if dbg:
        sdbg_d = nc.dram_tensor("sdbg", [128, 16], F32,
                                kind="ExternalOutput").ap()
        qf_d = nc.dram_tensor("qf", [NTOK, QKD], F32,
                              kind="ExternalOutput").ap()
        ktf_d = nc.dram_tensor("ktf", [128, 65536], F32,
                               kind="ExternalOutput").ap()
        h1f_d = nc.dram_tensor("h1f", [128, 16384], F32,
                               kind="ExternalOutput").ap()
        ppf_d = nc.dram_tensor("ppfd", [128, 132], F32,
                               kind="ExternalOutput").ap()
        nfmf_d = nc.dram_tensor("nfmf", [D, NTOK], F32,
                                kind="ExternalOutput").ap()
        kf_d = nc.dram_tensor("kf", [NTOK, QKD], F32,
                              kind="ExternalOutput").ap()

    with ExitStack() as stk:
        sbuf = lambda *a, **k: stk.enter_context(nc.sbuf_tensor(*a, **k))

        KT = sbuf("KT", [128, 65536], BF16)
        wA = sbuf("wA", [128, 16384], BF16)
        scrA = sbuf("scrA", [128, 8192], BF16)
        scrB = sbuf("scrB", [128, 3072], F32)
        scrC = sbuf("scrC", [128, 1024], F32)
        qsb = sbuf("qsb", [128, 2048], BF16)
        nfm = sbuf("nfm", [128, 4096], BF16)
        qtm = sbuf("qtm", [128, 1024], BF16)
        stats = sbuf("stats", [128, 64], F32)
        ppf = sbuf("ppf", [128, 132], F32)
        ppb = sbuf("ppb", [128, 8], BF16)
        identb = sbuf("identb", [128, 128], BF16)
        onesb = sbuf("onesb", [1, 128], BF16)
        onesf = sbuf("onesf", [1, 128], F32)
        onef1 = sbuf("onef1", [1, 1], F32)

        psum = stk.enter_context(nc.psum_tensor("psum", [128, 4096], F32))

        S = Sched(nc)
        for s in ["ldx0", "ldx1", "ldw0", "ldw1", "ldq0", "ldq1", "ldnf",
                  "ldr1", "ldy0", "ldy1", "ldr", "ldr2", "stq0", "stq1",
                  "stn0", "stn1", "str1", "str1o", "stx1", "stout", "pe", "dve", "se",
                  "pool", "stdbg"]:
            S.new_sem(stk, s)

        def bank(b, off=0, w=512):
            return psum[:, b * 512 + off:b * 512 + off + w]

        psum_b = psum.bitcast(BF16)
        pT = psum_b[:, 4096:5120]
        pT6 = psum_b[:, 6144:7168]
        pT7 = psum_b[:, 7168:8192]
        browp = psum[0:1, 3584:4096]

        KTf = KT.bitcast(F32)
        rowf = KTf[0:1, 31744:32768]
        rowf2 = KTf[0:1, 30720:31744]
        wstg = [KTf[:, 28672 + i * 512:28672 + (i + 1) * 512] for i in range(2)]
        idstg = KTf[:, 28672:28800]
        ubf = KT[:, 56320:56832]

        wqk_sb = [wA[:, c * 2048:(c + 1) * 2048] for c in range(8)]
        wout_sb = [wA[:, j * 1024:(j + 1) * 1024] for j in range(16)]
        h1_sb = [wA[:, j * 512:(j + 1) * 512] for j in range(32)]

        k_tile = [KT[:, t * 2048:(t + 1) * 2048] for t in range(NT)]
        w1t_ck = [KT[:, c * 4096:(c + 1) * 4096] for c in range(8)]
        whid_ck = [KT[:, 32768 + c * 4096:32768 + (c + 1) * 4096]
                   for c in range(8)]
        w2t_ck = [KT[:, 32768 + j * 1024:32768 + (j + 1) * 1024]
                  for j in range(32)]
        KT_w1t_3d = KT.rearrange("p (c m) -> p c m", m=4096)
        KT_w2t_3d = KT[:, 32768:65536].rearrange("p (j m) -> p j m", m=1024)

        reps = {v: scrA[:, i * 2048:(i + 1) * 2048]
                for i, v in enumerate(["g0", "b0", "g1", "b1"])}
        qslice = [scrA[:, i * 4096:(i + 1) * 4096] for i in range(2)]
        A_sl = [scrA[:, j * 512:(j + 1) * 512] for j in range(16)]
        sqdump3b = scrA[:, 0:1024]

        xb = [scrB[:, 0:1024], scrB[:, 1024:2048]]
        lnx = scrB[:, 2048:3072]
        attn_un = scrB[:, 0:2048]
        stgB = [scrB[:, 2048:2560], scrB[:, 2560:3072]]
        r1rd = scrB[:, 0:1024]
        vg = scrB[:, 1024:2048]
        x1rd = [scrB[:, 0:1024], scrB[:, 1024:2048]]

        evacC = scrC[:, 0:1024]
        scrCb2 = scrC.bitcast(BF16)
        qtm_sl2 = scrCb2[:, 0:1024]
        qksilu4 = [qsb[:, 1024:1536], qsb[:, 1536:2048],
                   scrCb2[:, 1024:1536], scrCb2[:, 1536:2048]]

        qtm_sl = None
        normed_bf = qsb[:, 0:1024]
        qksilu = [qsb[:, 1024:1536], qsb[:, 1536:2048]]
        ubf3a = qsb[:, 0:512]
        w1q_stg = [qsb[:, 0:512], qsb[:, 512:1024]]
        fb2b = qsb[0:1, 0:1024]
        w2q_stg = [qsb[:, 1024:1536], qsb[:, 1536:2048]]
        y_bf = qsb[:, 1024:2048]

        nfm_wr = [nfm[:, 0:1024], nfm[:, 1024:2048]]
        brow_qk = KT[0:1, 63488:65536]
        lnxB = nfm.bitcast(F32)[:, 1024:2048]
        attn_dump = nfm[:, 0:2048]
        nfm_grp = nfm[:]
        nfm_grp3d = nfm.rearrange("p (c m) -> p c m", m=512)
        rb_row = nfm[0:1, 0:2048]

        qch = [qtm[:, 0:512], qtm[:, 512:1024]]
        boutb = qtm[0:1, 0:1024]

        lng_pp = ppf[:, 0:8]
        lnb_pp = ppb[:, 0:8]
        bhid_pp = ppf[:, 8:40]
        fb1_pp = ppf[:, 40:72]
        beff_h = ppf[:, 72:104]
        s_sb = ppf[:, 104:120]
        inv1_pp = ppf[:, 120:121]
        inv2_pp = ppf[:, 121:122]
        absacc = ppf[:, 122:124]
        tmpred = ppf[:, 126:127]
        epsc = ppf[:, 127:128]
        onescol = ppf[:, 128:129]
        mrow = ppf[0:1, 124:126]

        def sl(t, i):
            return stats[:, (t % 4) * 16 + i:(t % 4) * 16 + i + 1]

        nfm_h3 = nfm_h.rearrange("(c p) m -> p c m", p=128)
        q_h3 = q_h.rearrange("(t p) f -> p t f", p=128)

        xld_t = {}
        S.dma("sync", xb[0], x_d[0:128, :], inc="ldx0")
        xld_t[0] = ("ldx0", S.t("ldx0"))
        S.dma("sync", xb[1], x_d[128:256, :], inc="ldx1")
        xld_t[1] = ("ldx1", S.t("ldx1"))

        S.emit("vector", lambda e: e.memset(onesb[:], 1.0))
        S.emit("vector", lambda e: e.memset(onesf[:], 1.0))
        S.emit("vector", lambda e: e.memset(onef1[:], 1.0))
        S.emit("vector", lambda e: e.memset(epsc, EPS))
        S.emit("vector", lambda e: e.memset(onescol, 1.0), inc="dve")
        t_ones = S.t("dve")

        S.dma("gpsimd", idstg, iden_d, inc="ldr")
        S.dma("gpsimd", rowf, lng_d, inc="ldr")
        S.dma("gpsimd", rowf2, lnb_d, inc="ldr2")
        S.emit("vector", lambda e: e.tensor_copy(identb[:], idstg),
               waits=[("ldr", S.t("ldr"))], inc="dve")
        t_ident = S.t("dve")
        first5 = [True]

        def scat(col, row_ap, wts, last=False):
            f = first5[0]
            first5[0] = False
            S.emit("tensor", lambda e, c=col, r=row_ap, ff=f, ll=last: e.matmul(
                bank(5, c, 1), r, onef1[:], start=ff, stop=ll),
                waits=wts, inc="pe")

        for c in range(8):
            scat(c, rowf[0:1, c * 128:(c + 1) * 128],
                 [("ldr", S.t("ldr")), ("ldr2", S.t("ldr2")),
                  ("dve", t_ones)])
        for c in range(8):
            scat(8 + c, rowf2[0:1, c * 128:(c + 1) * 128], [])
        for h in range(4):
            S.dma("gpsimd", rowf, bhid_d[0:1, h * 1024:(h + 1) * 1024],
                  waits=[("pe", S.t("pe"))], inc="ldr")
            tl = S.t("ldr")
            for c in range(8):
                scat(16 + h * 8 + c, rowf[0:1, c * 128:(c + 1) * 128],
                     [("ldr", tl)])
        for h in range(4):
            S.dma("gpsimd", rowf, fb1_d[0:1, h * 1024:(h + 1) * 1024],
                  waits=[("pe", S.t("pe"))], inc="ldr")
            tl = S.t("ldr")
            for c in range(8):
                scat(48 + h * 8 + c, rowf[0:1, c * 128:(c + 1) * 128],
                     [("ldr", tl)], last=(h == 3 and c == 7))
        t_scat = S.t("pe")
        S.emit("vector", lambda e: e.tensor_copy(lng_pp[:], bank(5, 0, 8)),
               waits=[("pe", t_scat)], inc="dve")
        S.emit("vector", lambda e: e.tensor_copy(lnb_pp[:], bank(5, 8, 8)),
               inc="dve")
        S.emit("vector", lambda e: e.tensor_copy(bhid_pp[:], bank(5, 16, 32)),
               inc="dve")
        S.emit("vector", lambda e: e.tensor_copy(fb1_pp[:], bank(5, 48, 32)),
               inc="dve")
        t_ppcopy = S.t("dve")

        t_rep_prev = 0
        for i, vd in enumerate([g0_d, b0_d, g1_d, b1_d]):
            nm = ["g0", "b0", "g1", "b1"][i]
            for h in range(2):
                S.dma("gpsimd", rowf, vd[0:1, h * 1024:(h + 1) * 1024],
                      waits=[("pe", S.t("pe")), ("dve", S.t("dve"))],
                      inc="ldr")
                S.emit("vector", lambda e, hh=h: e.tensor_copy(
                    rb_row[0:1, hh * 1024:(hh + 1) * 1024], rowf),
                    waits=[("ldr", S.t("ldr"))], inc="dve")
            t_rb = S.t("dve")
            for n in range(4):
                S.emit("tensor", lambda e, nn=n: e.matmul(
                    bank(6), onesb[:], rb_row[0:1, nn * 512:(nn + 1) * 512],
                    start=True, stop=True),
                    waits=[("dve", t_rb), ("dve", t_ident),
                           ("dve", t_rep_prev)], inc="pe")
                S.emit("vector", lambda e, nn=n, nmx=nm: e.tensor_copy(
                    reps[nmx][:, nn * 512:(nn + 1) * 512], bank(6)),
                    waits=[("pe", S.t("pe"))], inc="dve")
                t_rep_prev = S.t("dve")

        t_wstg_free = [S.t("dve"), S.t("dve")]
        t_ubf_free = S.t("pe")
        for n in range(4):
            for c in range(8):
                sub = n * 8 + c
                slot = sub % 2
                S.dma("gpsimd", wstg[slot],
                      wqk_d[c * 128:(c + 1) * 128, n * 512:(n + 1) * 512],
                      waits=[("dve", t_wstg_free[slot])],
                      inc="ldw%d" % slot)
                tl = ("ldw%d" % slot, S.t("ldw%d" % slot))
                S.emit("vector", lambda e, s2=slot: e.tensor_copy(
                    ubf[:], wstg[s2]),
                    waits=[tl, ("pe", t_ubf_free)], inc="dve")
                t_cast = S.t("dve")
                S.emit("tensor", lambda e, cc=c: e.matmul(
                    browp, lnb_pp[:, cc:cc + 1], ubf[:],
                    start=(cc == 0), stop=(cc == 7)),
                    waits=[("dve", t_cast)], inc="pe")
                t_ubf_free = S.t("pe")
                S.emit("vector", lambda e, cc=c, nn=n, s2=slot: e.tensor_scalar(
                    out=wqk_sb[cc][:, nn * 512:(nn + 1) * 512], in0=wstg[s2],
                    scalar1=lng_pp[:, cc:cc + 1], scalar2=0.0,
                    op0=ALU.mult, op1=ALU.add),
                    waits=[("dve", t_ppcopy)], inc="dve")
                t_wstg_free[slot] = S.t("dve")
            t_brow_mm = S.t("pe")
            S.emit("scalar", lambda e: e.copy(rowf2[0:1, 0:512], browp),
                   waits=[("pe", t_brow_mm), ("dve", S.t("dve"))], inc="se")
            S.dma("gpsimd", rowf2[0:1, 512:1024],
                  bqk_d[0:1, n * 512:(n + 1) * 512],
                  waits=[("dve", S.t("dve"))], inc="ldr2")
            S.emit("vector", lambda e: e.tensor_add(
                rowf2[0:1, 0:512], rowf2[0:1, 0:512], rowf2[0:1, 512:1024]),
                waits=[("se", S.t("se")), ("ldr2", S.t("ldr2"))], inc="dve")
            S.emit("vector", lambda e, nn=n: e.tensor_copy(
                brow_qk[0:1, nn * 512:(nn + 1) * 512], rowf2[0:1, 0:512]),
                inc="dve")
        t_brow = S.t("dve")
        t_setup_pe = S.t("pe")
        t_setup_se = S.t("se")

        sq1_t = {}
        r1_t = {}
        transp_t = {}
        nfmcopy_t = {}
        silu_t = {}
        kaff_t = {}
        qaff_t = {}
        qdma_t = {}
        mmn_t = {}
        lnse_t = {}
        core_t = {}

        lnx_of = lambda t: lnx if t % 2 == 0 else lnxB
        sq2_t = {}

        last_dump = [0]

        negmu_t = {}

        def emit_chain_front(t):
            if t is None or t >= NT:
                return
            wl = [xld_t[t]]
            if t >= 4:
                wl.append(("se", lnse_t[t - 4]))
                wl.append(("dve", core_t[t - 4]))
            S.emit("vector", lambda e, x2=xb[t % 2], tt=t: e.reduce_sum(
                sl(tt, 0), x2, axis=AX), waits=wl, inc="dve")
            t_red1 = S.t("dve")
            S.emit("vector", lambda e, tt=t: e.tensor_scalar(
                out=sl(tt, 1), in0=sl(tt, 0), scalar1=-1.0 / D,
                scalar2=None, op0=ALU.mult), waits=[("dve", t_red1)],
                inc="dve")
            negmu_t[t] = S.t("dve")

        def emit_chain_rest(t, guard_pe=None):
            if t is None or t >= NT:
                return
            wl = [("dve", negmu_t[t])]
            if guard_pe is not None:
                wl.append(("pe", guard_pe))
            S.emit("scalar", lambda e, x2=xb[t % 2], tt=t: e.activation(
                out=normed_bf[:], in_=x2, func=AF.Square, bias=sl(tt, 1),
                accum_out=sl(tt, 2)), waits=wl, inc="se")
            sq1_t[t] = S.t("se")
            S.emit("scalar", lambda e, tt=t: e.activation(
                out=sl(tt, 3), in_=sl(tt, 2), func=AF.Sqrt, scale=1.0 / D,
                bias=epsc), inc="se")
            t_sqrt1 = S.t("se")
            S.emit("vector", lambda e, tt=t: e.reciprocal(
                sl(tt, 4), sl(tt, 3)), waits=[("se", t_sqrt1)], inc="dve")
            t_recip = S.t("dve")
            S.emit("vector", lambda e, x2=xb[t % 2], tt=t,
                   lx=lnx_of(t): e.tensor_scalar(
                out=lx, in0=x2, scalar1=sl(tt, 1), scalar2=sl(tt, 4),
                op0=ALU.add, op1=ALU.mult), waits=[("dve", t_recip)],
                inc="dve")
            S.emit("vector", lambda e, tt=t: e.scalar_tensor_tensor(
                out=sl(tt, 5), in0=sl(tt, 1), scalar=float(D),
                in1=sl(tt, 0), op0=ALU.mult, op1=ALU.add), inc="dve")
            S.emit("vector", lambda e, tt=t: e.tensor_scalar(
                out=sl(tt, 6), in0=sl(tt, 5), scalar1=sl(tt, 4),
                scalar2=-1.0 / D, op0=ALU.mult, op1=ALU.mult), inc="dve")
            t_mu2 = S.t("dve")
            S.emit("scalar", lambda e, tt=t, lx=lnx_of(t): e.activation(
                out=normed_bf[:], in_=lx, func=AF.Square, bias=sl(tt, 6),
                accum_out=sl(tt, 7)), waits=[("dve", t_mu2)], inc="se")
            sq2_t[t] = S.t("se")
            S.emit("scalar", lambda e, tt=t: e.activation(
                out=sl(tt, 8), in_=sl(tt, 7), func=AF.Sqrt, scale=1.0 / D,
                bias=epsc), inc="se")
            lnse_t[t] = S.t("se")
            S.emit("vector", lambda e, tt=t: e.reciprocal(
                sl(tt, 9), sl(tt, 8)), waits=[("se", lnse_t[t])],
                inc="dve")
            last_dump[0] = S.t("se")

        str_t = {}

        def emit_r1(t):
            S.emit("vector", lambda e, x2=xb[t % 2], lx=lnx_of(t):
                   e.tensor_add(x2, lx, x2), waits=[("se", sq1_t[t])],
                   inc="dve")
            r1_t[t] = S.t("dve")
            sm = "str1" if t % 2 == 0 else "str1o"
            S.dma("sync", r1_h[t * 128:(t + 1) * 128, :], xb[t % 2],
                  waits=[("dve", r1_t[t])], inc=sm)
            str_t[t] = (sm, S.t(sm))
            emit_prefetch(t)

        def emit_core(t):
            wl = [("se", last_dump[0])]
            if t >= 1:
                wl.append(("pe", transp_t[t - 1]))
            S.emit("vector", lambda e, tt=t, lx=lnx_of(t): e.tensor_scalar(
                out=normed_bf[:], in0=lx, scalar1=sl(tt, 6),
                scalar2=sl(tt, 9), op0=ALU.add, op1=ALU.mult),
                waits=wl, inc="dve")
            core_t[t] = S.t("dve")

        def emit_transp(t):
            wl = [("dve", core_t[t])]
            if t >= 1:
                wl.append(("dve", nfmcopy_t[t - 1]))
            for k in range(8):
                S.emit("tensor", lambda e, kk=k: e.transpose(
                    pT[:, kk * 128:(kk + 1) * 128],
                    normed_bf[:, kk * 128:(kk + 1) * 128], identb[:]),
                    waits=(wl if k == 0 else []), inc="pe")
            transp_t[t] = S.t("pe")
            wl = [("pe", transp_t[t])]
            if t >= 2:
                wl.append(("stn%d" % (t % 2), 16 * ((t - 2) // 2 + 1)))
            S.emit("vector", lambda e, s2=t % 2: e.tensor_copy(
                nfm_wr[s2][:], pT), waits=wl, inc="dve")
            nfmcopy_t[t] = S.t("dve")
            S.dma("sync", nfm_h3[:, :, t * 128:(t + 1) * 128],
                  nfm_wr[t % 2].rearrange("p (c j) -> p c j", j=128),
                  waits=[("dve", nfmcopy_t[t])], inc="stn%d" % (t % 2))
            if dbg:
                S.dma("gpsimd",
                      nfmf_d.rearrange("(c p) m -> p c m", p=128)[
                          :, :, t * 128:(t + 1) * 128],
                      nfm_wr[t % 2].rearrange("p (c j) -> p c j", j=128),
                      inc="stdbg")

        def emit_mms(t):
            for n in range(4):
                wl = [("dve", nfmcopy_t[t]), ("dve", t_brow)]
                if t >= 1:
                    wl.append(("se", silu_t[(t - 1, n)]))
                for c in range(8):
                    S.emit("tensor", lambda e, cc=c, nn=n, s2=t % 2: e.matmul(
                        bank(nn), nfm_wr[s2][:, cc * 128:(cc + 1) * 128],
                        wqk_sb[cc][:, nn * 512:(nn + 1) * 512],
                        start=(cc == 0), stop=False),
                        waits=(wl if c == 0 else []), inc="pe")
                S.emit("tensor", lambda e, nn=n: e.matmul(
                    bank(nn), onesb[:], brow_qk[0:1, nn * 512:(nn + 1) * 512],
                    start=False, stop=True), inc="pe")
                mmn_t[(t, n)] = S.t("pe")

        def emit_silu_affine(t, pr):
            for n in (2 * pr, 2 * pr + 1):
                p = n // 2
                wl = [("pe", mmn_t[(t, n)])]
                prev = (t - 1, p) if t >= 1 else None
                if prev in kaff_t:
                    wl.append(("dve", kaff_t[prev]))
                S.emit("scalar", lambda e, nn=n: e.activation(
                    out=qksilu4[nn][:], in_=bank(nn), func=AF.Silu),
                    waits=wl, inc="se")
                silu_t[(t, n)] = S.t("se")
                if n % 2 == 0:
                    continue
                qkpair = qsb[:, 1024:2048] if p == 0 else scrCb2[:, 1024:2048]
                wl = [("se", silu_t[(t, n)])]
                npair = t * 2 + p
                qt = qtm[:] if npair % 2 == 0 else qtm_sl2
                if npair >= 2:
                    wl.append(qdma_t[npair - 2])
                S.emit("vector", lambda e, pp=p, qq=qt: e.tensor_mul(
                    qq, qkpair,
                    reps["g0"][:, pp * 1024:(pp + 1) * 1024]), waits=wl,
                    inc="dve")
                S.emit("vector", lambda e, pp=p, qq=qt: e.tensor_add(
                    qq, qq,
                    reps["b0"][:, pp * 1024:(pp + 1) * 1024]), inc="dve")
                t_q = S.t("dve")
                S.dma("sync",
                      q_h[t * 128:(t + 1) * 128, p * 1024:(p + 1) * 1024],
                      qt, waits=[("dve", t_q)],
                      inc="stq%d" % (p % 2))
                qdma_t[npair] = ("stq%d" % (p % 2), S.t("stq%d" % (p % 2)))
                if dbg:
                    S.dma("gpsimd",
                          qf_d[t * 128:(t + 1) * 128,
                               p * 1024:(p + 1) * 1024],
                          qt, inc="stdbg")
                wl = []
                if t in (27, 28, 30, 31) and p == 0:
                    wl += [("pe", t_setup_pe), ("se", t_setup_se)]
                S.emit("vector", lambda e, pp=p, tt=t: e.tensor_mul(
                    k_tile[tt][:, pp * 1024:(pp + 1) * 1024], qkpair,
                    reps["g1"][:, pp * 1024:(pp + 1) * 1024]), waits=wl,
                    inc="dve")
                S.emit("vector", lambda e, pp=p, tt=t: e.tensor_add(
                    k_tile[tt][:, pp * 1024:(pp + 1) * 1024],
                    k_tile[tt][:, pp * 1024:(pp + 1) * 1024],
                    reps["b1"][:, pp * 1024:(pp + 1) * 1024]), inc="dve")
                kaff_t[(t, p)] = S.t("dve")

        def emit_prefetch(t):
            if t + 2 < NT:
                sm = "ldx%d" % (t % 2)
                S.dma("sync", xb[t % 2], x_d[(t + 2) * 128:(t + 3) * 128, :],
                      waits=[str_t[t]], inc=sm)
                xld_t[t + 2] = (sm, S.t(sm))

        emit_chain_front(0)
        emit_chain_rest(0)
        emit_chain_front(1)
        emit_chain_rest(1)
        emit_core(0)
        emit_transp(0)
        emit_r1(0)
        emit_r1(1)
        for i in range(NT // 2):
            a, b = 2 * i, 2 * i + 1
            emit_mms(a)
            emit_core(b)
            emit_silu_affine(a, 0)
            emit_transp(b)
            emit_silu_affine(a, 1)
            emit_mms(b)
            if a + 2 < NT:
                emit_chain_front(a + 2)
                emit_chain_rest(a + 2, guard_pe=transp_t[b])
                emit_core(a + 2)
                emit_r1(a + 2)
                emit_transp(a + 2)
            emit_silu_affine(b, 0)
            emit_silu_affine(b, 1)
            if b + 2 < NT:
                emit_chain_front(b + 2)
                emit_chain_rest(b + 2, guard_pe=transp_t[a + 2])
                emit_r1(b + 2)

        t_klast = ("dve", S.t("dve"))
        t_qall = [("stq0", S.t("stq0")), ("stq1", S.t("stq1"))]
        if dbg:
            for t in range(NT):
                S.dma("gpsimd", kf_d[t * 128:(t + 1) * 128, :], k_tile[t],
                      waits=[t_klast], inc="stdbg")

        pemm2_t = {}
        exp_t = {}
        attnts_t = {}

        wsub_jobs = ([("wout", j, h) for j in range(16) for h in range(2)] +
                     [("w1s", ft, h) for ft in range(32) for h in range(2)] +
                     [("w2s", cd, h) for cd in range(8) for h in range(8)])
        t_stgB_free = [S.t("dve"), S.t("dve")]
        wsi = [0]
        nacc = {"w1s": 0, "w2s": 0}
        wsl = [0]

        def emit_wsubs(kmax):
            while wsi[0] < min(kmax, len(wsub_jobs)):
                kind, a, h = wsub_jobs[wsi[0]]
                slot = wsi[0] % 2
                if kind == "wout":
                    src = wout_d[a * 128:(a + 1) * 128, h * 512:(h + 1) * 512]
                elif kind == "w1s":
                    src = w1_d[a * 128:(a + 1) * 128, h * 512:(h + 1) * 512]
                else:
                    src = w2_d[a * 128:(a + 1) * 128, h * 512:(h + 1) * 512]
                S.dma("gpsimd", stgB[slot], src,
                      waits=[("dve", t_stgB_free[slot])],
                      inc="ldw%d" % slot)
                tl = ("ldw%d" % slot, S.t("ldw%d" % slot))
                if kind == "wout":
                    S.emit("vector", lambda e, aa=a, hh=h, s2=slot:
                           e.tensor_copy(
                               wout_sb[aa][:, hh * 512:(hh + 1) * 512],
                               stgB[s2]), waits=[tl], inc="dve")
                else:
                    acc = absacc[:, 0:1] if kind == "w1s" else absacc[:, 1:2]
                    if nacc[kind] == 0:
                        S.emit("vector", lambda e, s2=slot, ac=acc:
                               e.reduce_sum(ac, stgB[s2], axis=AX,
                                            apply_absolute_value=True),
                               waits=[tl], inc="dve")
                    else:
                        S.emit("vector", lambda e, s2=slot: e.reduce_sum(
                            tmpred[:], stgB[s2], axis=AX,
                            apply_absolute_value=True),
                            waits=[tl], inc="dve")
                        t_redw = S.t("dve")
                        S.emit("vector", lambda e, ac=acc: e.tensor_add(
                            ac, ac, tmpred[:]), waits=[("dve", t_redw)],
                            inc="dve")
                    nacc[kind] += 1
                t_stgB_free[slot] = S.t("dve")
                wsi[0] += 1

        for c in range(16):
            half = c % 2
            slot = c % 2
            if c == 0:
                S.dma("sync", qslice[0].rearrange("p (t j) -> p t j", j=128),
                      q_h3[:, :, 0:128],
                      waits=t_qall + [t_klast], inc="ldq0")
                qld_t = {0: ("ldq0", S.t("ldq0"))}
                S.dma("sync", qslice[1].rearrange("p (t j) -> p t j", j=128),
                      q_h3[:, :, 128:256], inc="ldq1")
                qld_t[1] = ("ldq1", S.t("ldq1"))
            elif c + 1 < 16:
                sm = "ldq%d" % ((c + 1) % 2)
                S.dma("sync",
                      qslice[(c + 1) % 2].rearrange("p (t j) -> p t j", j=128),
                      q_h3[:, :, (c + 1) * 128:(c + 2) * 128],
                      waits=[("pe", pemm2_t[c - 1])], inc=sm)
                qld_t[c + 1] = (sm, S.t(sm))
            wl = [qld_t[c], t_klast]
            if c >= 2:
                wl += [("se", exp_t[c - 2]), ("dve", attnts_t[c - 2])]
            for n in range(4):
                for t2 in range(NT):
                    S.emit("tensor", lambda e, nn=n, tx=t2, sx=slot, hx=half:
                           e.matmul(
                               bank(hx * 4 + nn),
                               qslice[sx][:, tx * 128:(tx + 1) * 128],
                               k_tile[tx][:, nn * 512:(nn + 1) * 512],
                               start=(tx == 0), stop=(tx == NT - 1)),
                           waits=(wl if (n == 0 and t2 == 0) else []),
                           inc="pe")
            pemm2_t[c] = S.t("pe")
            phalf = psum[:, half * 2048:half * 2048 + 2048]
            wl = [("pe", pemm2_t[c])]
            if c >= 2:
                wl.append(("se", exp_t[c - 2]))
            S.emit("vector", lambda e, ph=phalf, cc=c: e.reduce_max(
                sl(cc, 10), ph, axis=AX, negate=True), waits=wl, inc="dve")
            t_rmax = S.t("dve")
            S.emit("vector", lambda e, cc=c: e.tensor_scalar(
                out=sl(cc, 11), in0=sl(cc, 10), scalar1=1.0 / NTOK,
                scalar2=None, op0=ALU.mult), waits=[("dve", t_rmax)],
                inc="dve")
            t_be = S.t("dve")
            wl = [("dve", t_be)]
            if c >= 1:
                wl.append(("dve", attnts_t[c - 1]))
            else:
                wl += [("str1", 16 * (NT // 2)), ("str1o", 16 * (NT // 2))]
            S.emit("scalar", lambda e, ph=phalf, cc=c: e.activation(
                out=attn_un[:], in_=ph, func=AF.Exp, scale=1.0 / NTOK,
                bias=sl(cc, 11), accum_out=sl(cc, 12)), waits=wl, inc="se")
            exp_t[c] = S.t("se")
            S.emit("vector", lambda e, cc=c: e.reciprocal(
                sl(cc, 13), sl(cc, 12)), waits=[("se", exp_t[c])], inc="dve")
            t_rz = S.t("dve")
            wl2 = [("dve", t_rz)]
            if c == 0:
                wl2 += [("stn0", 16 * (NT // 2)), ("stn1", 16 * (NT // 2))]
            S.emit("vector", lambda e, cc=c: e.tensor_scalar(
                out=attn_dump[:], in0=attn_un[:], scalar1=sl(cc, 13),
                scalar2=0.0, op0=ALU.mult, op1=ALU.add,
                accum_out=s_sb[:, cc:cc + 1]), waits=wl2,
                inc="dve")
            attnts_t[c] = S.t("dve")
            emit_wsubs((c + 1) * 10)
        emit_wsubs(len(wsub_jobs))
        t_absdone = S.t("dve")
        t_attnlast = S.t("dve")
        if dbg:
            S.dma("gpsimd", sdbg_d, s_sb, waits=[("dve", t_attnlast)],
                  inc="stdbg")

        wl = [("dve", t_absdone), ("se", exp_t[15]), ("dve", t_attnlast)]
        S.emit("tensor", lambda e: e.matmul(
            psum[0:1, 3584:3585], absacc[:, 0:1], onescol, start=True,
            stop=True),
            waits=wl, inc="pe")
        S.emit("tensor", lambda e: e.matmul(
            psum[0:1, 3585:3586], absacc[:, 1:2], onescol, start=True,
            stop=True),
            inc="pe")
        S.emit("scalar", lambda e: e.activation(
            out=mrow[0:1, 0:1], in_=psum[0:1, 3584:3585], func=AF.Identity,
            scale=1.0 / (FF * D)), waits=[("pe", S.t("pe"))], inc="se")
        S.emit("scalar", lambda e: e.activation(
            out=mrow[0:1, 1:2], in_=psum[0:1, 3585:3586], func=AF.Identity,
            scale=1.0 / (FF * D)), inc="se")
        S.emit("vector", lambda e: e.tensor_scalar_max(
            mrow[0:1, 0:2], mrow[0:1, 0:2], 1e-8),
            waits=[("se", S.t("se"))], inc="dve")
        S.emit("vector", lambda e: e.reciprocal(mrow[0:1, 0:2], mrow[0:1, 0:2]),
               inc="dve")
        t_minv = S.t("dve")
        S.dma("gpsimd", bcast_h[0:1, 0:2], mrow[0:1, 0:2],
              waits=[("dve", t_minv)], inc="ldr")
        for k in [2, 4, 8, 16, 32, 64, 128]:
            S.dma("gpsimd", bcast_h[0:1, k:2 * k], bcast_h[0:1, 0:k],
                  waits=[("ldr", S.t("ldr"))], inc="ldr")
        S.dma("gpsimd", ppf[:, 120:122],
              bcast_h.rearrange("a (p c) -> (a p) c", p=128),
              waits=[("ldr", S.t("ldr"))], inc="ldr")
        t_invdone = ("ldr", S.t("ldr"))

        for h in range(2):
            S.dma("gpsimd", stgB[0][0:1, 0:512],
                  bout_d[0:1, h * 512:(h + 1) * 512],
                  waits=[("dve", t_stgB_free[0])], inc="ldw0")
            S.emit("vector", lambda e, hh=h: e.tensor_copy(
                boutb[0:1, hh * 512:(hh + 1) * 512], stgB[0][0:1, 0:512]),
                waits=[("ldw0", S.t("ldw0")), t_klast], inc="dve")
            t_stgB_free[0] = S.t("dve")
        t_boutb = S.t("dve")

        whid_done = 0
        t_ubf3a_free = [S.t("pe"), S.t("pe")]
        t_fold_n = {}
        first_whid = [True]
        ubf3a2 = [qsb[:, 0:512], qsb[:, 512:1024]]
        for n in [0, 4, 1, 5, 2, 6, 3, 7]:
            for c in range(8):
                sub_i = len(t_fold_n) * 8 + c
                slot = sub_i % 2
                S.dma("gpsimd", stgB[slot],
                      whid_d[c * 128:(c + 1) * 128, n * 512:(n + 1) * 512],
                      waits=[("dve", t_stgB_free[slot])],
                      inc="ldw%d" % slot)
                tl = ("ldw%d" % slot, S.t("ldw%d" % slot))
                wl = [tl, ("pe", t_ubf3a_free[slot]), t_invdone]
                if first_whid[0]:
                    first_whid[0] = False
                    wl.append(t_klast)
                    wl.append(("pe", pemm2_t[15]))
                S.emit("vector", lambda e, s2=slot: e.tensor_copy(
                    ubf3a2[s2][:], stgB[s2]), waits=wl, inc="dve")
                t_cast = S.t("dve")
                for jj in range(4):
                    jh = n * 4 + jj
                    S.emit("tensor", lambda e, jx=jj, jhx=jh, cc=c, s2=slot:
                           e.matmul(bank(7, 4 + cc * 32 + jhx, 1),
                                    ubf3a2[s2][:, jx * 128:(jx + 1) * 128],
                                    lnb_pp[:, cc:cc + 1], start=True,
                                    stop=True),
                           waits=([("dve", t_cast)] if jj == 0 else []),
                           inc="pe")
                t_ubf3a_free[slot] = S.t("pe")
                S.emit("vector", lambda e, cc=c, nn=n, s2=slot: e.tensor_scalar(
                    out=whid_ck[cc][:, nn * 512:(nn + 1) * 512], in0=stgB[s2],
                    scalar1=lng_pp[:, cc:cc + 1], scalar2=0.0,
                    op0=ALU.mult, op1=ALU.add), inc="dve")
                t_stgB_free[slot] = S.t("dve")
            t_fold_n[n] = S.t("dve")
        t_beffmm = S.t("pe")
        S.emit("vector", lambda e: e.tensor_add(
            beff_h[:], bank(7, 4, 32), bhid_pp[:]),
            waits=[("pe", t_beffmm)], inc="dve")
        for cc in range(1, 8):
            S.emit("vector", lambda e, c2=cc: e.tensor_add(
                beff_h[:], beff_h[:], bank(7, 4 + c2 * 32, 32)), inc="dve")
        t_beff = S.t("dve")

        hmm_t = {}
        stt_t = {}
        silu3a_t = {}
        evac3a_t = {}
        evac3a_nt = {}
        ldnfm_t = {}
        ldr1_t = {}
        w1st = {"i": 0, "stg": [t_stgB_free[0], t_stgB_free[1]],
                "qfree": [S.t("pe"), S.t("pe")], "pt6": S.t("dve"),
                "copy": 0}

        def emit_w1q(upto):
            while w1st["i"] < min(upto, 64):
                si = w1st["i"]
                ft, hh = si // 2, si % 2
                slot = si % 2
                S.dma("gpsimd", stgB[slot],
                      w1_d[ft * 128:(ft + 1) * 128, hh * 512:(hh + 1) * 512],
                      waits=[("dve", w1st["stg"][slot])],
                      inc="ldw%d" % slot)
                tl = ("ldw%d" % slot, S.t("ldw%d" % slot))
                wl = [tl, ("dve", t_beff), t_invdone,
                      ("pe", w1st["qfree"][slot])]
                S.emit("vector", lambda e, s2=slot: e.tensor_scalar(
                    out=stgB[s2], in0=stgB[s2], scalar1=inv1_pp[:],
                    scalar2=RND, op0=ALU.mult, op1=ALU.add), waits=wl,
                    inc="dve")
                S.emit("vector", lambda e, s2=slot: e.tensor_scalar(
                    out=stgB[s2], in0=stgB[s2], scalar1=RND, scalar2=1.0,
                    op0=ALU.subtract, op1=ALU.min), inc="dve")
                S.emit("vector", lambda e, s2=slot: e.tensor_scalar(
                    out=w1q_stg[s2][:], in0=stgB[s2], scalar1=-1.0,
                    scalar2=None, op0=ALU.max), inc="dve")
                w1st["stg"][slot] = S.t("dve")
                t_q3 = S.t("dve")
                for i in range(4):
                    S.emit("tensor", lambda e, ii=i, s2=slot: e.transpose(
                        pT6[:, ii * 128:(ii + 1) * 128],
                        w1q_stg[s2][:, ii * 128:(ii + 1) * 128], identb[:]),
                        waits=([("dve", t_q3), ("dve", w1st["pt6"])]
                               if i == 0 else []), inc="pe")
                w1st["qfree"][slot] = S.t("pe")
                t_tr = S.t("pe")
                S.emit("vector", lambda e, ftx=ft, hx=hh: e.tensor_copy(
                    KT_w1t_3d[:, hx * 4:hx * 4 + 4,
                              ftx * 128:(ftx + 1) * 128],
                    pT6[:, 0:512].rearrange("p (c m) -> p c m", m=128)),
                    waits=[("pe", t_tr)], inc="dve")
                w1st["pt6"] = S.t("dve")
                w1st["copy"] = S.t("dve")
                w1st["i"] += 1

        for g in range(8):
            wl = []
            if g == 0:
                wl += [("stn0", 16 * (NT // 2)), ("stn1", 16 * (NT // 2)),
                       ("dve", t_attnlast)]
            else:
                wl.append(("pe", hmm_t[g - 1]))
            S.dma("sync", nfm_grp3d, nfm_h3[:, :, g * 512:(g + 1) * 512],
                  waits=wl, inc="ldnf")
            ldnfm_t[g] = ("ldnf", S.t("ldnf"))
            for jj in range(16):
                for hv in range(2):
                    jh = jj if hv == 0 else 16 + jj
                    bk = (2 * jj + hv) % 4
                    wl = [ldnfm_t[g], ("dve", t_fold_n[jh // 4])]
                    prev = (g, jj - 2) if jj >= 2 else (
                        (g - 1, 14 + jj) if g >= 1 else None)
                    if prev in silu3a_t:
                        wl.append(("se", silu3a_t[prev]))
                    for c in range(8):
                        S.emit("tensor", lambda e, cc=c, jhx=jh, bkx=bk:
                               e.matmul(
                                   bank(bkx),
                                   whid_ck[cc][:, jhx * 128:(jhx + 1) * 128],
                                   nfm_grp[:, cc * 512:(cc + 1) * 512],
                                   start=(cc == 0), stop=(cc == 7)),
                               waits=(wl if c == 0 else []), inc="pe")
                t_pair = S.t("pe")
                wl = [("pe", t_pair), ("dve", t_beff)]
                prevs = (g, jj - 1) if jj >= 1 else (
                    (g - 1, 15) if g >= 1 else None)
                if prevs in stt_t:
                    wl.append(("dve", stt_t[prevs]))
                if g == 0 and jj == 0:
                    wl.append(("dve", t_attnlast))
                S.emit("scalar", lambda e, jx=jj: e.activation(
                    out=vg[:, 0:512], in_=bank((2 * jx) % 4), func=AF.Silu,
                    bias=beff_h[:, jx:jx + 1]), waits=wl, inc="se")
                S.emit("scalar", lambda e, jx=jj: e.activation(
                    out=vg[:, 512:1024], in_=bank((2 * jx + 1) % 4),
                    func=AF.Silu, bias=beff_h[:, 16 + jx:17 + jx]), inc="se")
                silu3a_t[(g, jj)] = S.t("se")
                wl = [("se", silu3a_t[(g, jj)])]
                if g == 0 and jj < 4:
                    wl.append(("pe", pemm2_t[15]))
                S.emit("vector", lambda e, jx=jj: e.scalar_tensor_tensor(
                    out=A_sl[jx], in0=vg[:, 0:512],
                    scalar=s_sb[:, jx:jx + 1], in1=vg[:, 512:1024],
                    op0=ALU.mult, op1=ALU.mult), waits=wl, inc="dve")
                stt_t[(g, jj)] = S.t("dve")
            hmm_t[g] = S.t("pe")
            for s4 in range(4):
                tt = g * 4 + s4
                wl = [("str1", 16 * (NT // 2)),
                      ("str1o", 16 * (NT // 2))]
                if tt >= 1:
                    wl.append(("dve", evac3a_t[tt - 1]))
                else:
                    wl.append(("dve", t_attnlast))
                S.dma("sync", r1rd, r1_h[tt * 128:(tt + 1) * 128, :],
                      waits=wl, inc="ldr1")
                ldr1_t[tt] = ("ldr1", S.t("ldr1"))
                for n in range(2):
                    wl = [("dve", stt_t[(g, 15)]), ("dve", t_boutb)]
                    if (tt - 1, n) in evac3a_nt:
                        wl.append(("dve", evac3a_nt[(tt - 1, n)]))
                    for j in range(16):
                        S.emit("tensor", lambda e, jx=j, nn=n, sx=s4: e.matmul(
                            bank(4 + nn),
                            A_sl[jx][:, sx * 128:(sx + 1) * 128],
                            wout_sb[jx][:, nn * 512:(nn + 1) * 512],
                            start=(jx == 0), stop=False),
                            waits=(wl if j == 0 else []), inc="pe")
                    S.emit("tensor", lambda e, nn=n: e.matmul(
                        bank(4 + nn), onesb[:],
                        boutb[0:1, nn * 512:(nn + 1) * 512],
                        start=False, stop=True), inc="pe")
                    t_omm = S.t("pe")
                    wl2 = [("pe", t_omm), ldr1_t[tt]]
                    if tt >= 1:
                        wl2.append(("stx1", 16 * tt))
                    else:
                        wl2.append(("str1", 16 * (NT // 2)))
                        wl2.append(("str1o", 16 * (NT // 2)))
                    S.emit("vector", lambda e, nn=n: e.tensor_add(
                        evacC[:, nn * 512:(nn + 1) * 512], bank(4 + nn),
                        r1rd[:, nn * 512:(nn + 1) * 512]), waits=wl2,
                        inc="dve")
                    evac3a_nt[(tt, n)] = S.t("dve")
                evac3a_t[tt] = S.t("dve")
                S.dma("gpsimd", x1_h[tt * 128:(tt + 1) * 128, :], evacC,
                      waits=[("dve", evac3a_t[tt])], inc="stx1")
            emit_w1q((g + 1) * 8)
        emit_w1q(64)
        t_w1done = w1st["copy"]
        t_hlast = hmm_t[7]
        t_out3a_last = S.t("pe")
        t_x1all = S.t("stx1")

        for h in range(2):
            S.dma("gpsimd", stgB[0][0:1, 0:512],
                  fb2_d[0:1, h * 512:(h + 1) * 512],
                  waits=[("dve", w1st["stg"][0])], inc="ldw0")
            S.emit("vector", lambda e, hh=h: e.tensor_copy(
                fb2b[0:1, hh * 512:(hh + 1) * 512], stgB[0][0:1, 0:512]),
                waits=[("ldw0", S.t("ldw0")), ("pe", w1st["qfree"][0]),
                       ("pe", w1st["qfree"][1])], inc="dve")
            w1st["stg"][0] = S.t("dve")
        t_fb2 = S.t("dve")

        w2_copy_t = 0
        t_pt7_free = t_beff
        t_w2stg = [w1st["stg"][0], w1st["stg"][1]]
        t_w2q_free = [t_fb2, t_fb2]
        for cd in range(8):
            for ss in range(8):
                si = cd * 8 + ss
                slot = si % 2
                S.dma("gpsimd", stgB[slot],
                      w2_d[cd * 128:(cd + 1) * 128, ss * 512:(ss + 1) * 512],
                      waits=[("dve", t_w2stg[slot])], inc="ldw%d" % slot)
                tl = ("ldw%d" % slot, S.t("ldw%d" % slot))
                S.emit("vector", lambda e, s2=slot: e.tensor_scalar(
                    out=stgB[s2], in0=stgB[s2], scalar1=inv2_pp[:],
                    scalar2=RND, op0=ALU.mult, op1=ALU.add),
                    waits=[tl, ("dve", t_w2q_free[slot]), t_invdone],
                    inc="dve")
                S.emit("vector", lambda e, s2=slot: e.tensor_scalar(
                    out=stgB[s2], in0=stgB[s2], scalar1=RND, scalar2=1.0,
                    op0=ALU.subtract, op1=ALU.min), inc="dve")
                S.emit("vector", lambda e, s2=slot: e.tensor_scalar(
                    out=w2q_stg[s2][:], in0=stgB[s2], scalar1=-1.0,
                    scalar2=None, op0=ALU.max), inc="dve")
                t_w2stg[slot] = S.t("dve")
                t_q3 = S.t("dve")
                for i in range(4):
                    wl = []
                    if i == 0:
                        wl = [("dve", t_q3), ("dve", t_pt7_free)]
                    S.emit("tensor", lambda e, ii=i, s2=slot: e.transpose(
                        pT7[:, ii * 128:(ii + 1) * 128],
                        w2q_stg[s2][:, ii * 128:(ii + 1) * 128], identb[:]),
                        waits=wl, inc="pe")
                t_tr = S.t("pe")
                wl = [("pe", t_tr)]
                if si == 0:
                    wl.append(("pe", t_hlast))
                S.emit("vector", lambda e, cdx=cd, ssx=ss: e.tensor_copy(
                    KT_w2t_3d[:, ssx * 4:ssx * 4 + 4,
                              cdx * 128:(cdx + 1) * 128],
                    pT7[:, 0:512].rearrange("p (c m) -> p c m", m=128)),
                    waits=wl, inc="dve")
                t_pt7_free = S.t("dve")
                t_w2q_free[slot] = S.t("dve")
                w2_copy_t = S.t("dve")
        t_w2done = w2_copy_t

        ffn1_t = {}
        ffn2_t = {}
        gelu_t = {}
        ycopy_t = {}
        evac3b_t = {}
        evac3b_nt = {}
        ldx1_t = {}
        lnse3_t = {}
        sq3_t = {}
        transp3_t = {}

        y_t = {}
        for g in range(8):
            for s4 in range(4):
                tt = g * 4 + s4
                wl = [("stx1", 16 * NT)]
                if s4 >= 2:
                    wl.append(("dve", y_t[tt - 2]))
                elif g >= 1:
                    wl.append(("dve", evac3b_nt[(tt - 2, 1)]))
                else:
                    wl += [("dve", evac3a_t[NT - 1]),
                           ("dve", stt_t[(7, 15)]),
                           ("dve", t_w2stg[0]), ("dve", t_w2stg[1])]
                sm = "ldy%d" % (tt % 2)
                S.dma("sync", x1rd[tt % 2], x1_h[tt * 128:(tt + 1) * 128, :],
                      waits=wl, inc=sm)
                ldx1_t[tt] = (sm, S.t(sm))
                wl = [ldx1_t[tt]]
                if tt >= 2:
                    wl.append(("se", lnse3_t[tt - 2]))
                S.emit("vector", lambda e, x2=x1rd[tt % 2], tx=tt:
                       e.reduce_sum(sl(tx, 0), x2, axis=AX), waits=wl,
                       inc="dve")
                t_red3 = S.t("dve")
                S.emit("vector", lambda e, tx=tt: e.tensor_scalar(
                    out=sl(tx, 1), in0=sl(tx, 0), scalar1=-1.0 / D,
                    scalar2=None, op0=ALU.mult), waits=[("dve", t_red3)],
                    inc="dve")
                t_negmu = S.t("dve")
                wl = [("dve", t_negmu)]
                if tt == 0:
                    wl.append(("pe", t_out3a_last))
                S.emit("scalar", lambda e, x2=x1rd[tt % 2], tx=tt:
                       e.activation(out=sqdump3b[:], in_=x2, func=AF.Square,
                                    bias=sl(tx, 1), accum_out=sl(tx, 2)),
                       waits=wl, inc="se")
                sq3_t[tt] = S.t("se")
                S.emit("scalar", lambda e, tx=tt: e.activation(
                    out=sl(tx, 3), in_=sl(tx, 2), func=AF.Sqrt, scale=1.0 / D,
                    bias=epsc), inc="se")
                lnse3_t[tt] = S.t("se")
                S.emit("vector", lambda e, tx=tt: e.reciprocal(
                    sl(tx, 4), sl(tx, 3)), waits=[("se", lnse3_t[tt])],
                    inc="dve")
                t_rstd3 = S.t("dve")
                wl = []
                if tt == 0:
                    wl.append(("dve", t_w2done))
                else:
                    wl.append(("pe", transp3_t[tt - 1]))
                wl.append(("dve", t_rstd3))
                S.emit("vector", lambda e, x2=x1rd[tt % 2], tx=tt:
                       e.tensor_scalar(
                           out=y_bf[:], in0=x2, scalar1=sl(tx, 1),
                           scalar2=sl(tx, 4), op0=ALU.add, op1=ALU.mult),
                       waits=wl, inc="dve")
                t_y = S.t("dve")
                y_t[tt] = t_y
                wl = [("dve", t_y)]
                if tt >= 1:
                    wl.append(("dve", ycopy_t[tt - 1]))
                else:
                    wl.append(("dve", t_w1done))
                for k in range(8):
                    S.emit("tensor", lambda e, kk=k: e.transpose(
                        pT6[:, kk * 128:(kk + 1) * 128],
                        y_bf[:, kk * 128:(kk + 1) * 128], identb[:]),
                        waits=(wl if k == 0 else []), inc="pe")
                transp3_t[tt] = S.t("pe")
                wl = [("pe", transp3_t[tt])]
                if s4 == 0:
                    if g >= 1:
                        wl.append(("pe", ffn1_t[g - 1]))
                    else:
                        wl.append(("pe", t_hlast))
                S.emit("vector", lambda e, sx=s4: e.tensor_copy(
                    nfm_grp3d[:, :, sx * 128:(sx + 1) * 128],
                    pT6.rearrange("p (c m) -> p c m", m=128)),
                    waits=wl, inc="dve")
                ycopy_t[tt] = S.t("dve")
            t_ygrp = S.t("dve")
            for j in range(32):
                bk = j % 4
                wl = [("dve", t_ygrp), ("dve", t_w1done)]
                prev = (g, j - 4) if j >= 4 else (
                    (g - 1, 28 + j) if g >= 1 else None)
                if prev in gelu_t:
                    wl.append(("se", gelu_t[prev]))
                for c in range(8):
                    S.emit("tensor", lambda e, cc=c, jx=j, bkx=bk: e.matmul(
                        bank(bkx), w1t_ck[cc][:, jx * 128:(jx + 1) * 128],
                        nfm_grp[:, cc * 512:(cc + 1) * 512],
                        start=(cc == 0), stop=(cc == 7)),
                        waits=(wl if c == 0 else []), inc="pe")
                t_f1 = S.t("pe")
                wl = [("pe", t_f1)]
                if g >= 1:
                    wl.append(("pe", ffn2_t[g - 1]))
                elif j == 0:
                    wl.append(("pe", t_out3a_last))
                S.emit("scalar", lambda e, jx=j, bkx=bk: e.activation(
                    out=h1_sb[jx][:], in_=bank(bkx), func=AF.Gelu,
                    bias=fb1_pp[:, jx:jx + 1]), waits=wl, inc="se")
                gelu_t[(g, j)] = S.t("se")
            ffn1_t[g] = S.t("pe")
            t_gelu_all = S.t("se")
            for s4 in range(4):
                tt = g * 4 + s4
                if s4 < 2:
                    wl = [("dve", y_t[g * 4 + s4 + 2])]
                else:
                    wl = [("dve", evac3b_nt[(tt - 2, 1)])]
                sm = "ldy%d" % (tt % 2)
                S.dma("sync", x1rd[tt % 2], x1_h[tt * 128:(tt + 1) * 128, :],
                      waits=wl, inc=sm)
                t_x1b = (sm, S.t(sm))
                for n in range(2):
                    wl = [("se", t_gelu_all), ("dve", t_w2done),
                          ("dve", t_fb2), t_x1b]
                    if (tt - 1, n) in evac3b_nt:
                        wl.append(("dve", evac3b_nt[(tt - 1, n)]))
                    elif tt == 0:
                        wl.append(("dve", evac3a_nt[(NT - 1, n)]))
                    for j in range(32):
                        S.emit("tensor", lambda e, jx=j, nn=n, sx=s4: e.matmul(
                            bank(4 + nn),
                            h1_sb[jx][:, sx * 128:(sx + 1) * 128],
                            w2t_ck[jx][:, nn * 512:(nn + 1) * 512],
                            start=(jx == 0), stop=False),
                            waits=(wl if j == 0 else []), inc="pe")
                    S.emit("tensor", lambda e, nn=n: e.matmul(
                        bank(4 + nn), onesb[:],
                        fb2b[0:1, nn * 512:(nn + 1) * 512],
                        start=False, stop=True), inc="pe")
                    t_omm = S.t("pe")
                    wl2 = [("pe", t_omm), t_x1b]
                    if tt >= 1:
                        wl2.append(("stout", 16 * tt))
                    else:
                        wl2.append(("stx1", t_x1all))
                    S.emit("vector", lambda e, nn=n, sx=s4: e.tensor_add(
                        evacC[:, nn * 512:(nn + 1) * 512], bank(4 + nn),
                        x1rd[(g * 4 + sx) % 2][:, nn * 512:(nn + 1) * 512]),
                        waits=wl2, inc="dve")
                    evac3b_nt[(tt, n)] = S.t("dve")
                evac3b_t[tt] = S.t("dve")
                S.dma("gpsimd", out_d[tt * 128:(tt + 1) * 128, :], evacC,
                      waits=[("dve", evac3b_t[tt])], inc="stout")
            ffn2_t[g] = S.t("pe")

        if dbg:
            t_done = S.t("pe")
            for hh in range(8):
                S.dma("gpsimd", ktf_d[:, hh * 8192:(hh + 1) * 8192],
                      KT[:, hh * 8192:(hh + 1) * 8192],
                      waits=[("pe", t_done)], inc="stdbg")
            S.dma("gpsimd", h1f_d, wA[:], waits=[("pe", t_done)], inc="stdbg")
            S.dma("gpsimd", ppf_d, ppf[:], waits=[("pe", t_done)],
                  inc="stdbg")
        fw = [("stout", 16 * NT)]
        if dbg:
            fw.append(("stdbg", S.t("stdbg")))
        S.emit("gpsimd", lambda e: e.nop(), waits=fw)

        with nc.Block() as block:
            @block.sync
            def _(eng):
                S.replay("sync", eng)

            @block.vector
            def _(eng):
                S.replay("vector", eng)

            @block.scalar
            def _(eng):
                S.replay("scalar", eng)

            @block.tensor
            def _(eng):
                S.replay("tensor", eng)

            @block.gpsimd
            def _(eng):
                S.replay("gpsimd", eng)

    return nc


_NC_CACHE = {}


def _get_nc(dbg=False):
    key = bool(dbg)
    if key not in _NC_CACHE:
        _NC_CACHE[key] = build_nc(dbg=key)
    return _NC_CACHE[key]


def kernel(L0_attn_x, L0_attn_ln_g, L0_attn_ln_b, L0_attn_W_hidden,
           L0_attn_b_hidden, L0_attn_W_qk, L0_attn_b_qk, L0_attn_os_gamma,
           L0_attn_os_beta, L0_attn_W_out, L0_attn_b_out, L0_ffn_W1,
           L0_ffn_b1, L0_ffn_W2, L0_ffn_b2):
    x = np.ascontiguousarray(np.asarray(L0_attn_x, dtype=np.float32))
    B = x.shape[0]
    assert B == 8 and x.shape[1] == NTOK and x.shape[2] == D

    com = dict(
        wqk=np.ascontiguousarray(np.asarray(L0_attn_W_qk, np.float32)),
        whid=np.ascontiguousarray(np.asarray(L0_attn_W_hidden, np.float32)),
        wout=np.ascontiguousarray(np.asarray(L0_attn_W_out, np.float32)),
        w1=np.ascontiguousarray(np.asarray(L0_ffn_W1, np.float32)),
        w2=np.ascontiguousarray(np.asarray(L0_ffn_W2, np.float32)),
        lng=np.asarray(L0_attn_ln_g, np.float32).reshape(1, D),
        lnb=np.asarray(L0_attn_ln_b, np.float32).reshape(1, D),
        bqk=np.asarray(L0_attn_b_qk, np.float32).reshape(1, QKD),
        bhid=np.asarray(L0_attn_b_hidden, np.float32).reshape(1, HID2),
        bout=np.asarray(L0_attn_b_out, np.float32).reshape(1, D),
        g0=np.ascontiguousarray(
            np.asarray(L0_attn_os_gamma, np.float32)[0]).reshape(1, QKD),
        b0=np.ascontiguousarray(
            np.asarray(L0_attn_os_beta, np.float32)[0]).reshape(1, QKD),
        g1=np.ascontiguousarray(
            np.asarray(L0_attn_os_gamma, np.float32)[1]).reshape(1, QKD),
        b1=np.ascontiguousarray(
            np.asarray(L0_attn_os_beta, np.float32)[1]).reshape(1, QKD),
        fb1=np.asarray(L0_ffn_b1, np.float32).reshape(1, FF),
        fb2=np.asarray(L0_ffn_b2, np.float32).reshape(1, D),
        iden=np.eye(128, dtype=np.float32),
    )
    in_maps = [dict(com, x=np.ascontiguousarray(x[i])) for i in range(B)]

    nc = _get_nc(dbg=DBG)
    res = run_bass_kernel_spmd(nc, in_maps, core_ids=list(range(B)))
    out = np.stack([np.asarray(r["out"]) for r in res.results], axis=0)
    if DBG:
        kernel.last_results = res.results
    return out
